# version 45
# speedup vs baseline: 1.0791x; 1.0090x over previous
import numpy as np
from contextlib import ExitStack

import concourse.bass as bass
import concourse.mybir as mybir
from concourse.bass_utils import run_bass_kernel_spmd

F32 = mybir.dt.float32
BF16 = mybir.dt.bfloat16
AF = mybir.ActivationFunctionType
ALU = mybir.AluOpType
AX = mybir.AxisListType

D = 1024
NC = 8
HEADS_A = 16
PAIRS_A = 8
HEADS_F = 8
PAIRS_F = 4
INNER = 512
HID = 2730
NEG = -1.0e37
LN_EPS = 1e-5


def make_consts(T):
    import ml_dtypes
    c = {}
    c["identf"] = np.eye(128, dtype=np.float32)
    c["identb"] = np.eye(128, dtype=ml_dtypes.bfloat16)
    c["onesrow"] = np.ones((1, 512), dtype=np.float32)
    c["onesrow_b"] = np.ones((1, 512), dtype=ml_dtypes.bfloat16)
    c["onescol"] = np.ones((128, 1), dtype=ml_dtypes.bfloat16)
    c["onescol_f"] = np.ones((128, 1), dtype=np.float32)
    c["epscol"] = np.full((1, 1), 1e-5, dtype=np.float32)
    ii = np.arange(128)
    c["m2_hi"] = np.where(ii[:, None] > ii[None, :], NEG, 0.0).astype(np.float32)
    c["m2_lo"] = np.where(ii[:, None] < ii[None, :], NEG, 0.0).astype(np.float32)
    half = np.arange(0, 64, 2, dtype=np.float64)
    inv_freq = 1.0 / 10000.0 ** (half / 64.0)
    t = np.arange(T, dtype=np.float64)
    freqs = t[:, None] * inv_freq[None, :]
    freqs = np.concatenate([freqs, freqs], axis=-1)
    base = (half + 0.4 * 64) / (1.4 * 64)
    power = (t - T // 2) / 256.0
    scale = base[None, :] ** power[:, None]
    scale = np.concatenate([scale, scale], axis=-1)
    cos, sin = np.cos(freqs), np.sin(freqs)
    sign = np.concatenate([-np.ones(32), np.ones(32)])[None, :]

    def tm(a):
        return np.ascontiguousarray(
            a.reshape(T // 128, 128, 64).transpose(1, 0, 2)).astype(np.float64)
    c["CQ"] = tm(cos * scale)
    c["SQ"] = tm(sign * sin * scale)
    c["CK"] = tm(cos / scale)
    c["SK"] = tm(sign * sin / scale)
    return c


ENGINES = ("pe", "act", "dve", "sp", "gp")


class Buf:
    __slots__ = ("w", "r")

    def __init__(self):
        self.w = []
        self.r = []

    def wdeps(self):
        return self.w + self.r

    def rdeps(self):
        return list(self.w)

    def wrote(self, *toks):
        self.w = list(toks)
        self.r = []

    def read(self, *toks):
        self.r.extend(toks)


class PB:
    def __init__(self, nc):
        self.nc = nc
        self.es = ExitStack()
        self.sem = {e: self.es.enter_context(nc.semaphore(f"s_{e}")) for e in ENGINES}
        self.streams = {}
        self.stream_semids = set()
        self.cnt = {}
        self.semobj = {}
        for s in self.sem.values():
            self.cnt[id(s)] = 0
            self.semobj[id(s)] = s
        self.wm = {e: {} for e in ENGINES}
        self.prog = {e: [] for e in ENGINES}
        import os as _os
        self.graph = [] if _os.environ.get("PB_GRAPH") else None
        self.markers = []
        self.tok2node = {}
        self.last_on = {}
        import collections as _c
        self.last_on = _c.defaultdict(lambda: None)

    def sb(self, name, shape, dtype, stack=None):
        return (stack or self.es).enter_context(self.nc.sbuf_tensor(name, shape, dtype))

    def ps_alloc(self, name, shape, dtype, stack=None):
        return (stack or self.es).enter_context(self.nc.psum_tensor(name, shape, dtype))

    def _waits(self, eng, deps):
        need = {}
        for tok in deps:
            if tok is None:
                continue
            s, v = tok
            k = id(s)
            if k in self.stream_semids:
                v = self.cnt[k]
            if need.get(k, 0) < v:
                need[k] = v
        waits = []
        wm = self.wm[eng]
        for k, v in need.items():
            if wm.get(k, 0) < v:
                waits.append((self.semobj[k], v))
                wm[k] = v
        return waits

    def _record(self, eng, deps, cost):
        if self.graph is None:
            return
        nid = len(self.graph)
        dd = []
        for t in deps:
            if t is None:
                continue
            key = (id(t[0]), t[1])
            j = self.tok2node.get(key)
            if j is not None:
                dd.append(j)
        if self.last_on[eng] is not None:
            dd.append(self.last_on[eng])
        self.graph.append((eng, dd, cost))
        self.last_on[eng] = nid
        return nid

    def _toklog(self, tok, nid):
        if self.graph is not None and nid is not None:
            self.tok2node[(id(tok[0]), tok[1])] = nid

    def op(self, eng, fn, deps=(), cost=500):
        waits = self._waits(eng, deps)
        s = self.sem[eng]
        self.cnt[id(s)] += 1
        v = self.cnt[id(s)]
        self.prog[eng].append((waits, fn, s, 1))
        nid = self._record(eng, deps, cost)
        self._toklog((s, v), nid)
        return (s, v)

    def ops(self, eng, fns, deps=(), cost=None):
        waits = self._waits(eng, deps)
        s = self.sem[eng]
        self.cnt[id(s)] += 1
        v = self.cnt[id(s)]

        def run(engobj, fns=tuple(fns)):
            ins = None
            for f in fns:
                ins = f(engobj)
            return ins
        self.prog[eng].append((waits, run, s, 1))
        nid = self._record(eng, deps, cost if cost is not None else 300 * len(fns))
        self._toklog((s, v), nid)
        return (s, v)

    def stream_sem(self, q, stream):
        key = ("gp" if q == "gp" else "hw", stream)
        if key not in self.streams:
            sem = self.es.enter_context(self.nc.semaphore(f"str_{key[0]}_{stream}"))
            self.streams[key] = sem
            self.cnt[id(sem)] = 0
            self.semobj[id(sem)] = sem
            self.stream_semids.add(id(sem))
        return self.streams[key]

    def dma(self, q, out, in_, deps=(), stream="misc", **kw):
        waits = self._waits(q, deps)
        sem = self.stream_sem(q, stream)
        self.cnt[id(sem)] += 16
        v = self.cnt[id(sem)]

        def run(engobj, out=out, in_=in_, kw=dict(kw)):
            return engobj.dma_start(out=out, in_=in_, **kw)
        self.prog[q].append((waits, run, sem, 16))
        try:
            nb = out.nbytes()
        except Exception:
            nb = 0
        nid = self._record("dma_" + q + "_" + stream, deps, 2000 + nb // 300)
        self._toklog((sem, v), nid)
        return (sem, v)

    def wait_only(self, eng, deps):
        waits = self._waits(eng, deps)
        if waits:
            self.prog[eng].append((waits, None, None, 0))

    def marker(self, name):
        if self.graph is not None:
            self.markers.append((name, len(self.graph)))

    def now(self, eng):
        s = self.sem[eng]
        return (s, self.cnt[id(s)])

    def barrier(self, except_streams=()):
        toks = [self.now(e) for e in ENGINES]
        toks += [(s, self.cnt[id(s)]) for k, s in self.streams.items()
                 if k[1] not in except_streams]
        for e in ENGINES:
            self.wait_only(e, toks)

    def finalize(self, final_dma_toks):
        self.wait_only("sp", final_dma_toks)

    def materialize(self):
        nc = self.nc
        with nc.Block() as block:
            def mk(ename):
                def run(engobj):
                    for waits, fn, s, inc in self.prog[ename]:
                        for (sem, v) in waits:
                            engobj.wait_ge(sem, v)
                        if fn is not None:
                            ins = fn(engobj)
                            if inc and ins is not None:
                                ins.then_inc(s, inc)
                return run
            block.tensor(mk("pe"))
            block.scalar(mk("act"))
            block.vector(mk("dve"))
            block.sync(mk("sp"))
            block.gpsimd(mk("gp"))


def build_nc(T, debug_taps=(), upto="all"):
    taps = set(debug_taps)
    nc = bass.Bass("TRN2", target_bir_lowering=False)
    TT = T // 128
    TG = (T + 511) // 512

    def gw(g):
        return min(512, T - g * 512)

    dram = {}

    def din(name, shape, dtype=F32):
        dram[name] = nc.dram_tensor(name, list(shape), dtype, kind="ExternalInput").ap()
        return dram[name]

    x_in = din("x", (T, D))
    W1d = din("L0_attn_W1", (D, 4 * D)); b1d = din("L0_attn_b1", (4 * D,))
    W2d = din("L0_attn_W2", (D, D)); b2d = din("L0_attn_b2", (D,))
    gammad = din("L0_attn_gamma", (D,)); betad = din("L0_attn_beta", (D,))
    wqkvd = din("L0_ffn_wqkv", (2, D, 1536))
    qscaled = din("L0_ffn_q_scale", (2, 64)); kscaled = din("L0_ffn_k_scale", (2, 64))
    wod = din("L0_ffn_wo", (2, INNER, D))
    lngd = din("L0_ffn_ln_g", (2, D)); lnbd = din("L0_ffn_ln_b", (2, D))
    w1fd = din("L0_ffn_w1", (2, D, 2 * HID)); b1fd = din("L0_ffn_b1", (2, 2 * HID))
    w2fd = din("L0_ffn_w2", (2, HID, D)); b2fd = din("L0_ffn_b2", (2, D))
    identf_d = din("identf", (128, 128)); identb_d = din("identb", (128, 128), BF16)
    onesrow_d = din("onesrow", (1, 512)); onesrowb_d = din("onesrow_b", (1, 512), BF16)
    onescol_d = din("onescol", (128, 1), BF16)
    onescolf_d = din("onescol_f", (128, 1))
    epscol_d = din("epscol", (1, 1))
    m2hi_d = din("m2_hi", (128, 128)); m2lo_d = din("m2_lo", (128, 128))
    CQd = din("CQ", (128, TT, 64)); SQd = din("SQ", (128, TT, 64))
    CKd = din("CK", (128, TT, 64)); SKd = din("SK", (128, TT, 64))

    out_d = nc.dram_tensor("out", [T, D], F32, kind="ExternalOutput").ap()
    xdram = nc.dram_tensor("xdram", [128, NC, T], F32, kind="Internal").ap()
    x2dram = nc.dram_tensor("x2dram", [128, NC, T], F32, kind="Internal").ap()

    P = PB(nc)
    global _LAST_PB
    _LAST_PB = P

    idf = P.sb("idf", [128, 128], F32)
    idb = P.sb("idb", [128, 128], BF16)
    ones = P.sb("ones", [1, 512], F32)
    onesb = P.sb("onesb", [1, 512], BF16)
    onescol = P.sb("onescol_s", [128, 1], BF16)
    onescolf = P.sb("onescolf_s", [128, 1], F32)
    epscol = P.sb("epscol_s", [1, 1], F32)
    MK = P.sb("MK", [128, 2, 128], F32)
    ROWB = P.sb("ROWB", [1, 10240], BF16)
    ROWX = P.sb("ROWX", [1, 7168], F32)
    ROWF = P.sb("ROWF", [1, 3072], F32)
    COLF = P.sb("COLF", [128, 128], F32)
    COLB = P.sb("COLB", [128, 32], BF16)
    PS = P.ps_alloc("PS", [128, 7, 512], F32)
    PSWG = PS.rearrange("p b n -> p (b n)")
    PSB = P.ps_alloc("PSB", [128, 1024], BF16)

    bufs = {}

    def B(name):
        if name not in bufs:
            bufs[name] = Buf()
        return bufs[name]

    def psbank(i):
        return B(f"psbank{i}")

    RF = {"A": 0, "Bv": 512, "mu2": 1024, "bias": 1536,
          "A2": 2048, "Bv2": 2560}

    def rf(slot, w=512):
        off = RF[slot]
        return ROWF[0:1, off:off + w], B("rf_" + slot)

    RB = {}
    rb_ptr = [0]
    RX = {}
    rx_ptr = [0]

    def rb_reset():
        RB.clear()
        rb_ptr[0] = 0
        RX.clear()
        rx_ptr[0] = 0

    def rb(name, width=None):
        if name not in RB:
            assert width is not None, name
            off = rb_ptr[0]
            assert off + width <= 10240, (name, off, width)
            RB[name] = (off, width)
            rb_ptr[0] = off + width
        off, wd = RB[name]
        return ROWB[0:1, off:off + wd], B("rb_" + name)

    def rx(name, width=None):
        if name not in RX:
            assert width is not None, name
            off = rx_ptr[0]
            assert off + width <= 7168, (name, off, width)
            RX[name] = (off, width)
            rx_ptr[0] = off + width
        off, wd = RX[name]
        return ROWX[0:1, off:off + wd], B("rx_" + name)

    tap_toks = []

    def tap(name, ap_sbuf, deps, dtype=F32):
        if name not in taps:
            return
        t = nc.dram_tensor("tap_" + name, list(ap_sbuf.shape), dtype,
                           kind="ExternalOutput").ap()
        tk = P.dma("gp", t, ap_sbuf, deps=deps, stream="tap")
        tap_toks.append(tk)

    def tap_dram(name, dram_ap):
        if name not in taps:
            return None
        t = nc.dram_tensor("tap_" + name, list(dram_ap.shape), F32,
                           kind="ExternalOutput").ap()
        return (name, t, dram_ap)

    P.marker("x0-setup")
    t_idf = P.dma("sp", idf[:, :], identf_d, stream="const")
    t_idb = P.dma("sp", idb[:, :], identb_d, stream="const")
    t_ones = P.dma("sp", ones[:, :], onesrow_d, stream="const")
    t_onesb = P.dma("sp", onesb[:, :], onesrowb_d, stream="const")
    t_onescol = P.dma("sp", onescol[:, :], onescol_d, stream="const")
    t_onescolf = P.dma("sp", onescolf[:, :], onescolf_d, stream="const")
    t_eps = P.dma("sp", epscol[:, :], epscol_d, stream="const")
    t_mk = [P.dma("sp", MK[:, i, :], m, stream="const") for i, m in
            enumerate([m2hi_d, m2lo_d])]
    t_consts = [t_idf, t_idb, t_ones, t_onesb, t_onescol, t_eps] + t_mk

    def row_to_cols(rowap, rowbuf, coltensor, colslot, n, colbufname, bank=5):
        nch = (n + 127) // 128
        cb = B(colbufname)
        for i in range(nch):
            w = min(128, n - 128 * i)
            pb = psbank(bank)

            def tr(e, i=i, w=w, rowap=rowap, bank=bank):
                return e.transpose(PS[0:w, bank, 0:1], rowap[0:1, 128 * i:128 * i + w],
                                   idf[0:1, 0:1])
            tk = P.op("pe", tr, deps=rowbuf.rdeps() + [t_idf] + pb.wdeps())
            pb.wrote(tk)
            rowbuf.read(tk)

            def cp(e, i=i, w=w, colslot=colslot, bank=bank, coltensor=coltensor):
                return e.tensor_copy(coltensor[0:w, colslot + i:colslot + i + 1],
                                     PS[0:w, bank, 0:1])
            tc = P.op("dve", cp, deps=[tk] + cb.wdeps())
            pb.read(tc)
            cb.w.append(tc)
        return cb

    def vec_to_cols(vecap, n, colslot, colbufname, voff=0):
        cb = B(colbufname)
        bias_ap, bias_b = rf("bias")
        for s0 in range(0, n, 512):
            w = min(512, n - s0)
            tk = P.dma("sp", ROWF[0:1, RF["bias"]:RF["bias"] + w],
                       vecap.rearrange("(one n) -> one n", one=1)[0:1, voff + s0:voff + s0 + w],
                       deps=bias_b.wdeps(), stream="bias")
            bias_b.wrote(tk)
            nslice = (w + 127) // 128
            for i in range(nslice):
                wi = min(128, w - 128 * i)
                pb = psbank(5)

                def tr(e, wi=wi, i=i):
                    return e.transpose(PS[0:wi, 5, 0:1],
                                       ROWF[0:1, RF["bias"] + 128 * i:RF["bias"] + 128 * i + wi],
                                       idf[0:1, 0:1])
                tk2 = P.op("pe", tr, deps=[tk, t_idf] + pb.wdeps())
                pb.wrote(tk2)
                bias_b.read(tk2)
                ci = colslot + (s0 // 128) + i

                def cp(e, wi=wi, ci=ci):
                    return e.tensor_copy(COLF[0:wi, ci:ci + 1], PS[0:wi, 5, 0:1])
                tc = P.op("dve", cp, deps=[tk2] + cb.wdeps())
                pb.read(tc)
                cb.w.append(tc)
        return cb

    def vec_to_row(vecap, n, rbname, voff=0, stream="bias2", f32=False):
        rrow, r_b = (rx(rbname) if f32 else rb(rbname))
        bias_ap, bias_b = rf("bias")
        for s0 in range(0, n, 512):
            w = min(512, n - s0)
            tk = P.dma("sp", ROWF[0:1, RF["bias"]:RF["bias"] + w],
                       vecap.rearrange("(one n) -> one n", one=1)[0:1, voff + s0:voff + s0 + w],
                       deps=bias_b.wdeps(), stream=stream)
            bias_b.wrote(tk)

            def cp(e, s0=s0, w=w, rrow=rrow):
                return e.activation(rrow[0:1, s0:s0 + w],
                                    ROWF[0:1, RF["bias"]:RF["bias"] + w], AF.Copy)
            tc = P.op("act", cp, deps=[tk] + r_b.wdeps())
            bias_b.read(tc)
            r_b.w.append(tc)
        return rrow, r_b

    def ln_stats_rows(w, arow, a_b, brow, b_b, rsrow, rs_b, rs_off, nmurs, nmurs_b,
                      nmursb=None, nmursb_b=None, par=0):
        if par == 0:
            mu2row, mu2_b = rf("mu2", w)
        else:
            mu2row, mu2_b = PS[0:1, 3, 0:w], psbank(3)

        def f1(e):
            return e.tensor_scalar_mul(arow, arow, 1.0 / D)
        t1 = P.op("dve", f1, deps=a_b.wdeps())
        a_b.wrote(t1)

        if par == 0:
            def f2(e):
                return e.activation(mu2row, arow, AF.Square)
            t2 = P.op("act", f2, deps=[t1] + mu2_b.wdeps())
        else:
            def f2(e):
                return e.tensor_tensor(mu2row, arow, arow, op=ALU.mult)
            t2 = P.op("dve", f2, deps=[t1] + mu2_b.wdeps())
        a_b.read(t2); mu2_b.wrote(t2)

        def f3(e):
            return e.scalar_tensor_tensor(brow, brow, 1.0 / D, mu2row,
                                          op0=ALU.mult, op1=ALU.subtract)
        t3 = P.op("dve", f3, deps=b_b.wdeps() + [t2])
        mu2_b.read(t3); b_b.wrote(t3)

        def f4(e):
            return e.activation(brow, brow, AF.Ln, bias=epscol[0:1, 0:1])
        t4 = P.op("act", f4, deps=[t3, t_eps])
        b_b.wrote(t4)

        def f5(e):
            return e.activation(rsrow[0:1, rs_off:rs_off + w], brow, AF.Exp, scale=-0.5)
        t5 = P.op("act", f5, deps=[t4] + rs_b.wdeps())
        b_b.read(t5); rs_b.w.append(t5)

        def f6(e):
            return e.scalar_tensor_tensor(nmurs[0:1, rs_off:rs_off + w], arow, -1.0,
                                          rsrow[0:1, rs_off:rs_off + w],
                                          op0=ALU.mult, op1=ALU.mult)
        t6 = P.op("dve", f6, deps=[t1, t5] + nmurs_b.wdeps())
        a_b.read(t6); rs_b.read(t6); nmurs_b.w.append(t6)
        if nmursb is not None:
            def f7(e):
                return e.activation(nmursb[0:1, rs_off:rs_off + w],
                                    nmurs[0:1, rs_off:rs_off + w], AF.Copy)
            t7 = P.op("act", f7, deps=[t6] + nmursb_b.wdeps())
            nmurs_b.read(t7)
            nmursb_b.w.append(t7)

    def ln_stats_bf16(src_fn, w, rsname, nmursname, rs_off, sq, sqbufname,
                      src_f32=False, make_bf=True, par=0):
        rsrow, rs_b = rx(rsname)
        nmurs, nmurs_b = rx(nmursname)
        if make_bf:
            nmursb, nmursb_b = rb(nmursname + "_b")
        else:
            nmursb, nmursb_b = None, None
        sqb = B(sqbufname)
        sq_toks = []
        srcs = []
        for c in range(NC):
            ap, deps, bo = src_fn(c)
            srcs.append((ap, deps, bo))

            def f(e, ap=ap, c=c, w=w):
                return e.activation(sq[:, c, 0:w], ap, AF.Square)
            tk = P.op("act", f, deps=deps + sqb.wdeps())
            sq_toks.append(tk)
            if bo is not None:
                bo.read(tk)
        sqb.wrote(*sq_toks)
        arow, a_b = rf("A" if par == 0 else "A2", w)
        brow, b_b = rf("Bv" if par == 0 else "Bv2", w)
        bpair = (5, 6) if par == 0 else (3, 4)
        for (bank, which, drow, dbuf) in ((bpair[0], "src", arow, a_b),
                                          (bpair[1], "sq", brow, b_b)):
            pb = psbank(bank)
            fns = []
            for c in range(NC):
                apx = srcs[c][0] if which == "src" else sq[:, c, 0:w]
                lhs = onescolf[:, :] if (which == "src" and src_f32) else onescol[:, :]

                def mm(e, apx=apx, c=c, bank=bank, w=w, lhs=lhs):
                    return e.matmul(PS[0:1, bank, 0:w], lhs, apx,
                                    start=(c == 0), stop=(c == NC - 1))
                fns.append(mm)
            dps = [t_onescol, t_onescolf] + pb.wdeps()
            if which == "src":
                for (ap, deps, bo) in srcs:
                    dps += deps
            else:
                dps += sq_toks
            tk = P.ops("pe", fns, deps=dps)
            pb.wrote(tk)
            if which == "src":
                for (ap, deps, bo) in srcs:
                    if bo is not None:
                        bo.read(tk)
            else:
                sqb.read(tk)

            def cpr(e, drow=drow, w=w, bank=bank):
                return e.activation(drow, PS[0:1, bank, 0:w], AF.Copy)
            tc = P.op("act", cpr, deps=[tk] + dbuf.wdeps())
            pb.read(tc)
            dbuf.wrote(tc)
        ln_stats_rows(w, arow, a_b, brow, b_b, rsrow, rs_b, rs_off, nmurs, nmurs_b,
                      nmursb, nmursb_b, par=par)
        return rsrow, rs_b, nmurs, nmurs_b, nmursb, nmursb_b

    def bcast_row(rowap, rowbuf, off, w, bank):
        pb = psbank(bank)

        def mm(e, off=off, w=w, bank=bank, rowap=rowap):
            return e.matmul(PS[:, bank, 0:w], ones[0:1, 0:128], rowap[0:1, off:off + w],
                            start=True, stop=True)
        tk = P.op("pe", mm, deps=rowbuf.rdeps() + [t_ones] + pb.wdeps())
        pb.wrote(tk)
        rowbuf.read(tk)
        return tk

    def colsum(wt, wtoks, rbname, n, gcol=None, gdeps=(), bank=5):
        rrow, r_b = rb(rbname)
        for s0 in range(0, n, 512):
            w = min(512, n - s0)
            pb = psbank(bank)
            fns = []
            for c in range(NC):
                lhs = onescol[:, :] if gcol is None else gcol[:, c:c + 1]

                def mm(e, c=c, s0=s0, w=w, bank=bank, wt=wt, lhs=lhs):
                    return e.matmul(PS[0:1, bank, 0:w], lhs, wt[:, c, s0:s0 + w],
                                    start=(c == 0), stop=(c == NC - 1))
                fns.append(mm)
            tk = P.ops("pe", fns, deps=list(wtoks) + list(gdeps) + [t_onescol] + pb.wdeps())
            pb.wrote(tk)

            def cpr(e, s0=s0, w=w, bank=bank, rrow=rrow):
                return e.activation(rrow[0:1, s0:s0 + w], PS[0:1, bank, 0:w], AF.Copy)
            tc = P.op("act", cpr, deps=[tk] + r_b.wdeps())
            pb.read(tc)
            r_b.w.append(tc)
        return rrow, r_b

    def XD(g):
        return B(f"XD_g{g}")

    def X2DB(g):
        return B(f"X2D_g{g}")

    def load_grp(dst, dstbuf, dram_t, g, dbuf_g, stream):
        w = gw(g)
        tk = P.dma("sp", dst[:, :, 0:w], dram_t[:, :, 512 * g:512 * g + w],
                   deps=dstbuf.wdeps() + dbuf_g.rdeps(), stream=stream)
        dstbuf.wrote(tk)
        dbuf_g.read(tk)
        return tk

    def store_grp(src, srcbuf, dram_t, g, dbuf_g, stream):
        w = gw(g)
        tk = P.dma("sp", dram_t[:, :, 512 * g:512 * g + w], src[:, :, 0:w],
                   deps=srcbuf.rdeps() + dbuf_g.wdeps(), stream=stream)
        srcbuf.read(tk)
        dbuf_g.wrote(tk)
        return tk

    def fin_dram_taps(dtaps):
        fin_st = ExitStack()
        TMPT = P.sb("TMPTAP2", [128, NC, 512], F32, stack=fin_st)
        tb_ = B("TMPTAP2")
        for item in dtaps:
            if item is None:
                continue
            name, t, src = item
            for g in range(TG):
                w = gw(g)
                tk = P.dma("sp", TMPT[:, :, 0:w], src[:, :, 512 * g:512 * g + w],
                           deps=tb_.wdeps() + XD(g).wdeps() + X2DB(g).wdeps(),
                           stream="tapl")
                tb_.wrote(tk)
                tk2 = P.dma("gp", t[:, :, 512 * g:512 * g + w], TMPT[:, :, 0:w],
                            deps=[tk], stream="tap")
                tb_.read(tk2)
                tap_toks.append(tk2)
        P.barrier()
        P.finalize(tap_toks)
        P.materialize()
        fin_st.close()
        P.es.close()

    b1col_b = vec_to_cols(b1d, 4096, 0, "b1col")
    gcol_b = vec_to_cols(gammad, 1024, 32, "gcol")
    bcolf_b = vec_to_cols(betad, 1024, 40, "bcolf")
    betab_b = B("betacolb")
    bt = []
    for i in range(8):
        def cp(e, i=i):
            return e.tensor_copy(COLB[:, i:i + 1], COLF[:, 40 + i:40 + i + 1])
        tk = P.op("dve", cp, deps=bcolf_b.rdeps() + betab_b.wdeps())
        bcolf_b.read(tk)
        bt.append(tk)
    betab_b.wrote(*bt)
    rx("b1v", 1024)
    b1vrow, b1v_b = vec_to_row(b1d, 1024, "b1v", voff=1024, f32=True)

    rx("rs0", T)
    rx("nmurs0", T)
    rb("nmurs0_b", T)
    rs0, rs0_b = rx("rs0")
    nmurs0, nmurs0_b = rx("nmurs0")
    nmurs0bf, nmurs0bf_b = rb("nmurs0_b")

    x0_st = ExitStack()
    XTM = P.sb("XTM", [128, 4, D], F32, stack=x0_st)
    XG = P.sb("XG", [128, 2, NC, 512], F32, stack=x0_st)
    SQS = P.sb("SQS", [128, 2, NC, 512], BF16, stack=x0_st)

    for g in range(TG):
        P.marker(f"x0-g{g}")
        w = gw(g)
        nt = w // 128
        gp = g % 2
        xg_b = B(f"XG{gp}")
        for j in range(nt):
            tile = 4 * g + j
            sl = tile % 4
            xtm_b = B(f"XTM{sl}")
            t_x = P.dma("sp" if sl % 2 == 0 else "act", XTM[:, sl, :],
                        x_in.rearrange("(a p) d -> p a d", p=128)[:, tile, :],
                        deps=xtm_b.wdeps(), stream=f"x{sl}")
            xtm_b.wrote(t_x)
            for c in range(NC):
                bank = c % 4
                pb = psbank(bank)

                def tr(e, sl=sl, c=c, bank=bank):
                    return e.transpose(PS[:, bank, 0:128], XTM[:, sl, 128 * c:128 * c + 128],
                                       idf[:, :])
                tok_tr = P.op("pe", tr, deps=[t_x, t_idf] + pb.wdeps())
                pb.wrote(tok_tr)
                xtm_b.read(tok_tr)

                def cpf(e, c=c, j=j, bank=bank, gp=gp):
                    return e.tensor_copy(XG[:, gp, c, 128 * j:128 * j + 128],
                                         PS[:, bank, 0:128])
                tok_cp = P.op("dve", cpf, deps=[tok_tr] + xg_b.wdeps())
                pb.read(tok_cp)
                xg_b.w.append(tok_cp)
        ln_stats_bf16(lambda c: (XG[:, gp, c, 0:w], xg_b.rdeps(), xg_b), w,
                      "rs0", "nmurs0", 512 * g, SQS[:, gp], f"SQS{gp}",
                      src_f32=True, par=gp)
        store_grp(XG[:, gp], xg_b, xdram, g, XD(g), "xs")
        xg_b.wrote()
        xg_b.read(*XD(g).w)

    tap("rs0", rs0, rs0_b.w)
    tap("nmurs0", nmurs0, nmurs0_b.w)
    P.barrier()
    x0_st.close()

    if upto == "p1":
        P.barrier()
        P.finalize(tap_toks)
        P.materialize()
        P.es.close()
        return nc

    sa_st = ExitStack()
    KF = P.sb("KF", [128, PAIRS_A, T], BF16, stack=sa_st)
    VT = P.sb("VT", [128, TT, D], BF16, stack=sa_st)
    W2S = P.sb("W2S", [128, NC, 1024], BF16, stack=sa_st)
    a1_st = ExitStack()
    W1K = P.sb("W1K", [128, NC, 1024], BF16, stack=a1_st)
    W1V = P.sb("W1V", [128, NC, 1024], BF16, stack=a1_st)
    XGa = P.sb("XGa", [128, NC, 512], F32, stack=a1_st)
    XNG = P.sb("XNG", [128, NC, 512], BF16, stack=a1_st)

    t_w1k = P.dma("gp", W1K[:, :, :],
                  W1d[:, 3072:4096].rearrange("(kc p) n -> p kc n", p=128), stream="w1k")
    t_w1v = P.dma("gp", W1V[:, :, :],
                  W1d[:, 1024:2048].rearrange("(kc p) n -> p kc n", p=128), stream="w1v")
    rb("csk", 1024)
    rb("csv", 1024)
    cskrow, csk_b = colsum(W1K, [t_w1k], "csk", 1024)
    csvrow, csv_b = colsum(W1V, [t_w1v], "csv", 1024)

    def make_xng(g, rs_row, rs_buf, rs_off, dst, dstname, src, srcbuf, src_toks):
        w = gw(g)
        tb_ = bcast_row(rs_row, rs_buf, rs_off, w, 6)
        toks = []
        xng_b = B(dstname)
        for c in range(NC):
            def f(e, c=c, w=w, dst=dst, src=src):
                return e.tensor_tensor(dst[:, c, 0:w], src[:, c, 0:w],
                                       PS[:, 6, 0:w], op=ALU.mult)
            tk = P.op("dve", f, deps=[tb_] + list(src_toks) + xng_b.wdeps())
            toks.append(tk)
            srcbuf.read(tk)
            psbank(6).read(tk)
        xng_b.wrote(*toks)
        return toks

    for g in range(TG):
        P.marker(f"a1-g{g}")
        w = gw(g)
        xga_b = B("XGa")
        t_load = load_grp(XGa, xga_b, xdram, g, XD(g), "xl")
        xng_toks = make_xng(g, rs0, rs0_b, 512 * g, XNG, "XNG", XGa, xga_b, [t_load])
        xng_b = B("XNG")
        for i in range(PAIRS_A):
            bank = i % 2
            pb = psbank(bank)
            fns = []
            for c in range(NC):
                def mm(e, c=c, i=i, w=w, bank=bank):
                    return e.matmul(PS[:, bank, 0:w], W1K[:, c, 128 * i:128 * i + 128],
                                    XNG[:, c, 0:w], start=(c == 0), stop=False)
                fns.append(mm)

            def mmr(e, i=i, w=w, g=g, bank=bank):
                return e.matmul(PS[:, bank, 0:w], cskrow[0:1, 128 * i:128 * i + 128],
                                nmurs0bf[0:1, 512 * g:512 * g + w], start=False, stop=True)
            fns.append(mmr)
            tk = P.ops("pe", fns, deps=xng_toks + [t_w1k] + csk_b.rdeps()
                       + nmurs0bf_b.rdeps() + pb.wdeps())
            pb.wrote(tk)
            xng_b.read(tk)
            csk_b.read(tk)
            nmurs0bf_b.read(tk)
            kf_b = B(f"KF{i}")

            def ev(e, i=i, w=w, g=g, bank=bank):
                return e.activation(KF[:, i, 512 * g:512 * g + w], PS[:, bank, 0:w],
                                    AF.Silu, bias=COLF[:, 24 + i:24 + i + 1])
            te = P.op("act", ev, deps=[tk] + B("b1col").rdeps() + kf_b.wdeps())
            pb.read(te)
            B("b1col").read(te)
            kf_b.w.append(te)
        for j in range(w // 128):
            tau = 4 * g + j
            for ng in range(2):
                bank = 2 + ng
                pb = psbank(bank)
                fns = []
                for c in range(NC):
                    def mm(e, c=c, j=j, ng=ng, bank=bank):
                        return e.matmul(PS[:, bank, :], XNG[:, c, 128 * j:128 * j + 128],
                                        W1V[:, c, 512 * ng:512 * ng + 512],
                                        start=(c == 0), stop=False)
                    fns.append(mm)

                def mmr(e, tau=tau, ng=ng, bank=bank):
                    return e.matmul(PS[:, bank, :], nmurs0bf[0:1, 128 * tau:128 * tau + 128],
                                    csvrow[0:1, 512 * ng:512 * ng + 512],
                                    start=False, stop=False)

                def mmb(e, ng=ng, bank=bank):
                    return e.matmul(PS[:, bank, :], ones[0:1, 0:128],
                                    b1vrow[0:1, 512 * ng:512 * ng + 512],
                                    start=False, stop=True)
                fns.extend([mmr, mmb])
                tk = P.ops("pe", fns, deps=xng_toks + [t_w1v, t_ones] + csv_b.rdeps()
                           + nmurs0bf_b.rdeps() + b1v_b.rdeps() + pb.wdeps())
                pb.wrote(tk)
                xng_b.read(tk)
                csv_b.read(tk)
                nmurs0bf_b.read(tk)
                b1v_b.read(tk)
                vt_b = B(f"VT{g}")

                def ev(e, tau=tau, ng=ng, bank=bank):
                    return e.activation(VT[:, tau, 512 * ng:512 * ng + 512], PS[:, bank, :],
                                        AF.Silu)
                te = P.op("act", ev, deps=[tk] + vt_b.wdeps())
                pb.read(te)
                vt_b.w.append(te)

    tap("KF", KF[:, :, :], [t for i in range(PAIRS_A) for t in B(f"KF{i}").w], dtype=BF16)
    tap("VT", VT[:, :, :], [t for g in range(TG) for t in B(f"VT{g}").w], dtype=BF16)

    if upto == "a1":
        P.barrier()
        P.finalize(tap_toks)
        P.materialize()
        a1_st.close()
        sa_st.close()
        P.es.close()
        return nc

    t_w2 = P.dma("gp", W2S[:, :, :], W2d.rearrange("(kc p) n -> p kc n", p=128), stream="w2")
    w2s_b = B("W2S")
    rb("b2eff", 1024)
    b2eff, b2eff_b = rb("b2eff")
    bias_ap, bias_b = rf("bias")
    beff_toks = []
    for s0 in (0, 512):
        t_b2 = P.dma("sp", ROWF[0:1, RF["bias"]:RF["bias"] + 512],
                     b2d.rearrange("(one n) -> one n", one=1)[0:1, s0:s0 + 512],
                     deps=bias_b.wdeps(), stream="bias2")
        bias_b.wrote(t_b2)
        pb = psbank(6)
        fns = []
        for c in range(NC):
            def mm(e, c=c, s0=s0):
                return e.matmul(PS[0:1, 6, 0:512], COLB[:, c:c + 1], W2S[:, c, s0:s0 + 512],
                                start=(c == 0), stop=(c == NC - 1))
            fns.append(mm)
        tk = P.ops("pe", fns, deps=[t_w2] + B("betacolb").rdeps() + pb.wdeps())
        pb.wrote(tk)
        B("betacolb").read(tk)
        w2s_b.read(tk)

        def addb(e, s0=s0):
            return e.scalar_tensor_tensor(b2eff[0:1, s0:s0 + 512], PS[0:1, 6, 0:512], 1.0,
                                          ROWF[0:1, RF["bias"]:RF["bias"] + 512],
                                          op0=ALU.mult, op1=ALU.add)
        tc = P.op("dve", addb, deps=[tk, t_b2] + b2eff_b.wdeps())
        pb.read(tc)
        bias_b.read(tc)
        b2eff_b.w.append(tc)
        beff_toks.append(tk)
    gtoks = []
    for c in range(NC):
        def gf(e, c=c):
            return e.tensor_scalar_mul(W2S[:, c, :], W2S[:, c, :], COLF[:, 32 + c:33 + c])
        tk = P.op("dve", gf, deps=[t_w2] + beff_toks + B("gcol").rdeps() + w2s_b.wdeps())
        B("gcol").read(tk)
        gtoks.append(tk)
    w2s_b.wrote(*gtoks)

    P.barrier()
    a1_st.close()

    a2_st = ExitStack()
    XGb = P.sb("XGb", [128, NC, 512], F32, stack=a2_st)
    GBUF = P.sb("GBUF", [128, NC, 512], BF16, stack=a2_st)
    WQU = P.sb("WQU", [128, NC, 3, 128], BF16, stack=a2_st)
    QFp = P.sb("QFp", [128, 2, 512], BF16, stack=a2_st)
    UFp = P.sb("UFp", [128, 2, 512], F32, stack=a2_st)
    YB = P.sb("YB", [128, NC, 512], F32, stack=a2_st)
    ZB = YB
    PT = P.sb("PT", [128, 2, 1024], BF16, stack=a2_st)
    TYF = P.sb("TYF", [128, 512], F32, stack=a2_st)
    rx("rsz", 512)
    rx("nmursz", 512)
    rx("rsy", 512)
    rx("nmursy", 512)
    rb("nmursy_b", 512)
    rb("csq", 1024)
    rb("csu", 1024)

    tap_xa = []
    for g in range(TG):
        P.marker(f"a2-g{g}")
        w = gw(g)
        xgb_b = B("XGb")
        t_load = load_grp(XGb, xgb_b, xdram, g, XD(g), "xl2")
        xng_toks = make_xng(g, rs0, rs0_b, 512 * g, GBUF, "GBUF", XGb, xgb_b, [t_load])
        xng2_b = B("GBUF")
        def qu_dma(i, pu):
            slot = (i % 2) if pu == 0 else 2
            wq_b = B(f"WQU{slot}")
            base = 2048 + 128 * i if pu == 0 else 128 * i
            tk = P.dma("gp", WQU[:, :, slot, :],
                       W1d[:, base:base + 128]
                       .rearrange("(kc p) n -> p kc n", p=128),
                       deps=wq_b.wdeps(), stream=f"w1q{slot}")
            wq_b.wrote(tk)
            return tk

        def qu_proj(i):
            rot = i % 2
            for (pu, slot, dst, dbufn, bcol) in (
                    (0, rot, QFp, "QFp", 16 + i), (1, 2, UFp, "UFp", i)):
                wq_b = B(f"WQU{slot}")
                t_w = wq_b.w[-1]
                csrow_, cs_b_ = rb("csq" if pu == 0 else "csu")
                if g == 0:
                    pbq = psbank(6)
                    fns = []
                    for c in range(NC):
                        def mm(e, c=c, slot=slot):
                            return e.matmul(PS[0:1, 6, 0:128], onescol[:, :],
                                            WQU[:, c, slot, :],
                                            start=(c == 0), stop=(c == NC - 1))
                        fns.append(mm)
                    tkc = P.ops("pe", fns, deps=[t_w, t_onescol] + pbq.wdeps())
                    pbq.wrote(tkc)
                    wq_b.read(tkc)

                    def cpr(e, i=i, csrow_=csrow_):
                        return e.activation(csrow_[0:1, 128 * i:128 * i + 128],
                                            PS[0:1, 6, 0:128], AF.Copy)
                    tcc = P.op("act", cpr, deps=[tkc] + cs_b_.wdeps())
                    pbq.read(tcc)
                    cs_b_.w.append(tcc)
                bank = 6
                pb = psbank(bank)
                fns = []
                for c in range(NC):
                    def mm(e, c=c, slot=slot, w=w, bank=bank):
                        return e.matmul(PS[:, bank, 0:w], WQU[:, c, slot, :],
                                        GBUF[:, c, 0:w], start=(c == 0), stop=False)
                    fns.append(mm)

                def mmr(e, i=i, w=w, g=g, bank=bank, csrow_=csrow_):
                    return e.matmul(PS[:, bank, 0:w], csrow_[0:1, 128 * i:128 * i + 128],
                                    nmurs0bf[0:1, 512 * g:512 * g + w],
                                    start=False, stop=True)
                fns.append(mmr)
                tk = P.ops("pe", fns, deps=[t_w] + xng_toks + cs_b_.rdeps()
                           + nmurs0bf_b.rdeps() + pb.wdeps() + xng2_b.wdeps())
                pb.wrote(tk)
                wq_b.read(tk)
                xng2_b.read(tk)
                cs_b_.read(tk)
                nmurs0bf_b.read(tk)
                db = B(dbufn + str(rot))

                def ev(e, w=w, bank=bank, dst=dst, bcol=bcol, rot=rot):
                    return e.activation(dst[:, rot, 0:w], PS[:, bank, 0:w],
                                        AF.Silu, bias=COLF[:, bcol:bcol + 1])
                te = P.op("act", ev, deps=[tk] + B("b1col").rdeps() + db.wdeps())
                pb.read(te)
                B("b1col").read(te)
                db.wrote(te)

        qu_dma(0, 0)
        qu_dma(0, 1)
        qu_dma(1, 0)
        qu_proj(0)
        for i in range(PAIRS_A):
            rot = i % 2
            if i + 1 < PAIRS_A:
                qu_dma(i + 1, 1)
                if i + 2 < PAIRS_A:
                    qu_dma(i + 2, 0)
                qu_proj(i + 1)
            avb = psbank(4)
            avb2 = psbank(5)
            av_deps = avb.wdeps()
            av_deps2 = avb2.wdeps()
            qf_b = B("QFp" + str(rot))
            pv_last = None
            pend = None
            for jc in range(TT + 1):
                if jc < TT:
                    bA = 2 if jc % 2 == 0 else 0
                    bB = 3 if jc % 2 == 0 else 1
                    sA = psbank(bA)
                    sB = psbank(bB)
                    kf_b = B(f"KF{i}")

                    def mmA(e, i=i, jc=jc, w=w, bA=bA, rot=rot):
                        return e.matmul(PS[:, bA, 0:w], KF[0:64, i, 128 * jc:128 * jc + 128],
                                        QFp[0:64, rot, 0:w], start=True, stop=True,
                                        tile_position=(0, 0))

                    def mmB(e, i=i, jc=jc, w=w, bB=bB, rot=rot):
                        return e.matmul(PS[:, bB, 0:w], KF[64:128, i, 128 * jc:128 * jc + 128],
                                        QFp[64:128, rot, 0:w], start=True, stop=True,
                                        tile_position=(64, 0))
                    tkAB = P.ops("pe", [mmA, mmB], deps=kf_b.rdeps() + qf_b.rdeps()
                                 + sA.wdeps() + sB.wdeps(), cost=600)
                    sA.wrote(tkAB)
                    sB.wrote(tkAB)
                    kf_b.read(tkAB)
                    qf_b.read(tkAB)
                    sl = jc % 2
                    ptb_m = B(f"PTm{sl}")
                    assert w == 512 and bB == bA + 1

                    def evAB(e, sl=sl, bA=bA):
                        return e.activation(PT[:, sl, :], PSWG[:, 512 * bA:512 * bA + 1024],
                                            AF.Silu)
                    teAB = P.op("act", evAB, deps=[tkAB] + ptb_m.wdeps(), cost=1150)
                    sA.read(teAB)
                    sB.read(teAB)
                    ptb_m.wrote(teAB)
                if pend is not None:
                    pjc, psl, pteAB = pend
                    vt_b = B(f"VT{pjc // 4}")
                    ptb_m = B(f"PTm{psl}")

                    def pvA(e, i=i, pjc=pjc, psl=psl, w=w):
                        return e.matmul(PS[0:64, 4, 0:w], VT[:, pjc, 128 * i:128 * i + 64],
                                        PT[:, psl, 0:512], start=(pjc == 0), stop=(pjc == TT - 1),
                                        tile_position=(0, 0))

                    def pvB(e, i=i, pjc=pjc, psl=psl, w=w):
                        return e.matmul(PS[64:128, 5, 0:w], VT[:, pjc, 128 * i + 64:128 * i + 128],
                                        PT[:, psl, 512:1024], start=(pjc == 0), stop=(pjc == TT - 1),
                                        tile_position=(0, 64))
                    tpAB = P.ops("pe", [pvA, pvB], deps=[pteAB] + vt_b.rdeps()
                                 + av_deps + av_deps2, cost=600)
                    vt_b.read(tpAB)
                    ptb_m.read(tpAB)
                    pv_last = (tpAB, tpAB)
                if jc < TT:
                    pend = (jc, sl, teAB)
            avb.wrote(pv_last[0])
            avb2.wrote(pv_last[1])
            zb_ = B(f"YB{i}")
            uf_b = B("UFp" + str(rot))

            def zfA(e, i=i, w=w, rot=rot):
                return e.scalar_tensor_tensor(ZB[0:64, i, 0:w], PS[0:64, 4, 0:w], 1.0 / T,
                                              UFp[0:64, rot, 0:w], op0=ALU.mult, op1=ALU.mult)
            tzA = P.op("dve", zfA, deps=list(pv_last) + uf_b.rdeps() + zb_.wdeps())

            def zfB(e, i=i, w=w, rot=rot):
                return e.scalar_tensor_tensor(ZB[64:128, i, 0:w], PS[64:128, 5, 0:w], 1.0 / T,
                                              UFp[64:128, rot, 0:w], op0=ALU.mult, op1=ALU.mult)
            tzB = P.op("dve", zfB, deps=list(pv_last) + uf_b.rdeps() + zb_.wdeps())
            avb.read(tzA)
            avb2.read(tzB)
            uf_b.read(tzA, tzB)
            zb_.wrote(tzA, tzB)
        rsz, rsz_b, nmursz, nmursz_b, _, _ = ln_stats_bf16(
            lambda c: (ZB[:, c, 0:w], B(f"YB{c}").rdeps(), B(f"YB{c}")), w,
            "rsz", "nmursz", 0, GBUF, "GBUF", src_f32=True, make_bf=False)
        tbz = bcast_row(rsz, rsz_b, 0, w, 6)
        tbzm = bcast_row(nmursz, nmursz_b, 0, w, 5)
        zn_toks = []
        gb = B("GBUF")
        for c in range(NC):
            tyb = B("TYF")

            def zn1(e, c=c, w=w):
                return e.tensor_tensor(TYF[:, 0:w], ZB[:, c, 0:w], PS[:, 6, 0:w],
                                       op=ALU.mult)
            tk1 = P.op("dve", zn1, deps=[tbz] + B(f"YB{c}").rdeps() + tyb.wdeps())
            B(f"YB{c}").read(tk1)
            psbank(6).read(tk1)
            tyb.wrote(tk1)

            def zn2(e, c=c, w=w):
                return e.tensor_tensor(GBUF[:, c, 0:w], TYF[:, 0:w], PS[:, 5, 0:w],
                                       op=ALU.add)
            tk = P.op("dve", zn2, deps=[tbzm, tk1] + gb.wdeps())
            zn_toks.append(tk)
            tyb.read(tk)
            psbank(5).read(tk)
        gb.wrote(*zn_toks)
        for nc_i in range(NC):
            bank = nc_i % 2
            pb = psbank(bank)
            fns = []
            for c in range(NC):
                def mm(e, c=c, nc_i=nc_i, w=w, bank=bank):
                    return e.matmul(PS[:, bank, 0:w], W2S[:, c, 128 * nc_i:128 * nc_i + 128],
                                    GBUF[:, c, 0:w], start=(c == 0), stop=False)
                fns.append(mm)

            def mmb(e, nc_i=nc_i, w=w, bank=bank):
                return e.matmul(PS[:, bank, 0:w], b2eff[0:1, 128 * nc_i:128 * nc_i + 128],
                                onesb[0:1, 0:w], start=False, stop=True)
            fns.append(mmb)
            tk = P.ops("pe", fns, deps=zn_toks + w2s_b.rdeps()
                       + b2eff_b.rdeps() + [t_onesb] + pb.wdeps())
            pb.wrote(tk)
            gb.read(tk)
            w2s_b.read(tk)
            b2eff_b.read(tk)
            yb_ = B(f"YB{nc_i}")

            def ev(e, nc_i=nc_i, w=w, bank=bank):
                return e.activation(YB[:, nc_i, 0:w], PS[:, bank, 0:w], AF.Copy)
            te = P.op("act", ev, deps=[tk] + yb_.wdeps())
            pb.read(te)
            yb_.wrote(te)
        rsy, rsy_b, nmursy, nmursy_b, _, _ = ln_stats_bf16(
            lambda c: (YB[:, c, 0:w], B(f"YB{c}").rdeps(), B(f"YB{c}")), w,
            "rsy", "nmursy", 0, GBUF, "GBUF", src_f32=True, make_bf=False)
        tby = bcast_row(rsy, rsy_b, 0, w, 6)
        tbm = bcast_row(nmursy, nmursy_b, 0, w, 5)
        xup = []
        for c in range(NC):
            tyb = B("TYF")

            def f1(e, c=c, w=w):
                return e.tensor_tensor(TYF[:, 0:w], YB[:, c, 0:w], PS[:, 6, 0:w],
                                       op=ALU.mult)
            t1 = P.op("dve", f1, deps=[tby] + B(f"YB{c}").rdeps() + tyb.wdeps())
            B(f"YB{c}").read(t1)
            psbank(6).read(t1)
            tyb.wrote(t1)

            def f2(e, c=c, w=w):
                return e.tensor_tensor(XGb[:, c, 0:w], XGb[:, c, 0:w], PS[:, 5, 0:w],
                                       op=ALU.add)
            t2 = P.op("dve", f2, deps=[tbm] + xgb_b.wdeps())
            psbank(5).read(t2)

            def f3(e, c=c, w=w):
                return e.tensor_tensor(XGb[:, c, 0:w], XGb[:, c, 0:w], TYF[:, 0:w],
                                       op=ALU.add)
            t3 = P.op("dve", f3, deps=[t1, t2])
            tyb.read(t3)
            xup.append(t3)
        xgb_b.wrote(*xup)
        store_grp(XGb, xgb_b, xdram, g, XD(g), "xs2")

    txa = tap_dram("xA", xdram)
    P.barrier()
    a2_st.close()
    sa_st.close()

    if upto == "a2":
        fin_dram_taps([txa])
        return nc

    rb_reset()
    rx("rs20", 512)
    rx("nmurs20", 512)
    rx("rs21", 512)
    rx("nmurs21", 512)
    b0_st = ExitStack()
    XGc = P.sb("XGc", [128, 2, NC, 512], F32, stack=b0_st)
    SQB0 = P.sb("SQB0", [128, 2, NC, 512], BF16, stack=b0_st)
    ld_pend = {}
    for g in range(TG):
        P.marker(f"b0-g{g}")
        w = gw(g)
        gp = g % 2
        xgc_b = B(f"XGc{gp}")
        if g == 0:
            ld_pend[0] = load_grp(XGc[:, 0], xgc_b, xdram, 0, XD(0), "xl3")
        t_load = ld_pend.pop(g)
        if g + 1 < TG:
            ld_pend[g + 1] = load_grp(XGc[:, (g + 1) % 2], B(f"XGc{(g + 1) % 2}"),
                                      xdram, g + 1, XD(g + 1), "xl3")
        bk_r, bk_m = (6, 5) if gp == 0 else (4, 3)
        rs2, rs2_b, nmurs2, nmurs2_b, _, _ = ln_stats_bf16(
            lambda c: (XGc[:, gp, c, 0:w], [t_load], xgc_b), w,
            f"rs2{gp}", f"nmurs2{gp}", 0, SQB0[:, gp], f"SQB0{gp}",
            src_f32=True, make_bf=False, par=gp)
        tbr = bcast_row(rs2, rs2_b, 0, w, bk_r)
        tbm = bcast_row(nmurs2, nmurs2_b, 0, w, bk_m)
        x2toks = []
        for c in range(NC):
            def f1(e, c=c, w=w, gp=gp, bk_r=bk_r):
                return e.tensor_tensor(XGc[:, gp, c, 0:w], XGc[:, gp, c, 0:w],
                                       PS[:, bk_r, 0:w], op=ALU.mult)
            t1 = P.op("dve", f1, deps=[tbr, t_load] + xgc_b.wdeps())
            psbank(bk_r).read(t1)

            def f2(e, c=c, w=w, gp=gp, bk_m=bk_m):
                return e.tensor_tensor(XGc[:, gp, c, 0:w], XGc[:, gp, c, 0:w],
                                       PS[:, bk_m, 0:w], op=ALU.add)
            t2 = P.op("dve", f2, deps=[tbm, t1])
            psbank(bk_m).read(t2)
            x2toks.append(t2)
        xgc_b.wrote(*x2toks)
        store_grp(XGc[:, gp], xgc_b, x2dram, g, X2DB(g), "xs3")
    P.barrier()
    b0_st.close()

    tx2 = tap_dram("x2", x2dram)
    if upto == "b0":
        fin_dram_taps([tx2])
        return nc

    def stage_b_layer(l, upto_l=None):
        rb_reset()
        b1_stk = ExitStack()
        X2G = P.sb(f"X2G_{l}", [128, NC, 513], F32, stack=b1_stk)
        VTB = P.sb(f"VTB_{l}", [128, TT, 512], BF16, stack=b1_stk)
        QR = P.sb(f"QR_{l}", [128, PAIRS_F, T], BF16, stack=b1_stk)
        KR = P.sb(f"KR_{l}", [128, PAIRS_F, T], BF16, stack=b1_stk)
        b1i_stk = ExitStack()
        WQKV = P.sb(f"WQKV_{l}", [128, NC, 1536], BF16, stack=b1i_stk)
        CQL = P.sb(f"CQL_{l}", [128, TT, 64], F32, stack=b1i_stk)
        SQL = P.sb(f"SQL_{l}", [128, TT, 64], F32, stack=b1i_stk)
        CKL = P.sb(f"CKL_{l}", [128, TT, 64], F32, stack=b1i_stk)
        SKL = P.sb(f"SKL_{l}", [128, TT, 64], F32, stack=b1i_stk)
        QS64 = P.sb(f"QS64_{l}", [128, 4, 64], F32, stack=b1i_stk)
        X2B = P.sb(f"X2B_{l}", [128, NC, 513], BF16, stack=b1i_stk)
        SQR = P.sb(f"SQR_{l}", [128, 2, 512], BF16, stack=b1i_stk)
        NRM = P.sb(f"NRM_{l}", [128, 2, 4, 8], F32, stack=b1i_stk)
        TAF = P.sb(f"TAF_{l}", [128, 2, 512], F32, stack=b1i_stk)
        TBF = P.sb(f"TBF_{l}", [128, 2, 512], F32, stack=b1i_stk)
        TCF = P.sb(f"TCF_{l}", [128, 2, 512], F32, stack=b1i_stk)
        QRtm = P.sb(f"QRtm_{l}", [128, 2, 4, 512], BF16, stack=b1i_stk)

        wqkv_toks = []
        for nq in range(3):
            tkq = P.dma("gp", WQKV[:, :, 512 * nq:512 * nq + 512],
                        wqkvd[l][:, 512 * nq:512 * nq + 512]
                        .rearrange("(kc p) n -> p kc n", p=128),
                        deps=B("WQKV").wdeps(), stream=f"wqkv{nq}")
            wqkv_toks.append(tkq)
        B("WQKV").wrote(*wqkv_toks)
        tbl_toks = []
        for (dst, dname, raw) in ((CQL, "CQL", CQd), (SQL, "SQL", SQd),
                                  (CKL, "CKL", CKd), (SKL, "SKL", SKd)):
            tk = P.dma("sp", dst[:, :, :], raw, deps=B(dname).wdeps(), stream="tbl")
            B(dname).wrote(tk)
            tbl_toks.append(tk)
        bias_ap, bias_b = rf("bias")
        for (iq, vec) in ((0, qscaled), (1, kscaled)):
            tk = P.dma("sp", ROWF[0:1, RF["bias"]:RF["bias"] + 64],
                       vec[l].rearrange("(one n) -> one n", one=1),
                       deps=bias_b.wdeps(), stream="bias")
            bias_b.wrote(tk)
            pb = psbank(6)

            def mm(e, iq=iq):
                return e.matmul(PS[:, 6, 0:64], ones[0:1, 0:128],
                                ROWF[0:1, RF["bias"]:RF["bias"] + 64],
                                start=True, stop=True)
            tk2 = P.op("pe", mm, deps=[tk, t_ones] + pb.wdeps())
            pb.wrote(tk2)
            bias_b.read(tk2)
            qs_b = B("QS64")

            def cp(e, iq=iq):
                return e.tensor_copy(QS64[:, iq, :], PS[:, 6, 0:64])
            tk3 = P.op("dve", cp, deps=[tk2] + qs_b.wdeps())
            pb.read(tk3)
            qs_b.w.append(tk3)
        qs_b = B("QS64")
        swp_toks = []
        for iq in range(2):
            def c1(e, iq=iq):
                return e.tensor_copy(QS64[:, 2 + iq, 0:32], QS64[:, iq, 32:64])
            ta = P.op("dve", c1, deps=qs_b.wdeps())

            def c2(e, iq=iq):
                return e.tensor_copy(QS64[:, 2 + iq, 32:64], QS64[:, iq, 0:32])
            tb2 = P.op("dve", c2, deps=qs_b.wdeps() + [ta])
            swp_toks.extend([ta, tb2])
        qs_b.w.extend(swp_toks)
        fold_toks = []
        for (dname, dst, iq) in (("CQL", CQL, 0), ("SQL", SQL, 2),
                                 ("CKL", CKL, 1), ("SKL", SKL, 3)):
            def ff(e, dst=dst, iq=iq):
                return e.tensor_tensor(
                    dst[:, :, :], dst[:, :, :],
                    QS64[:, iq:iq + 1, :].broadcast_to([128, TT, 64]), op=ALU.mult)
            tk = P.op("dve", ff, deps=B(dname).rdeps() + B("QS64").rdeps())
            B(dname).wrote(tk)
            B("QS64").read(tk)
            fold_toks.append(tk)

        tr_pend = []

        def drain_transposes(keep_last):
            while len(tr_pend) > keep_last:
                ptau, piq, ppar, pt10, PDST, pdstname = tr_pend.pop(0)
                qrtm_b = B(f"QRtm{piq}{ppar}")
                dstb = B(pdstname)
                for pr in range(4):
                    pbb = B("PSBbank")

                    def tr(e, piq=piq, ppar=ppar, pr=pr):
                        return e.transpose(PSB[:, 0:128],
                                           QRtm[:, piq, ppar, 128 * pr:128 * pr + 128],
                                           idb[:, :])
                    tt = P.op("pe", tr, deps=[pt10, t_idb] + pbb.wdeps())
                    pbb.wrote(tt)
                    qrtm_b.read(tt)

                    def cpv(e, pr=pr, ptau=ptau, PDST=PDST):
                        return e.tensor_copy(PDST[:, pr, 128 * ptau:128 * ptau + 128],
                                             PSB[:, 0:128])
                    tv = P.op("dve", cpv, deps=[tt] + dstb.wdeps())
                    pbb.read(tv)
                    dstb.w.append(tv)

        for g in range(TG):
            P.marker(f"b1-g{g}-L{l}")
            w = gw(g)
            x2g_b = B("X2G")
            x2b_b = B("X2B")
            if g == 0:
                tl = P.dma("sp", X2G[:, :, 1:513], x2dram[:, :, 0:512],
                           deps=x2g_b.wdeps() + X2DB(0).rdeps(), stream="x2l")

                def mz(e):
                    return e.memset(X2G[:, :, 0:1], 0.0)
                tz = P.op("dve", mz, deps=x2g_b.wdeps())
                ld_toks = [tl, tz]
            else:
                tl = P.dma("sp", X2G[:, :, 0:513],
                           x2dram[:, :, 512 * g - 1:512 * g + 512],
                           deps=x2g_b.wdeps() + X2DB(g).rdeps() + X2DB(g - 1).rdeps(),
                           stream="x2l")
                ld_toks = [tl]
            x2g_b.wrote(*ld_toks)
            X2DB(g).read(*ld_toks)
            casts = []
            for c in range(NC):
                def cp(e, c=c):
                    return e.activation(X2B[:, c, :], X2G[:, c, :], AF.Copy)
                tk = P.op("act", cp, deps=ld_toks + x2b_b.wdeps(), cost=600)
                casts.append(tk)
                x2g_b.read(tk)
            x2b_b.wrote(*casts)
            for j in range(w // 128):
                tau = 4 * g + j
                drain_transposes(3)
                bmap = (0, 1, 2) if tau % 2 == 0 else (3, 4, 5)
                mm_toks = []
                for nq in range(3):
                    bank = bmap[nq]
                    pb = psbank(bank)
                    fns = []
                    for c in range(NC):
                        off = (1 + 128 * j) if c < 4 else (128 * j)

                        def mm(e, c=c, nq=nq, off=off, bank=bank):
                            return e.matmul(PS[:, bank, :],
                                            X2B[:, c, off:off + 128],
                                            WQKV[:, c, 512 * nq:512 * nq + 512],
                                            start=(c == 0), stop=(c == NC - 1))
                        fns.append(mm)
                    tk = P.ops("pe", fns, deps=casts + [wqkv_toks[nq]] + pb.wdeps())
                    pb.wrote(tk)
                    x2b_b.read(tk)
                    mm_toks.append(tk)
                vtb_b = B(f"VTB{tau // 4}")

                def ev(e, tau=tau, bv=bmap[2]):
                    return e.activation(VTB[:, tau, :], PS[:, bv, :], AF.Copy)
                te = P.op("act", ev, deps=[mm_toks[2]] + vtb_b.wdeps())
                psbank(bmap[2]).read(te)
                vtb_b.w.append(te)
                for (iq, bank, TBL_C, TBL_S, DST, dstname) in (
                        (0, bmap[0], CQL, SQL, QR, "QR"), (1, bmap[1], CKL, SKL, KR, "KR")):
                    pb = psbank(bank)
                    cpar = (2 * tau + iq) % 2
                    tpar = tau % 2
                    sq_b = B(f"SQR{iq}")

                    def f_sq(e, bank=bank, iq=iq):
                        return e.activation(SQR[:, iq, :], PS[:, bank, :], AF.Square)
                    t1 = P.op("act", f_sq, deps=[mm_toks[iq]] + sq_b.wdeps(), cost=700)
                    sq_b.wrote(t1)
                    nrm_b = B(f"NRM{tpar}{iq}")

                    def f_red(e, iq=iq, tpar=tpar):
                        return e.reduce_sum(NRM[:, tpar, 2 * iq, :],
                                            SQR[:, iq, :].rearrange("p (h d) -> p h d", h=8),
                                            axis=AX.X)
                    t2 = P.op("dve", f_red, deps=[t1] + nrm_b.wdeps(), cost=700)
                    sq_b.read(t2)
                    nrm_b.w.append(t2)

                    def f_max(e, iq=iq, tpar=tpar):
                        return e.tensor_scalar_max(NRM[:, tpar, 2 * iq + 1, :],
                                                   NRM[:, tpar, 2 * iq, :], 1e-24)
                    t3 = P.op("dve", f_max, deps=[t2], cost=150)
                    nrm_b.w.append(t3)

                    def f_ln(e, iq=iq, tpar=tpar):
                        return e.activation(NRM[:, tpar, 2 * iq + 1, :],
                                            NRM[:, tpar, 2 * iq + 1, :], AF.Ln)
                    t4 = P.op("act", f_ln, deps=[t3], cost=250)
                    nrm_b.w.append(t4)

                    def f_exp(e, iq=iq, tpar=tpar):
                        return e.activation(NRM[:, tpar, 2 * iq, :],
                                            NRM[:, tpar, 2 * iq + 1, :],
                                            AF.Exp, scale=-0.5)
                    t5 = P.op("act", f_exp, deps=[t4], cost=250)
                    nrm_b.w.append(t5)
                    ta_b = B(f"TAF{cpar}")

                    def f_t0(e, iq=iq, bank=bank, cpar=cpar, tpar=tpar):
                        return e.tensor_tensor(
                            TAF[:, cpar, :].rearrange("p (h d) -> p h d", h=8),
                            PS[:, bank, :].rearrange("p (h d) -> p h d", h=8),
                            NRM[:, tpar, 2 * iq, :].unsqueeze(2)
                            .broadcast_to([128, 8, 64]),
                            op=ALU.mult)
                    t6 = P.op("dve", f_t0, deps=[t5, mm_toks[iq]] + ta_b.wdeps(), cost=700)
                    pb.read(t6)
                    nrm_b.r.append(t6)
                    ta_b.wrote(t6)
                    tb_b = B(f"TBF{cpar}")

                    def f_cos(e, tau=tau, TBL_C=TBL_C, cpar=cpar):
                        return e.tensor_tensor(
                            TBF[:, cpar, :].rearrange("p (h d) -> p h d", h=8),
                            TAF[:, cpar, :].rearrange("p (h d) -> p h d", h=8),
                            TBL_C[:, tau:tau + 1, :].broadcast_to([128, 8, 64]),
                            op=ALU.mult)
                    t7 = P.op("dve", f_cos, deps=[t6] + fold_toks + tb_b.wdeps(), cost=700)
                    ta_b.read(t7)
                    tb_b.wrote(t7)
                    def f_sinA(e, tau=tau, TBL_S=TBL_S, cpar=cpar):
                        return e.tensor_tensor(
                            TCF[:, cpar, :].rearrange("p (h d) -> p h d", h=8)[:, :, 0:32],
                            TAF[:, cpar, :].rearrange("p (h d) -> p h d", h=8)[:, :, 32:64],
                            TBL_S[:, tau:tau + 1, 0:32].broadcast_to([128, 8, 32]),
                            op=ALU.mult)
                    tc_b = B(f"TCF{cpar}")
                    t8 = P.op("dve", f_sinA, deps=[t6, t7] + fold_toks + tc_b.wdeps(),
                              cost=450)

                    def f_sinB(e, tau=tau, TBL_S=TBL_S, cpar=cpar):
                        return e.tensor_tensor(
                            TCF[:, cpar, :].rearrange("p (h d) -> p h d", h=8)[:, :, 32:64],
                            TAF[:, cpar, :].rearrange("p (h d) -> p h d", h=8)[:, :, 0:32],
                            TBL_S[:, tau:tau + 1, 32:64].broadcast_to([128, 8, 32]),
                            op=ALU.mult)
                    t9 = P.op("dve", f_sinB, deps=[t6, t7, t8], cost=450)
                    ta_b.read(t8, t9)
                    tc_b.wrote(t8, t9)
                    par = tau % 4
                    qrtm_b = B(f"QRtm{iq}{par}")

                    def f_add(e, iq=iq, par=par, cpar=cpar):
                        return e.tensor_tensor(QRtm[:, iq, par, :], TBF[:, cpar, :],
                                               TCF[:, cpar, :], op=ALU.add)
                    t10 = P.op("dve", f_add, deps=[t7, t8, t9] + qrtm_b.wdeps(), cost=700)
                    tb_b.read(t10)
                    tc_b.read(t10)
                    qrtm_b.wrote(t10)
                    tr_pend.append((tau, iq, par, t10, DST, dstname))

        drain_transposes(0)
        tap("VTB", VTB[:, :, :], [t for g in range(TG) for t in B(f"VTB{g}").w], dtype=BF16)
        tap("QR", QR[:, :, :], B("QR").w, dtype=BF16)
        tap("KR", KR[:, :, :], B("KR").w, dtype=BF16)
        if upto_l == "b1":
            return b1i_stk, b1_stk

        P.barrier()
        b1i_stk.close()
        b2_stk = ExitStack()
        WOS = P.sb(f"WOS_{l}", [128, 4, 1024], BF16, stack=b2_stk)
        AVF = P.sb(f"AVF_{l}", [128, PAIRS_F, T], BF16, stack=b2_stk)
        PTB = P.sb(f"PTB_{l}", [128, 2, 1280], BF16, stack=b2_stk)
        RCPT = P.sb(f"RCPT_{l}", [1, 2 * PAIRS_F, T], BF16, stack=b2_stk)
        t_wo = P.dma("gp", WOS[:, :, :],
                     wod[l].rearrange("(ic p) n -> p ic n", p=128),
                     deps=B("WOS").wdeps(), stream="wo")
        B("WOS").wrote(t_wo)
        PSf = PS.rearrange("p b n -> p (b n)")

        qr_b = B("QR")
        kr_b = B("KR")
        PSW = PS.rearrange("p b n -> p (b n)")

        def blk(par):
            return (1536 * par, [psbank(3 * par), psbank(3 * par + 1), psbank(3 * par + 2)])

        def segs_of(fb, ncw):
            out = []
            pos = fb
            endp = fb + 128 * ncw
            while pos < endp:
                seg = min(endp - pos, 512 - (pos % 512) if pos % 512 else 512)
                out.append((pos, seg))
                pos += seg
            return out

        for i in range(PAIRS_F):
            P.marker(f"b2-pair{i}-L{l}")

            def front(tau, i=i):
                c0 = max(0, tau - 4)
                ncw = tau - c0 + 1
                par = tau % 2
                fb, bbufs = blk(par)
                bdeps = [d for b_ in bbufs for d in b_.wdeps()]
                sl_hi = tau - c0
                lo_mask = (c0 == tau - 4)
                sim_fns = []
                for sidx in range(ncw):
                    for h, (p0, p1) in enumerate(((0, 64), (64, 128))):
                        c = c0 + sidx
                        off = fb + 640 * h + 128 * sidx
                        bank, boff = off // 512, off % 512

                        def mm(e, i=i, tau=tau, c=c, bank=bank, boff=boff,
                               p0=p0, p1=p1):
                            return e.matmul(PS[:, bank, boff:boff + 128],
                                            KR[p0:p1, i, 128 * c:128 * c + 128],
                                            QR[p0:p1, i, 128 * tau:128 * tau + 128],
                                            start=True, stop=True,
                                            tile_position=(p0, 0))
                        sim_fns.append(mm)
                tks = P.ops("pe", sim_fns, deps=qr_b.rdeps() + kr_b.rdeps() + bdeps,
                            cost=200 + 150 * ncw)
                qr_b.read(tks)
                kr_b.read(tks)
                for b_ in bbufs:
                    b_.wrote(tks)
                m2D = PSW[:, fb:fb + 1280].rearrange("p (two n) -> p two n", two=2)
                mtoks = [tks]

                def addm(e, m2D=m2D, sl=sl_hi):
                    return e.tensor_tensor(m2D[:, :, 128 * sl:128 * sl + 128],
                                           m2D[:, :, 128 * sl:128 * sl + 128],
                                           MK[:, 0:1, :].broadcast_to([128, 2, 128]),
                                           op=ALU.add)
                tm1 = P.op("dve", addm, deps=[tks] + t_mk, cost=500)
                mtoks.append(tm1)
                if lo_mask and sl_hi != 0:
                    def addl(e, m2D=m2D):
                        return e.tensor_tensor(m2D[:, :, 0:128],
                                               m2D[:, :, 0:128],
                                               MK[:, 1:2, :].broadcast_to([128, 2, 128]),
                                               op=ALU.add)
                    tm2 = P.op("dve", addl, deps=[tks, tm1] + t_mk, cost=500)
                    mtoks.append(tm2)
                for b_ in bbufs:
                    b_.w.extend(mtoks[1:])
                ptb_b = B(f"PTB{par}")
                exp_deps = mtoks + ptb_b.wdeps()
                tex_l = []
                for h in range(2):
                    def fex(e, fb=fb, h=h, par=par, ww=128 * ncw):
                        return e.activation(PTB[:, par, 640 * h:640 * h + ww],
                                            PSW[:, fb + 640 * h:fb + 640 * h + ww],
                                            AF.Exp, scale=8.0)
                    tex_l.append(P.op("act", fex, deps=exp_deps + tex_l,
                                      cost=200 + 130 * ncw))
                for b_ in bbufs:
                    b_.read(*tex_l)
                ptb_b.wrote(*tex_l)

            def mid(tau, i=i):
                c0 = max(0, tau - 4)
                ncw = tau - c0 + 1
                par = tau % 2
                fb, bbufs = blk(par)
                ptb_b = B(f"PTB{par}")
                avr_b = B(f"avr{par}")
                s6_b = B("s6")
                fns = []
                for h in range(2):
                    for sidx in range(ncw):
                        c = c0 + sidx

                        def pv(e, i=i, h=h, c=c, sidx=sidx, ncw=ncw, fb=fb, par=par):
                            if h == 0:
                                pslice = PSW[0:64, fb + 1280:fb + 1408]
                            else:
                                pslice = PSW[64:128, fb + 1408:fb + 1536]
                            return e.matmul(pslice,
                                            VTB[:, c, 64 * (2 * i + h):64 * (2 * i + h) + 64],
                                            PTB[:, par,
                                                640 * h + 128 * sidx:640 * h + 128 * sidx + 128],
                                            start=(sidx == 0), stop=(sidx == ncw - 1),
                                            tile_position=(0, 64 * h))
                        fns.append(pv)
                for h in range(2):
                    for sidx in range(ncw):
                        def sm(e, h=h, sidx=sidx, ncw=ncw, par=par):
                            return e.matmul(PS[0:1, 6, 256 + 128 * h:384 + 128 * h],
                                            onescol[:, :],
                                            PTB[:, par,
                                                640 * h + 128 * sidx:640 * h + 128 * sidx + 128],
                                            start=(sidx == 0), stop=(sidx == ncw - 1))
                        fns.append(sm)
                vdeps = []
                for cc in range(c0, tau + 1):
                    vdeps += B(f"VTB{cc // 4}").rdeps()
                tk = P.ops("pe", fns, deps=ptb_b.rdeps() + vdeps + avr_b.wdeps()
                           + s6_b.wdeps() + [t_onescol], cost=150 + 200 * ncw)
                for cc in range(c0, tau + 1):
                    B(f"VTB{cc // 4}").read(tk)
                ptb_b.read(tk)
                avr_b.wrote(tk)
                s6_b.wrote(tk)
                rcp_b = B(f"RCPT{i}")
                trcs = []
                for h in range(2):
                    def frc(e, tau=tau, h=h, i=i):
                        with nc.allow_low_precision(reason="bf16 recip rows for attn norm"):
                            return e.reciprocal(RCPT[0:1, 2 * i + h,
                                                     128 * tau:128 * tau + 128],
                                                PS[0:1, 6, 256 + 128 * h:384 + 128 * h])
                    trc = P.op("dve", frc, deps=[tk] + rcp_b.wdeps() + trcs, cost=300)
                    trcs.append(trc)
                s6_b.read(*trcs)
                rcp_b.w.extend(trcs)

            def back(tau, i=i):
                par = tau % 2
                fb, _ = blk(par)
                avr_b = B(f"avr{par}")
                avf_b = B("AVF")

                def evA(e, i=i, tau=tau, fb=fb):
                    return e.activation(AVF[0:64, i, 128 * tau:128 * tau + 128],
                                        PSW[0:64, fb + 1280:fb + 1408], AF.Copy)
                teA = P.op("act", evA, deps=avr_b.rdeps() + avf_b.wdeps(), cost=250)

                def evB(e, i=i, tau=tau, fb=fb):
                    return e.activation(AVF[64:128, i, 128 * tau:128 * tau + 128],
                                        PSW[64:128, fb + 1408:fb + 1536], AF.Copy)
                teB = P.op("act", evB, deps=avr_b.rdeps() + [teA] + avf_b.wdeps(),
                           cost=250)
                avr_b.read(teA, teB)
                avf_b.w.extend([teA, teB])

            for tau in range(TT):
                front(tau)
                if tau >= 1:
                    mid(tau - 1)
                if tau >= 2:
                    back(tau - 2)
            mid(TT - 1)
            back(TT - 2)
            back(TT - 1)

        tap("AVF", AVF[:, :, :], B("AVF").w, dtype=BF16)
        if upto_l == "b2":
            return b2_stk, b1_stk

        X2R = P.sb(f"X2R_{l}", [128, NC, 512], F32, stack=b2_stk)
        avf_b = B("AVF")
        for g in range(TG):
            P.marker(f"b3-g{g}-L{l}")
            w = gw(g)
            x2r_b = B("X2R")
            tl = load_grp(X2R, x2r_b, x2dram, g, X2DB(g), "x2r")
            for ic in range(PAIRS_F):
                rcp_b = B(f"RCPT{ic}")
                pb2 = psbank(2)
                pb3 = psbank(3)

                def nbA(e, ic=ic, g=g, w=w):
                    return e.matmul(PS[:, 2, 0:w], onesb[0:1, 0:128],
                                    RCPT[0:1, 2 * ic, 512 * g:512 * g + w],
                                    start=True, stop=True)

                def nbB(e, ic=ic, g=g, w=w):
                    return e.matmul(PS[:, 3, 0:w], onesb[0:1, 0:128],
                                    RCPT[0:1, 2 * ic + 1, 512 * g:512 * g + w],
                                    start=True, stop=True)
                tb_ = P.ops("pe", [nbA, nbB], deps=rcp_b.rdeps() + [t_onesb]
                            + pb2.wdeps() + pb3.wdeps()
                            + B("avr0").wdeps() + B("avr1").wdeps(), cost=2000)
                rcp_b.read(tb_)
                pb2.wrote(tb_)
                pb3.wrote(tb_)

                def nmA(e, ic=ic, g=g, w=w):
                    return e.tensor_tensor(AVF[0:64, ic, 512 * g:512 * g + w],
                                           AVF[0:64, ic, 512 * g:512 * g + w],
                                           PS[0:64, 2, 0:w], op=ALU.mult)
                tnA = P.op("dve", nmA, deps=[tb_] + avf_b.wdeps(), cost=700)

                def nmB(e, ic=ic, g=g, w=w):
                    return e.tensor_tensor(AVF[64:128, ic, 512 * g:512 * g + w],
                                           AVF[64:128, ic, 512 * g:512 * g + w],
                                           PS[64:128, 3, 0:w], op=ALU.mult)
                tnB = P.op("dve", nmB, deps=[tb_, tnA] + avf_b.wdeps(), cost=700)
                pb2.read(tnA)
                pb3.read(tnB)
                avf_b.w.extend([tnA, tnB])
            up_toks = []
            for nci in range(NC):
                bank = nci % 2
                pb = psbank(bank)
                fns = []
                for ic in range(4):
                    def mm(e, ic=ic, nci=nci, g=g, w=w, bank=bank):
                        return e.matmul(PS[:, bank, 0:w],
                                        WOS[:, ic, 128 * nci:128 * nci + 128],
                                        AVF[:, ic, 512 * g:512 * g + w],
                                        start=(ic == 0), stop=(ic == 3))
                    fns.append(mm)
                tk = P.ops("pe", fns, deps=avf_b.rdeps() + [t_wo] + pb.wdeps())
                pb.wrote(tk)
                avf_b.read(tk)

                def fr(e, nci=nci, w=w, bank=bank):
                    return e.tensor_tensor(X2R[:, nci, 0:w], X2R[:, nci, 0:w],
                                           PS[:, bank, 0:w], op=ALU.add)
                t2 = P.op("dve", fr, deps=[tk, tl])
                pb.read(t2)
                up_toks.append(t2)
            x2r_b.wrote(*up_toks)
            store_grp(X2R, x2r_b, x2dram, g, X2DB(g), "x2w")
        if upto_l == "b3":
            return (b2_stk, b1_stk)
        P.barrier()
        b2_stk.close()
        b1_stk.close()

        f_stk = ExitStack()
        X2G4 = P.sb(f"X2G4_{l}", [128, 2, NC, 513], F32, stack=f_stk)
        XNF = P.sb(f"XNF_{l}", [128, NC, T], BF16, stack=f_stk)
        HG = P.sb(f"HG_{l}", [128, 11, T], BF16, stack=f_stk)
        W1C = P.sb(f"W1C_{l}", [128, NC, 2, 2, 128], BF16, stack=f_stk)
        GELS = P.sb(f"GELS_{l}", [128, 2, 512], F32, stack=f_stk)
        f_sq_stk = ExitStack()
        SQF = P.sb(f"SQF_{l}", [128, NC, 512], BF16, stack=f_sq_stk)
        rx("rsf0", T // 2)
        rx("rsf0m", T // 2)
        rx("rsf1", T // 2)
        rx("rsf1m", T // 2)
        gcolf_b = vec_to_cols(lngd[l].rearrange("n -> (n)") if False else lngd[l],
                              1024, 48, f"gcolf{l}")
        lnbf_b = vec_to_cols(lnbd[l], 1024, 56, f"lnbcol{l}")
        lnbb_b = B(f"lnbcolb{l}")
        bt2 = []
        for c in range(NC):
            def cpb(e, c=c):
                return e.tensor_copy(COLB[:, 8 + c:9 + c], COLF[:, 56 + c:57 + c])
            tkb = P.op("dve", cpb, deps=lnbf_b.rdeps() + lnbb_b.wdeps())
            lnbf_b.read(tkb)
            bt2.append(tkb)
        lnbb_b.wrote(*bt2)
        rx("b2f", 1024)
        b2frow, b2f_b = vec_to_row(b2fd[l], 1024, "b2f", f32=True, stream="bias2")
        for _rot in (0, 1):
            for _pu in (0, 1):
                B(f"fbias{_rot}{_pu}").r.extend(B("rf_bias").w + B("rf_bias").r)

        P.marker(f"b4stats-L{l}")
        def b4s_load(g):
            gp = g % 2
            x4_b = B(f"X2G4{gp}")
            if g == 0:
                tl = P.dma("sp", X2G4[:, 0, :, 1:513], x2dram[:, :, 0:512],
                           deps=x4_b.wdeps() + X2DB(0).rdeps(), stream="x2f")

                def mz(e):
                    return e.memset(X2G4[:, 0, :, 0:1], 0.0)
                tz = P.op("dve", mz, deps=x4_b.wdeps())
                ld_toks = [tl, tz]
            else:
                tl = P.dma("sp", X2G4[:, gp, :, 0:513],
                           x2dram[:, :, 512 * g - 1:512 * g + 512],
                           deps=x4_b.wdeps() + X2DB(g).rdeps() + X2DB(g - 1).rdeps(),
                           stream="x2f")
                ld_toks = [tl]
            x4_b.wrote(*ld_toks)
            X2DB(g).read(*ld_toks)
            return ld_toks

        b4s_pend = {0: b4s_load(0)}
        for g in range(TG):
            w = gw(g)
            gp = g % 2
            x4_b = B(f"X2G4{gp}")
            ld_toks = b4s_pend.pop(g)
            if g + 1 < TG:
                b4s_pend[g + 1] = b4s_load(g + 1)

            bk_r, bk_m = (6, 5) if gp == 0 else (4, 3)
            roff = 512 * (g // 2)

            def shsrc(c, gp=gp, ld_toks=ld_toks, x4_b=x4_b):
                off = 1 if c < 4 else 0
                return (X2G4[:, gp, c, off:off + w], list(ld_toks), x4_b)
            rsf, rsf_b, nmursf, nmursf_b, _, _ = ln_stats_bf16(
                shsrc, w, f"rsf{gp}", f"rsf{gp}m", roff, SQF, "SQF",
                src_f32=True, make_bf=False, par=gp)
            tbr = bcast_row(rsf, rsf_b, roff, w, bk_r)
            tbm = bcast_row(nmursf, nmursf_b, roff, w, bk_m)
            xnf_b = B("XNF")
            for c in range(NC):
                off = 1 if c < 4 else 0
                tyb = B(f"GELS{gp}")

                def x1(e, c=c, w=w, off=off, gp=gp, bk_r=bk_r):
                    return e.scalar_tensor_tensor(GELS[:, gp, 0:w],
                                                  X2G4[:, gp, c, off:off + w],
                                                  COLF[:, 48 + c:49 + c], PS[:, bk_r, 0:w],
                                                  op0=ALU.mult, op1=ALU.mult)
                t1 = P.op("dve", x1, deps=[tbr] + list(ld_toks) + gcolf_b.rdeps()
                          + tyb.wdeps())
                x4_b.read(t1)
                gcolf_b.read(t1)
                psbank(bk_r).read(t1)
                tyb.wrote(t1)

                def x2f_(e, c=c, w=w, g=g, gp=gp, bk_m=bk_m):
                    return e.scalar_tensor_tensor(XNF[:, c, 512 * g:512 * g + w],
                                                  PS[:, bk_m, 0:w], COLF[:, 48 + c:49 + c],
                                                  GELS[:, gp, 0:w], op0=ALU.mult, op1=ALU.add)
                t2 = P.op("dve", x2f_, deps=[tbm, t1] + xnf_b.wdeps())
                psbank(bk_m).read(t2)
                tyb.read(t2)
                gcolf_b.read(t2)
                xnf_b.w.append(t2)

        f_sq_stk.close()
        W2H = P.sb(f"W2H_{l}", [128, 11, 1024], BF16, stack=f_stk)
        for half in range(2):
            P.marker(f"b4half{half}-L{l}")
            w2h_b = B("W2H")
            r0 = 1408 * half
            nfull = 11 if half == 0 else 10
            tw2toks = []
            tw2a = P.dma("gp", W2H[:, 0:nfull, :],
                         w2fd[l][r0:r0 + 128 * nfull, :]
                         .rearrange("(kc p) n -> p kc n", p=128),
                         deps=w2h_b.wdeps() + B("SQF").wdeps(), stream="w2h")
            tw2toks.append(tw2a)
            if half == 1:
                tw2b = P.dma("gp", W2H[0:42, 10, :], w2fd[l][2688:2730, :],
                             deps=w2h_b.wdeps(), stream="w2h")
                tw2toks.append(tw2b)
            w2h_b.wrote(*tw2toks)
            def issue_w1(ci, half=half):
                gci = 11 * half + ci
                aw = 128 if gci < 21 else 42
                rot = ci % 2
                wc_b = B(f"W1C{rot}")
                t_wa = P.dma("gp", W1C[:, :, 0, rot, 0:aw],
                             w1fd[l][:, 128 * gci:128 * gci + aw]
                             .rearrange("(kc p) n -> p kc n", p=128),
                             deps=wc_b.wdeps(), stream=f"w1c{rot}")
                t_wg = P.dma("gp", W1C[:, :, 1, rot, 0:aw],
                             w1fd[l][:, 2730 + 128 * gci:2730 + 128 * gci + aw]
                             .rearrange("(kc p) n -> p kc n", p=128),
                             deps=wc_b.wdeps(), stream=f"w1c{rot}")
                wc_b.wrote(t_wa, t_wg)
                for (pu, boff) in ((0, 128 * gci), (1, 2730 + 128 * gci)):
                    boffr = 256 * rot + 128 * pu
                    bias_b2 = B(f"fbias{rot}{pu}")
                    tb1 = P.dma("sp", ROWF[0:1, RF["bias"] + boffr:RF["bias"] + boffr + aw],
                                b1fd[l].rearrange("(one n) -> one n", one=1)
                                [0:1, boff:boff + aw],
                                deps=bias_b2.wdeps(), stream=f"bias{rot}{pu}")
                    bias_b2.wrote(tb1)
                return t_wa, t_wg

            w1dma = {0: issue_w1(0)}
            for ci in range(11):
                gci = 11 * half + ci
                aw = 128 if gci < 21 else 42
                rot = ci % 2
                wc_b = B(f"W1C{rot}")
                t_wa, t_wg = w1dma.pop(ci)
                if ci + 1 < 11:
                    w1dma[ci + 1] = issue_w1(ci + 1)
                bcol_toks = []
                for (pu, boff) in ((0, 128 * gci), (1, 2730 + 128 * gci)):
                    pb = psbank(4)
                    fns = []
                    for c in range(NC):
                        def mm(e, c=c, pu=pu, rot=rot, aw=aw):
                            return e.matmul(PS[0:1, 4, 0:aw], COLB[:, 8 + c:9 + c],
                                            W1C[:, c, pu, rot, 0:aw],
                                            start=(c == 0), stop=(c == NC - 1))
                        fns.append(mm)
                    tkc = P.ops("pe", fns, deps=[t_wa, t_wg] + lnbb_b.rdeps() + pb.wdeps())
                    pb.wrote(tkc)
                    wc_b.read(tkc)
                    lnbb_b.read(tkc)
                    boffr = 256 * rot + 128 * pu
                    bias_b2 = B(f"fbias{rot}{pu}")
                    tb1 = bias_b2.w[-1]

                    def addb(e, aw=aw, boffr=boffr):
                        return e.scalar_tensor_tensor(
                            ROWF[0:1, RF["bias"] + boffr:RF["bias"] + boffr + aw],
                            PS[0:1, 4, 0:aw], 1.0,
                            ROWF[0:1, RF["bias"] + boffr:RF["bias"] + boffr + aw],
                            op0=ALU.mult, op1=ALU.add)
                    ta = P.op("dve", addb, deps=[tkc, tb1])
                    pb.read(ta)
                    bias_b2.wrote(ta)
                    slot = 64 + 4 * rot + pu
                    pb2 = psbank(4)

                    def trp(e, aw=aw, boffr=boffr):
                        return e.transpose(PS[0:aw, 4, 4:5],
                                           ROWF[0:1, RF["bias"] + boffr:RF["bias"] + boffr + aw],
                                           idf[0:1, 0:1])
                    tt = P.op("pe", trp, deps=[ta, t_idf] + pb2.wdeps())
                    pb2.wrote(tt)
                    bias_b2.read(tt)
                    cbuf = B(f"bcol{pu}{rot}")

                    def cpc(e, aw=aw, slot=slot):
                        return e.tensor_copy(COLF[0:aw, slot:slot + 1], PS[0:aw, 4, 4:5])
                    tc2 = P.op("dve", cpc, deps=[tt] + cbuf.wdeps())
                    pb2.read(tc2)
                    cbuf.wrote(tc2)
                    bcol_toks.append(tc2)
                for g in range(TG):
                    w = gw(g)
                    ba = g % 2
                    bg = 2 + g % 2
                    pa = psbank(ba)
                    pg = psbank(bg)
                    fns_a = []
                    fns_g = []
                    for c in range(NC):
                        def mma(e, c=c, rot=rot, aw=aw, g=g, w=w, ba=ba):
                            return e.matmul(PS[0:aw, ba, 0:w], W1C[:, c, 0, rot, 0:aw],
                                            XNF[:, c, 512 * g:512 * g + w],
                                            start=(c == 0), stop=(c == NC - 1))
                        fns_a.append(mma)

                        def mmg(e, c=c, rot=rot, aw=aw, g=g, w=w, bg=bg):
                            return e.matmul(PS[0:aw, bg, 0:w], W1C[:, c, 1, rot, 0:aw],
                                            XNF[:, c, 512 * g:512 * g + w],
                                            start=(c == 0), stop=(c == NC - 1))
                        fns_g.append(mmg)
                    xnf_b = B("XNF")
                    tka = P.ops("pe", fns_a + fns_g, deps=[t_wa, t_wg] + xnf_b.rdeps()
                                + pa.wdeps() + pg.wdeps(), cost=3700)
                    tkg = tka
                    pa.wrote(tka)
                    pg.wrote(tka)
                    wc_b.read(tka)
                    xnf_b.read(tka)
                    gpar = g % 2
                    gel_b = B(f"GELS{gpar}")
                    sga = 64 + 4 * rot
                    bca = B(f"bcol0{rot}")
                    bcg = B(f"bcol1{rot}")

                    def fg(e, aw=aw, w=w, bg=bg, sga=sga, gpar=gpar):
                        return e.activation(GELS[0:aw, gpar, 0:w], PS[0:aw, bg, 0:w],
                                            AF.Gelu, bias=COLF[0:aw, sga + 1:sga + 2])
                    tge = P.op("act", fg, deps=[tkg] + bcg.rdeps() + gel_b.wdeps(),
                               cost=700)
                    pg.read(tge)
                    bcg.read(tge)
                    gel_b.wrote(tge)
                    hg_b = B(f"HG{ci}")

                    def fh(e, aw=aw, w=w, g=g, ci=ci, ba=ba, sga=sga, gpar=gpar):
                        return e.scalar_tensor_tensor(HG[0:aw, ci, 512 * g:512 * g + w],
                                                      PS[0:aw, ba, 0:w],
                                                      COLF[0:aw, sga:sga + 1],
                                                      GELS[0:aw, gpar, 0:w],
                                                      op0=ALU.add, op1=ALU.mult)
                    th = P.op("dve", fh, deps=[tka, tge] + bca.rdeps() + hg_b.wdeps(),
                              cost=700)
                    pa.read(th)
                    gel_b.read(th)
                    bca.read(th)
                    hg_b.wrote(th)
            w2r_pend = {}
            for g in range(TG):
                w = gw(g)
                gp = g % 2
                x4_b = B(f"X2G4{gp}")
                if g == 0:
                    w2r_pend[0] = P.dma("sp", X2G4[:, 0, :, 0:w],
                                        x2dram[:, :, 0:w],
                                        deps=x4_b.wdeps() + X2DB(0).rdeps(),
                                        stream="x2r2")
                    B("X2G40").wrote(w2r_pend[0])
                    X2DB(0).read(w2r_pend[0])
                tl = w2r_pend.pop(g)
                if g + 1 < TG:
                    gpn = (g + 1) % 2
                    x4n = B(f"X2G4{gpn}")
                    tln = P.dma("sp", X2G4[:, gpn, :, 0:gw(g + 1)],
                                x2dram[:, :, 512 * (g + 1):512 * (g + 1) + gw(g + 1)],
                                deps=x4n.wdeps() + X2DB(g + 1).rdeps(), stream="x2r2")
                    x4n.wrote(tln)
                    X2DB(g + 1).read(tln)
                    w2r_pend[g + 1] = tln
                up_toks = []
                for nci in range(NC):
                    bank = 4 + nci % 2
                    pb = psbank(bank)
                    fns = []
                    nk = 11 if half == 0 else 11
                    for ci in range(11):
                        gci = 11 * half + ci
                        aw = 128 if gci < 21 else 42

                        def mm(e, ci=ci, aw=aw, nci=nci, g=g, w=w, bank=bank, half=half):
                            return e.matmul(PS[:, bank, 0:w],
                                            W2H[0:aw, ci, 128 * nci:128 * nci + 128],
                                            HG[0:aw, ci, 512 * g:512 * g + w],
                                            start=(ci == 0), stop=(ci == 10 and half == 1))
                        fns.append(mm)
                    if half == 0:
                        def mmb(e, nci=nci, w=w, bank=bank, g=g):
                            return e.matmul(PS[:, bank, 0:w],
                                            b2frow[0:1, 128 * nci:128 * nci + 128],
                                            ones[0:1, 0:w], start=False, stop=True)
                        fns.append(mmb)
                    hdeps = [t for ci in range(11) for t in B(f"HG{ci}").rdeps()]
                    tk = P.ops("pe", fns, deps=tw2toks + hdeps + b2f_b.rdeps()
                               + [t_ones] + pb.wdeps())
                    pb.wrote(tk)
                    w2h_b.read(tk)
                    for ci in range(11):
                        B(f"HG{ci}").read(tk)
                    b2f_b.read(tk)

                    def fr(e, nci=nci, w=w, bank=bank, gp=gp):
                        return e.tensor_tensor(X2G4[:, gp, nci, 0:w], X2G4[:, gp, nci, 0:w],
                                               PS[:, bank, 0:w], op=ALU.add)
                    t2 = P.op("dve", fr, deps=[tk, tl])
                    pb.read(t2)
                    up_toks.append(t2)
                x4_b.wrote(*up_toks)
                tst = P.dma("sp", x2dram[:, :, 512 * g:512 * g + w],
                            X2G4[:, gp, :, 0:w],
                            deps=x4_b.rdeps() + X2DB(g).wdeps(), stream="x2w2")
                x4_b.read(tst)
                X2DB(g).wrote(tst)
        P.barrier()
        f_stk.close()
        return ()
    if upto in ("b1", "b2", "b3", "b4"):
        stks = stage_b_layer(0, upto_l=None if upto == "b4" else upto)
        if upto == "b4":
            tpf = tap_dram("post_ffn", x2dram)
            fin_dram_taps([tpf])
            return nc
        if upto == "b3":
            tpw = tap_dram("post_wo", x2dram)
            for st_ in stks:
                if st_ is not None:
                    P.barrier()
                    st_.close()
            fin_dram_taps([tpw])
            return nc
        P.barrier()
        P.finalize(tap_toks)
        P.materialize()
        for st_ in stks:
            if st_ is not None:
                st_.close()
        P.es.close()
        return nc

    for l in range(2):
        stage_b_layer(l)

    rb_reset()
    rx("rs30", 512)
    rx("nmurs30", 512)
    rx("rs31", 512)
    rx("nmurs31", 512)
    fin_st = ExitStack()
    FG = P.sb("FG", [128, 2, NC, 512], F32, stack=fin_st)
    XAG = P.sb("XAG", [128, 2, NC, 512], F32, stack=fin_st)
    SQN = P.sb("SQN", [128, NC, 512], BF16, stack=fin_st)
    TFN = P.sb("TFN", [128, 512], F32, stack=fin_st)
    OUTM = P.sb("OUTM", [128, 2, D], F32, stack=fin_st)
    out_toks = []
    fin_pend = {}

    def fin_load(g):
        gp = g % 2
        return (load_grp(FG[:, gp], B(f"FG{gp}"), x2dram, g, X2DB(g), "fl"),
                load_grp(XAG[:, gp], B(f"XAG{gp}"), xdram, g, XD(g), "xal"))

    fin_pend[0] = fin_load(0)
    for g in range(TG):
        P.marker(f"fin-g{g}")
        w = gw(g)
        gp = g % 2
        fg_b = B(f"FG{gp}")
        xag_b = B(f"XAG{gp}")
        tlf, tlx = fin_pend.pop(g)
        if g + 1 < TG:
            fin_pend[g + 1] = fin_load(g + 1)
        bk_r, bk_m = (6, 5) if gp == 0 else (4, 3)
        rs3, rs3_b, nmurs3, nmurs3_b, _, _ = ln_stats_bf16(
            lambda c: (FG[:, gp, c, 0:w], [tlf], fg_b), w,
            f"rs3{gp}", f"nmurs3{gp}", 0, SQN, "SQN", src_f32=True,
            make_bf=False, par=gp)
        tbr = bcast_row(rs3, rs3_b, 0, w, bk_r)
        tbm = bcast_row(nmurs3, nmurs3_b, 0, w, bk_m)
        up_toks = []
        for c in range(NC):
            tfb = B("TFN")

            def f1(e, c=c, w=w, gp=gp, bk_r=bk_r):
                return e.tensor_tensor(TFN[:, 0:w], FG[:, gp, c, 0:w],
                                       PS[:, bk_r, 0:w], op=ALU.mult)
            t1 = P.op("dve", f1, deps=[tbr, tlf] + tfb.wdeps())
            fg_b.read(t1)
            psbank(bk_r).read(t1)
            tfb.wrote(t1)

            def f2(e, c=c, w=w, gp=gp, bk_m=bk_m):
                return e.tensor_tensor(XAG[:, gp, c, 0:w], XAG[:, gp, c, 0:w],
                                       PS[:, bk_m, 0:w], op=ALU.add)
            t2 = P.op("dve", f2, deps=[tbm, tlx] + xag_b.wdeps())
            psbank(bk_m).read(t2)

            def f3(e, c=c, w=w, gp=gp):
                return e.tensor_tensor(XAG[:, gp, c, 0:w], XAG[:, gp, c, 0:w],
                                       TFN[:, 0:w], op=ALU.add)
            t3 = P.op("dve", f3, deps=[t1, t2])
            tfb.read(t3)
            up_toks.append(t3)
        xag_b.wrote(*up_toks)
        for j in range(w // 128):
            tau = 4 * g + j
            sl = tau % 2
            om_b = B(f"OUTM{sl}")
            cp_toks = []
            for c in range(NC):
                bank = c % 4
                pb = psbank(bank)

                def tr(e, c=c, j=j, bank=bank, gp=gp):
                    return e.transpose(PS[:, bank, 0:128],
                                       XAG[:, gp, c, 128 * j:128 * j + 128],
                                       idf[:, :])
                tt = P.op("pe", tr, deps=up_toks + [t_idf] + pb.wdeps())
                pb.wrote(tt)
                xag_b.read(tt)

                def cp(e, c=c, sl=sl, bank=bank):
                    return e.tensor_copy(OUTM[:, sl, 128 * c:128 * c + 128],
                                         PS[:, bank, 0:128])
                tc = P.op("dve", cp, deps=[tt] + om_b.wdeps())
                pb.read(tc)
                cp_toks.append(tc)
            om_b.wrote(*cp_toks)
            tso = P.dma("sp", out_d.rearrange("(a p) d -> p a d", p=128)[:, tau, :],
                        OUTM[:, sl, :], deps=cp_toks, stream=f"out{sl}")
            om_b.read(tso)
            out_toks.append(tso)

    P.finalize(out_toks + tap_toks)
    P.materialize()
    fin_st.close()
    P.es.close()
    return nc


def kernel(**inputs):
    T = 2048
    x = np.ascontiguousarray(np.asarray(inputs["L0_attn_x"], dtype=np.float32))
    consts = make_consts(T)
    nc = build_nc(T)
    common = {k: np.ascontiguousarray(np.asarray(v)) for k, v in inputs.items()
              if k != "L0_attn_x"}
    common.update(consts)
    in_maps = []
    for b in range(8):
        m = dict(common)
        m["x"] = np.ascontiguousarray(x[b])
        in_maps.append(m)
    res = run_bass_kernel_spmd(nc, in_maps, core_ids=list(range(8)))
    return np.stack([res.results[b]["out"] for b in range(8)], axis=0)


# version 46 (final)
# speedup vs baseline: 1.0919x; 1.0119x over previous
import numpy as np
from contextlib import ExitStack

import concourse.bass as bass
import concourse.mybir as mybir
from concourse.bass_utils import run_bass_kernel_spmd

F32 = mybir.dt.float32
BF16 = mybir.dt.bfloat16
AF = mybir.ActivationFunctionType
ALU = mybir.AluOpType
AX = mybir.AxisListType

D = 1024
NC = 8
HEADS_A = 16
PAIRS_A = 8
HEADS_F = 8
PAIRS_F = 4
INNER = 512
HID = 2730
NEG = -1.0e37
LN_EPS = 1e-5


def make_consts(T):
    import ml_dtypes
    c = {}
    c["identf"] = np.eye(128, dtype=np.float32)
    c["identb"] = np.eye(128, dtype=ml_dtypes.bfloat16)
    c["onesrow"] = np.ones((1, 512), dtype=np.float32)
    c["onesrow_b"] = np.ones((1, 512), dtype=ml_dtypes.bfloat16)
    c["onescol"] = np.ones((128, 1), dtype=ml_dtypes.bfloat16)
    c["onescol_f"] = np.ones((128, 1), dtype=np.float32)
    c["epscol"] = np.full((1, 1), 1e-5, dtype=np.float32)
    ii = np.arange(128)
    c["m2_hi"] = np.where(ii[:, None] > ii[None, :], NEG, 0.0).astype(np.float32)
    c["m2_lo"] = np.where(ii[:, None] < ii[None, :], NEG, 0.0).astype(np.float32)
    half = np.arange(0, 64, 2, dtype=np.float64)
    inv_freq = 1.0 / 10000.0 ** (half / 64.0)
    t = np.arange(T, dtype=np.float64)
    freqs = t[:, None] * inv_freq[None, :]
    freqs = np.concatenate([freqs, freqs], axis=-1)
    base = (half + 0.4 * 64) / (1.4 * 64)
    power = (t - T // 2) / 256.0
    scale = base[None, :] ** power[:, None]
    scale = np.concatenate([scale, scale], axis=-1)
    cos, sin = np.cos(freqs), np.sin(freqs)
    sign = np.concatenate([-np.ones(32), np.ones(32)])[None, :]

    def tm(a):
        return np.ascontiguousarray(
            a.reshape(T // 128, 128, 64).transpose(1, 0, 2)).astype(np.float64)
    c["CQ"] = tm(cos * scale)
    c["SQ"] = tm(sign * sin * scale)
    c["CK"] = tm(cos / scale)
    c["SK"] = tm(sign * sin / scale)
    return c


ENGINES = ("pe", "act", "dve", "sp", "gp")


class Buf:
    __slots__ = ("w", "r")

    def __init__(self):
        self.w = []
        self.r = []

    def wdeps(self):
        return self.w + self.r

    def rdeps(self):
        return list(self.w)

    def wrote(self, *toks):
        self.w = list(toks)
        self.r = []

    def read(self, *toks):
        self.r.extend(toks)


class PB:
    def __init__(self, nc):
        self.nc = nc
        self.es = ExitStack()
        self.sem = {e: self.es.enter_context(nc.semaphore(f"s_{e}")) for e in ENGINES}
        self.streams = {}
        self.stream_semids = set()
        self.cnt = {}
        self.semobj = {}
        for s in self.sem.values():
            self.cnt[id(s)] = 0
            self.semobj[id(s)] = s
        self.wm = {e: {} for e in ENGINES}
        self.prog = {e: [] for e in ENGINES}
        import os as _os
        self.graph = [] if _os.environ.get("PB_GRAPH") else None
        self.markers = []
        self.tok2node = {}
        self.last_on = {}
        import collections as _c
        self.last_on = _c.defaultdict(lambda: None)

    def sb(self, name, shape, dtype, stack=None):
        return (stack or self.es).enter_context(self.nc.sbuf_tensor(name, shape, dtype))

    def ps_alloc(self, name, shape, dtype, stack=None):
        return (stack or self.es).enter_context(self.nc.psum_tensor(name, shape, dtype))

    def _waits(self, eng, deps):
        need = {}
        for tok in deps:
            if tok is None:
                continue
            s, v = tok
            k = id(s)
            if k in self.stream_semids:
                v = self.cnt[k]
            if need.get(k, 0) < v:
                need[k] = v
        waits = []
        wm = self.wm[eng]
        for k, v in need.items():
            if wm.get(k, 0) < v:
                waits.append((self.semobj[k], v))
                wm[k] = v
        return waits

    def _record(self, eng, deps, cost):
        if self.graph is None:
            return
        nid = len(self.graph)
        dd = []
        for t in deps:
            if t is None:
                continue
            key = (id(t[0]), t[1])
            j = self.tok2node.get(key)
            if j is not None:
                dd.append(j)
        if self.last_on[eng] is not None:
            dd.append(self.last_on[eng])
        self.graph.append((eng, dd, cost))
        self.last_on[eng] = nid
        return nid

    def _toklog(self, tok, nid):
        if self.graph is not None and nid is not None:
            self.tok2node[(id(tok[0]), tok[1])] = nid

    def op(self, eng, fn, deps=(), cost=500):
        waits = self._waits(eng, deps)
        s = self.sem[eng]
        self.cnt[id(s)] += 1
        v = self.cnt[id(s)]
        self.prog[eng].append((waits, fn, s, 1))
        nid = self._record(eng, deps, cost)
        self._toklog((s, v), nid)
        return (s, v)

    def ops(self, eng, fns, deps=(), cost=None):
        waits = self._waits(eng, deps)
        s = self.sem[eng]
        self.cnt[id(s)] += 1
        v = self.cnt[id(s)]

        def run(engobj, fns=tuple(fns)):
            ins = None
            for f in fns:
                ins = f(engobj)
            return ins
        self.prog[eng].append((waits, run, s, 1))
        nid = self._record(eng, deps, cost if cost is not None else 300 * len(fns))
        self._toklog((s, v), nid)
        return (s, v)

    def stream_sem(self, q, stream):
        key = ("gp" if q == "gp" else "hw", stream)
        if key not in self.streams:
            sem = self.es.enter_context(self.nc.semaphore(f"str_{key[0]}_{stream}"))
            self.streams[key] = sem
            self.cnt[id(sem)] = 0
            self.semobj[id(sem)] = sem
            self.stream_semids.add(id(sem))
        return self.streams[key]

    def dma(self, q, out, in_, deps=(), stream="misc", **kw):
        waits = self._waits(q, deps)
        sem = self.stream_sem(q, stream)
        self.cnt[id(sem)] += 16
        v = self.cnt[id(sem)]

        def run(engobj, out=out, in_=in_, kw=dict(kw)):
            return engobj.dma_start(out=out, in_=in_, **kw)
        self.prog[q].append((waits, run, sem, 16))
        try:
            nb = out.nbytes()
        except Exception:
            nb = 0
        nid = self._record("dma_" + q + "_" + stream, deps, 2000 + nb // 300)
        self._toklog((sem, v), nid)
        return (sem, v)

    def wait_only(self, eng, deps):
        waits = self._waits(eng, deps)
        if waits:
            self.prog[eng].append((waits, None, None, 0))

    def marker(self, name):
        if self.graph is not None:
            self.markers.append((name, len(self.graph)))

    def now(self, eng):
        s = self.sem[eng]
        return (s, self.cnt[id(s)])

    def barrier(self, except_streams=()):
        toks = [self.now(e) for e in ENGINES]
        toks += [(s, self.cnt[id(s)]) for k, s in self.streams.items()
                 if k[1] not in except_streams]
        for e in ENGINES:
            self.wait_only(e, toks)

    def finalize(self, final_dma_toks):
        self.wait_only("sp", final_dma_toks)

    def materialize(self):
        nc = self.nc
        with nc.Block() as block:
            def mk(ename):
                def run(engobj):
                    for waits, fn, s, inc in self.prog[ename]:
                        for (sem, v) in waits:
                            engobj.wait_ge(sem, v)
                        if fn is not None:
                            ins = fn(engobj)
                            if inc and ins is not None:
                                ins.then_inc(s, inc)
                return run
            block.tensor(mk("pe"))
            block.scalar(mk("act"))
            block.vector(mk("dve"))
            block.sync(mk("sp"))
            block.gpsimd(mk("gp"))


def build_nc(T, debug_taps=(), upto="all"):
    taps = set(debug_taps)
    nc = bass.Bass("TRN2", target_bir_lowering=False)
    TT = T // 128
    TG = (T + 511) // 512

    def gw(g):
        return min(512, T - g * 512)

    dram = {}

    def din(name, shape, dtype=F32):
        dram[name] = nc.dram_tensor(name, list(shape), dtype, kind="ExternalInput").ap()
        return dram[name]

    x_in = din("x", (T, D))
    W1d = din("L0_attn_W1", (D, 4 * D)); b1d = din("L0_attn_b1", (4 * D,))
    W2d = din("L0_attn_W2", (D, D)); b2d = din("L0_attn_b2", (D,))
    gammad = din("L0_attn_gamma", (D,)); betad = din("L0_attn_beta", (D,))
    wqkvd = din("L0_ffn_wqkv", (2, D, 1536))
    qscaled = din("L0_ffn_q_scale", (2, 64)); kscaled = din("L0_ffn_k_scale", (2, 64))
    wod = din("L0_ffn_wo", (2, INNER, D))
    lngd = din("L0_ffn_ln_g", (2, D)); lnbd = din("L0_ffn_ln_b", (2, D))
    w1fd = din("L0_ffn_w1", (2, D, 2 * HID)); b1fd = din("L0_ffn_b1", (2, 2 * HID))
    w2fd = din("L0_ffn_w2", (2, HID, D)); b2fd = din("L0_ffn_b2", (2, D))
    identf_d = din("identf", (128, 128)); identb_d = din("identb", (128, 128), BF16)
    onesrow_d = din("onesrow", (1, 512)); onesrowb_d = din("onesrow_b", (1, 512), BF16)
    onescol_d = din("onescol", (128, 1), BF16)
    onescolf_d = din("onescol_f", (128, 1))
    epscol_d = din("epscol", (1, 1))
    m2hi_d = din("m2_hi", (128, 128)); m2lo_d = din("m2_lo", (128, 128))
    CQd = din("CQ", (128, TT, 64)); SQd = din("SQ", (128, TT, 64))
    CKd = din("CK", (128, TT, 64)); SKd = din("SK", (128, TT, 64))

    out_d = nc.dram_tensor("out", [T, D], F32, kind="ExternalOutput").ap()
    xdram = nc.dram_tensor("xdram", [128, NC, T], F32, kind="Internal").ap()
    x2dram = nc.dram_tensor("x2dram", [128, NC, T], F32, kind="Internal").ap()

    P = PB(nc)
    global _LAST_PB
    _LAST_PB = P

    idf = P.sb("idf", [128, 128], F32)
    idb = P.sb("idb", [128, 128], BF16)
    ones = P.sb("ones", [1, 512], F32)
    onesb = P.sb("onesb", [1, 512], BF16)
    onescol = P.sb("onescol_s", [128, 1], BF16)
    onescolf = P.sb("onescolf_s", [128, 1], F32)
    epscol = P.sb("epscol_s", [1, 1], F32)
    MK = P.sb("MK", [128, 2, 128], F32)
    ROWB = P.sb("ROWB", [1, 10240], BF16)
    ROWX = P.sb("ROWX", [1, 7168], F32)
    ROWF = P.sb("ROWF", [1, 3072], F32)
    COLF = P.sb("COLF", [128, 128], F32)
    COLB = P.sb("COLB", [128, 32], BF16)
    PS = P.ps_alloc("PS", [128, 7, 512], F32)
    PSWG = PS.rearrange("p b n -> p (b n)")
    PSB = P.ps_alloc("PSB", [128, 1024], BF16)

    bufs = {}

    def B(name):
        if name not in bufs:
            bufs[name] = Buf()
        return bufs[name]

    def psbank(i):
        return B(f"psbank{i}")

    RF = {"A": 0, "Bv": 512, "mu2": 1024, "bias": 1536,
          "A2": 2048, "Bv2": 2560}

    def rf(slot, w=512):
        off = RF[slot]
        return ROWF[0:1, off:off + w], B("rf_" + slot)

    RB = {}
    rb_ptr = [0]
    RX = {}
    rx_ptr = [0]

    def rb_reset():
        RB.clear()
        rb_ptr[0] = 0
        RX.clear()
        rx_ptr[0] = 0

    def rb(name, width=None):
        if name not in RB:
            assert width is not None, name
            off = rb_ptr[0]
            assert off + width <= 10240, (name, off, width)
            RB[name] = (off, width)
            rb_ptr[0] = off + width
        off, wd = RB[name]
        return ROWB[0:1, off:off + wd], B("rb_" + name)

    def rx(name, width=None):
        if name not in RX:
            assert width is not None, name
            off = rx_ptr[0]
            assert off + width <= 7168, (name, off, width)
            RX[name] = (off, width)
            rx_ptr[0] = off + width
        off, wd = RX[name]
        return ROWX[0:1, off:off + wd], B("rx_" + name)

    tap_toks = []

    def tap(name, ap_sbuf, deps, dtype=F32):
        if name not in taps:
            return
        t = nc.dram_tensor("tap_" + name, list(ap_sbuf.shape), dtype,
                           kind="ExternalOutput").ap()
        tk = P.dma("gp", t, ap_sbuf, deps=deps, stream="tap")
        tap_toks.append(tk)

    def tap_dram(name, dram_ap):
        if name not in taps:
            return None
        t = nc.dram_tensor("tap_" + name, list(dram_ap.shape), F32,
                           kind="ExternalOutput").ap()
        return (name, t, dram_ap)

    P.marker("x0-setup")
    t_idf = P.dma("sp", idf[:, :], identf_d, stream="const")
    t_idb = P.dma("sp", idb[:, :], identb_d, stream="const")
    t_ones = P.dma("sp", ones[:, :], onesrow_d, stream="const")
    t_onesb = P.dma("sp", onesb[:, :], onesrowb_d, stream="const")
    t_onescol = P.dma("sp", onescol[:, :], onescol_d, stream="const")
    t_onescolf = P.dma("sp", onescolf[:, :], onescolf_d, stream="const")
    t_eps = P.dma("sp", epscol[:, :], epscol_d, stream="const")
    t_mk = [P.dma("sp", MK[:, i, :], m, stream="const") for i, m in
            enumerate([m2hi_d, m2lo_d])]
    t_consts = [t_idf, t_idb, t_ones, t_onesb, t_onescol, t_eps] + t_mk

    def row_to_cols(rowap, rowbuf, coltensor, colslot, n, colbufname, bank=5):
        nch = (n + 127) // 128
        cb = B(colbufname)
        for i in range(nch):
            w = min(128, n - 128 * i)
            pb = psbank(bank)

            def tr(e, i=i, w=w, rowap=rowap, bank=bank):
                return e.transpose(PS[0:w, bank, 0:1], rowap[0:1, 128 * i:128 * i + w],
                                   idf[0:1, 0:1])
            tk = P.op("pe", tr, deps=rowbuf.rdeps() + [t_idf] + pb.wdeps())
            pb.wrote(tk)
            rowbuf.read(tk)

            def cp(e, i=i, w=w, colslot=colslot, bank=bank, coltensor=coltensor):
                return e.tensor_copy(coltensor[0:w, colslot + i:colslot + i + 1],
                                     PS[0:w, bank, 0:1])
            tc = P.op("dve", cp, deps=[tk] + cb.wdeps())
            pb.read(tc)
            cb.w.append(tc)
        return cb

    def vec_to_cols(vecap, n, colslot, colbufname, voff=0):
        cb = B(colbufname)
        bias_ap, bias_b = rf("bias")
        for s0 in range(0, n, 512):
            w = min(512, n - s0)
            tk = P.dma("sp", ROWF[0:1, RF["bias"]:RF["bias"] + w],
                       vecap.rearrange("(one n) -> one n", one=1)[0:1, voff + s0:voff + s0 + w],
                       deps=bias_b.wdeps(), stream="bias")
            bias_b.wrote(tk)
            nslice = (w + 127) // 128
            for i in range(nslice):
                wi = min(128, w - 128 * i)
                pb = psbank(5)

                def tr(e, wi=wi, i=i):
                    return e.transpose(PS[0:wi, 5, 0:1],
                                       ROWF[0:1, RF["bias"] + 128 * i:RF["bias"] + 128 * i + wi],
                                       idf[0:1, 0:1])
                tk2 = P.op("pe", tr, deps=[tk, t_idf] + pb.wdeps())
                pb.wrote(tk2)
                bias_b.read(tk2)
                ci = colslot + (s0 // 128) + i

                def cp(e, wi=wi, ci=ci):
                    return e.tensor_copy(COLF[0:wi, ci:ci + 1], PS[0:wi, 5, 0:1])
                tc = P.op("dve", cp, deps=[tk2] + cb.wdeps())
                pb.read(tc)
                cb.w.append(tc)
        return cb

    def vec_to_row(vecap, n, rbname, voff=0, stream="bias2", f32=False):
        rrow, r_b = (rx(rbname) if f32 else rb(rbname))
        bias_ap, bias_b = rf("bias")
        for s0 in range(0, n, 512):
            w = min(512, n - s0)
            tk = P.dma("sp", ROWF[0:1, RF["bias"]:RF["bias"] + w],
                       vecap.rearrange("(one n) -> one n", one=1)[0:1, voff + s0:voff + s0 + w],
                       deps=bias_b.wdeps(), stream=stream)
            bias_b.wrote(tk)

            def cp(e, s0=s0, w=w, rrow=rrow):
                return e.activation(rrow[0:1, s0:s0 + w],
                                    ROWF[0:1, RF["bias"]:RF["bias"] + w], AF.Copy)
            tc = P.op("act", cp, deps=[tk] + r_b.wdeps())
            bias_b.read(tc)
            r_b.w.append(tc)
        return rrow, r_b

    def ln_stats_rows(w, arow, a_b, brow, b_b, rsrow, rs_b, rs_off, nmurs, nmurs_b,
                      nmursb=None, nmursb_b=None, par=0):
        if par == 0:
            mu2row, mu2_b = rf("mu2", w)
        else:
            mu2row, mu2_b = PS[0:1, 3, 0:w], psbank(3)

        def f1(e):
            return e.tensor_scalar_mul(arow, arow, 1.0 / D)
        t1 = P.op("dve", f1, deps=a_b.wdeps())
        a_b.wrote(t1)

        if par == 0:
            def f2(e):
                return e.activation(mu2row, arow, AF.Square)
            t2 = P.op("act", f2, deps=[t1] + mu2_b.wdeps())
        else:
            def f2(e):
                return e.tensor_tensor(mu2row, arow, arow, op=ALU.mult)
            t2 = P.op("dve", f2, deps=[t1] + mu2_b.wdeps())
        a_b.read(t2); mu2_b.wrote(t2)

        def f3(e):
            return e.scalar_tensor_tensor(brow, brow, 1.0 / D, mu2row,
                                          op0=ALU.mult, op1=ALU.subtract)
        t3 = P.op("dve", f3, deps=b_b.wdeps() + [t2])
        mu2_b.read(t3); b_b.wrote(t3)

        def f4(e):
            return e.activation(brow, brow, AF.Ln, bias=epscol[0:1, 0:1])
        t4 = P.op("act", f4, deps=[t3, t_eps])
        b_b.wrote(t4)

        def f5(e):
            return e.activation(rsrow[0:1, rs_off:rs_off + w], brow, AF.Exp, scale=-0.5)
        t5 = P.op("act", f5, deps=[t4] + rs_b.wdeps())
        b_b.read(t5); rs_b.w.append(t5)

        def f6(e):
            return e.scalar_tensor_tensor(nmurs[0:1, rs_off:rs_off + w], arow, -1.0,
                                          rsrow[0:1, rs_off:rs_off + w],
                                          op0=ALU.mult, op1=ALU.mult)
        t6 = P.op("dve", f6, deps=[t1, t5] + nmurs_b.wdeps())
        a_b.read(t6); rs_b.read(t6); nmurs_b.w.append(t6)
        if nmursb is not None:
            def f7(e):
                return e.activation(nmursb[0:1, rs_off:rs_off + w],
                                    nmurs[0:1, rs_off:rs_off + w], AF.Copy)
            t7 = P.op("act", f7, deps=[t6] + nmursb_b.wdeps())
            nmurs_b.read(t7)
            nmursb_b.w.append(t7)

    def ln_stats_bf16(src_fn, w, rsname, nmursname, rs_off, sq, sqbufname,
                      src_f32=False, make_bf=True, par=0):
        rsrow, rs_b = rx(rsname)
        nmurs, nmurs_b = rx(nmursname)
        if make_bf:
            nmursb, nmursb_b = rb(nmursname + "_b")
        else:
            nmursb, nmursb_b = None, None
        sqb = B(sqbufname)
        sq_toks = []
        srcs = []
        for c in range(NC):
            ap, deps, bo = src_fn(c)
            srcs.append((ap, deps, bo))

            def f(e, ap=ap, c=c, w=w):
                return e.activation(sq[:, c, 0:w], ap, AF.Square)
            tk = P.op("act", f, deps=deps + sqb.wdeps())
            sq_toks.append(tk)
            if bo is not None:
                bo.read(tk)
        sqb.wrote(*sq_toks)
        arow, a_b = rf("A" if par == 0 else "A2", w)
        brow, b_b = rf("Bv" if par == 0 else "Bv2", w)
        bpair = (5, 6) if par == 0 else (3, 4)
        for (bank, which, drow, dbuf) in ((bpair[0], "src", arow, a_b),
                                          (bpair[1], "sq", brow, b_b)):
            pb = psbank(bank)
            fns = []
            for c in range(NC):
                apx = srcs[c][0] if which == "src" else sq[:, c, 0:w]
                lhs = onescolf[:, :] if (which == "src" and src_f32) else onescol[:, :]

                def mm(e, apx=apx, c=c, bank=bank, w=w, lhs=lhs):
                    return e.matmul(PS[0:1, bank, 0:w], lhs, apx,
                                    start=(c == 0), stop=(c == NC - 1))
                fns.append(mm)
            dps = [t_onescol, t_onescolf] + pb.wdeps()
            if which == "src":
                for (ap, deps, bo) in srcs:
                    dps += deps
            else:
                dps += sq_toks
            tk = P.ops("pe", fns, deps=dps)
            pb.wrote(tk)
            if which == "src":
                for (ap, deps, bo) in srcs:
                    if bo is not None:
                        bo.read(tk)
            else:
                sqb.read(tk)

            def cpr(e, drow=drow, w=w, bank=bank):
                return e.activation(drow, PS[0:1, bank, 0:w], AF.Copy)
            tc = P.op("act", cpr, deps=[tk] + dbuf.wdeps())
            pb.read(tc)
            dbuf.wrote(tc)
        ln_stats_rows(w, arow, a_b, brow, b_b, rsrow, rs_b, rs_off, nmurs, nmurs_b,
                      nmursb, nmursb_b, par=par)
        return rsrow, rs_b, nmurs, nmurs_b, nmursb, nmursb_b

    def bcast_row(rowap, rowbuf, off, w, bank):
        pb = psbank(bank)

        def mm(e, off=off, w=w, bank=bank, rowap=rowap):
            return e.matmul(PS[:, bank, 0:w], ones[0:1, 0:128], rowap[0:1, off:off + w],
                            start=True, stop=True)
        tk = P.op("pe", mm, deps=rowbuf.rdeps() + [t_ones] + pb.wdeps())
        pb.wrote(tk)
        rowbuf.read(tk)
        return tk

    def colsum(wt, wtoks, rbname, n, gcol=None, gdeps=(), bank=5):
        rrow, r_b = rb(rbname)
        for s0 in range(0, n, 512):
            w = min(512, n - s0)
            pb = psbank(bank)
            fns = []
            for c in range(NC):
                lhs = onescol[:, :] if gcol is None else gcol[:, c:c + 1]

                def mm(e, c=c, s0=s0, w=w, bank=bank, wt=wt, lhs=lhs):
                    return e.matmul(PS[0:1, bank, 0:w], lhs, wt[:, c, s0:s0 + w],
                                    start=(c == 0), stop=(c == NC - 1))
                fns.append(mm)
            tk = P.ops("pe", fns, deps=list(wtoks) + list(gdeps) + [t_onescol] + pb.wdeps())
            pb.wrote(tk)

            def cpr(e, s0=s0, w=w, bank=bank, rrow=rrow):
                return e.activation(rrow[0:1, s0:s0 + w], PS[0:1, bank, 0:w], AF.Copy)
            tc = P.op("act", cpr, deps=[tk] + r_b.wdeps())
            pb.read(tc)
            r_b.w.append(tc)
        return rrow, r_b

    def XD(g):
        return B(f"XD_g{g}")

    def X2DB(g):
        return B(f"X2D_g{g}")

    def load_grp(dst, dstbuf, dram_t, g, dbuf_g, stream):
        w = gw(g)
        tk = P.dma("sp", dst[:, :, 0:w], dram_t[:, :, 512 * g:512 * g + w],
                   deps=dstbuf.wdeps() + dbuf_g.rdeps(), stream=stream)
        dstbuf.wrote(tk)
        dbuf_g.read(tk)
        return tk

    def store_grp(src, srcbuf, dram_t, g, dbuf_g, stream):
        w = gw(g)
        tk = P.dma("sp", dram_t[:, :, 512 * g:512 * g + w], src[:, :, 0:w],
                   deps=srcbuf.rdeps() + dbuf_g.wdeps(), stream=stream)
        srcbuf.read(tk)
        dbuf_g.wrote(tk)
        return tk

    def fin_dram_taps(dtaps):
        fin_st = ExitStack()
        TMPT = P.sb("TMPTAP2", [128, NC, 512], F32, stack=fin_st)
        tb_ = B("TMPTAP2")
        for item in dtaps:
            if item is None:
                continue
            name, t, src = item
            for g in range(TG):
                w = gw(g)
                tk = P.dma("sp", TMPT[:, :, 0:w], src[:, :, 512 * g:512 * g + w],
                           deps=tb_.wdeps() + XD(g).wdeps() + X2DB(g).wdeps(),
                           stream="tapl")
                tb_.wrote(tk)
                tk2 = P.dma("gp", t[:, :, 512 * g:512 * g + w], TMPT[:, :, 0:w],
                            deps=[tk], stream="tap")
                tb_.read(tk2)
                tap_toks.append(tk2)
        P.barrier()
        P.finalize(tap_toks)
        P.materialize()
        fin_st.close()
        P.es.close()

    b1col_b = vec_to_cols(b1d, 4096, 0, "b1col")
    gcol_b = vec_to_cols(gammad, 1024, 32, "gcol")
    bcolf_b = vec_to_cols(betad, 1024, 40, "bcolf")
    betab_b = B("betacolb")
    bt = []
    for i in range(8):
        def cp(e, i=i):
            return e.tensor_copy(COLB[:, i:i + 1], COLF[:, 40 + i:40 + i + 1])
        tk = P.op("dve", cp, deps=bcolf_b.rdeps() + betab_b.wdeps())
        bcolf_b.read(tk)
        bt.append(tk)
    betab_b.wrote(*bt)
    rx("b1v", 1024)
    b1vrow, b1v_b = vec_to_row(b1d, 1024, "b1v", voff=1024, f32=True)

    rx("rs0", T)
    rx("nmurs0", T)
    rb("nmurs0_b", T)
    rs0, rs0_b = rx("rs0")
    nmurs0, nmurs0_b = rx("nmurs0")
    nmurs0bf, nmurs0bf_b = rb("nmurs0_b")

    x0_st = ExitStack()
    XTM = P.sb("XTM", [128, 4, D], F32, stack=x0_st)
    XG = P.sb("XG", [128, 2, NC, 512], F32, stack=x0_st)
    SQS = P.sb("SQS", [128, 2, NC, 512], BF16, stack=x0_st)

    for g in range(TG):
        P.marker(f"x0-g{g}")
        w = gw(g)
        nt = w // 128
        gp = g % 2
        xg_b = B(f"XG{gp}")
        for j in range(nt):
            tile = 4 * g + j
            sl = tile % 4
            xtm_b = B(f"XTM{sl}")
            t_x = P.dma("sp" if sl % 2 == 0 else "act", XTM[:, sl, :],
                        x_in.rearrange("(a p) d -> p a d", p=128)[:, tile, :],
                        deps=xtm_b.wdeps(), stream=f"x{sl}")
            xtm_b.wrote(t_x)
            for c in range(NC):
                bank = c % 4
                pb = psbank(bank)

                def tr(e, sl=sl, c=c, bank=bank):
                    return e.transpose(PS[:, bank, 0:128], XTM[:, sl, 128 * c:128 * c + 128],
                                       idf[:, :])
                tok_tr = P.op("pe", tr, deps=[t_x, t_idf] + pb.wdeps())
                pb.wrote(tok_tr)
                xtm_b.read(tok_tr)

                def cpf(e, c=c, j=j, bank=bank, gp=gp):
                    return e.tensor_copy(XG[:, gp, c, 128 * j:128 * j + 128],
                                         PS[:, bank, 0:128])
                tok_cp = P.op("dve", cpf, deps=[tok_tr] + xg_b.wdeps())
                pb.read(tok_cp)
                xg_b.w.append(tok_cp)
        ln_stats_bf16(lambda c: (XG[:, gp, c, 0:w], xg_b.rdeps(), xg_b), w,
                      "rs0", "nmurs0", 512 * g, SQS[:, gp], f"SQS{gp}",
                      src_f32=True, par=gp)
        store_grp(XG[:, gp], xg_b, xdram, g, XD(g), "xs")
        xg_b.wrote()
        xg_b.read(*XD(g).w)

    tap("rs0", rs0, rs0_b.w)
    tap("nmurs0", nmurs0, nmurs0_b.w)
    P.barrier()
    x0_st.close()

    if upto == "p1":
        P.barrier()
        P.finalize(tap_toks)
        P.materialize()
        P.es.close()
        return nc

    sa_st = ExitStack()
    KF = P.sb("KF", [128, PAIRS_A, T], BF16, stack=sa_st)
    VT = P.sb("VT", [128, TT, D], BF16, stack=sa_st)
    W2S = P.sb("W2S", [128, NC, 1024], BF16, stack=sa_st)
    a1_st = ExitStack()
    W1K = P.sb("W1K", [128, NC, 1024], BF16, stack=a1_st)
    W1V = P.sb("W1V", [128, NC, 1024], BF16, stack=a1_st)
    XGa = P.sb("XGa", [128, NC, 512], F32, stack=a1_st)
    XNG = P.sb("XNG", [128, NC, 512], BF16, stack=a1_st)

    t_w1k = P.dma("gp", W1K[:, :, :],
                  W1d[:, 3072:4096].rearrange("(kc p) n -> p kc n", p=128), stream="w1k")
    t_w1v = P.dma("gp", W1V[:, :, :],
                  W1d[:, 1024:2048].rearrange("(kc p) n -> p kc n", p=128), stream="w1v")
    rb("csk", 1024)
    rb("csv", 1024)
    cskrow, csk_b = colsum(W1K, [t_w1k], "csk", 1024)
    csvrow, csv_b = colsum(W1V, [t_w1v], "csv", 1024)

    def make_xng(g, rs_row, rs_buf, rs_off, dst, dstname, src, srcbuf, src_toks):
        w = gw(g)
        tb_ = bcast_row(rs_row, rs_buf, rs_off, w, 6)
        toks = []
        xng_b = B(dstname)
        for c in range(NC):
            def f(e, c=c, w=w, dst=dst, src=src):
                return e.tensor_tensor(dst[:, c, 0:w], src[:, c, 0:w],
                                       PS[:, 6, 0:w], op=ALU.mult)
            tk = P.op("dve", f, deps=[tb_] + list(src_toks) + xng_b.wdeps())
            toks.append(tk)
            srcbuf.read(tk)
            psbank(6).read(tk)
        xng_b.wrote(*toks)
        return toks

    for g in range(TG):
        P.marker(f"a1-g{g}")
        w = gw(g)
        xga_b = B("XGa")
        t_load = load_grp(XGa, xga_b, xdram, g, XD(g), "xl")
        xng_toks = make_xng(g, rs0, rs0_b, 512 * g, XNG, "XNG", XGa, xga_b, [t_load])
        xng_b = B("XNG")
        for i in range(PAIRS_A):
            bank = i % 2
            pb = psbank(bank)
            fns = []
            for c in range(NC):
                def mm(e, c=c, i=i, w=w, bank=bank):
                    return e.matmul(PS[:, bank, 0:w], W1K[:, c, 128 * i:128 * i + 128],
                                    XNG[:, c, 0:w], start=(c == 0), stop=False)
                fns.append(mm)

            def mmr(e, i=i, w=w, g=g, bank=bank):
                return e.matmul(PS[:, bank, 0:w], cskrow[0:1, 128 * i:128 * i + 128],
                                nmurs0bf[0:1, 512 * g:512 * g + w], start=False, stop=True)
            fns.append(mmr)
            tk = P.ops("pe", fns, deps=xng_toks + [t_w1k] + csk_b.rdeps()
                       + nmurs0bf_b.rdeps() + pb.wdeps())
            pb.wrote(tk)
            xng_b.read(tk)
            csk_b.read(tk)
            nmurs0bf_b.read(tk)
            kf_b = B(f"KF{i}")

            def ev(e, i=i, w=w, g=g, bank=bank):
                return e.activation(KF[:, i, 512 * g:512 * g + w], PS[:, bank, 0:w],
                                    AF.Silu, bias=COLF[:, 24 + i:24 + i + 1])
            te = P.op("act", ev, deps=[tk] + B("b1col").rdeps() + kf_b.wdeps())
            pb.read(te)
            B("b1col").read(te)
            kf_b.w.append(te)
        for j in range(w // 128):
            tau = 4 * g + j
            for ng in range(2):
                bank = 2 + ng
                pb = psbank(bank)
                fns = []
                for c in range(NC):
                    def mm(e, c=c, j=j, ng=ng, bank=bank):
                        return e.matmul(PS[:, bank, :], XNG[:, c, 128 * j:128 * j + 128],
                                        W1V[:, c, 512 * ng:512 * ng + 512],
                                        start=(c == 0), stop=False)
                    fns.append(mm)

                def mmr(e, tau=tau, ng=ng, bank=bank):
                    return e.matmul(PS[:, bank, :], nmurs0bf[0:1, 128 * tau:128 * tau + 128],
                                    csvrow[0:1, 512 * ng:512 * ng + 512],
                                    start=False, stop=False)

                def mmb(e, ng=ng, bank=bank):
                    return e.matmul(PS[:, bank, :], ones[0:1, 0:128],
                                    b1vrow[0:1, 512 * ng:512 * ng + 512],
                                    start=False, stop=True)
                fns.extend([mmr, mmb])
                tk = P.ops("pe", fns, deps=xng_toks + [t_w1v, t_ones] + csv_b.rdeps()
                           + nmurs0bf_b.rdeps() + b1v_b.rdeps() + pb.wdeps())
                pb.wrote(tk)
                xng_b.read(tk)
                csv_b.read(tk)
                nmurs0bf_b.read(tk)
                b1v_b.read(tk)
                vt_b = B(f"VT{g}")

                def ev(e, tau=tau, ng=ng, bank=bank):
                    return e.activation(VT[:, tau, 512 * ng:512 * ng + 512], PS[:, bank, :],
                                        AF.Silu)
                te = P.op("act", ev, deps=[tk] + vt_b.wdeps())
                pb.read(te)
                vt_b.w.append(te)

    tap("KF", KF[:, :, :], [t for i in range(PAIRS_A) for t in B(f"KF{i}").w], dtype=BF16)
    tap("VT", VT[:, :, :], [t for g in range(TG) for t in B(f"VT{g}").w], dtype=BF16)

    if upto == "a1":
        P.barrier()
        P.finalize(tap_toks)
        P.materialize()
        a1_st.close()
        sa_st.close()
        P.es.close()
        return nc

    t_w2 = P.dma("gp", W2S[:, :, :], W2d.rearrange("(kc p) n -> p kc n", p=128), stream="w2")
    w2s_b = B("W2S")
    rb("b2eff", 1024)
    b2eff, b2eff_b = rb("b2eff")
    bias_ap, bias_b = rf("bias")
    beff_toks = []
    for s0 in (0, 512):
        t_b2 = P.dma("sp", ROWF[0:1, RF["bias"]:RF["bias"] + 512],
                     b2d.rearrange("(one n) -> one n", one=1)[0:1, s0:s0 + 512],
                     deps=bias_b.wdeps(), stream="bias2")
        bias_b.wrote(t_b2)
        pb = psbank(6)
        fns = []
        for c in range(NC):
            def mm(e, c=c, s0=s0):
                return e.matmul(PS[0:1, 6, 0:512], COLB[:, c:c + 1], W2S[:, c, s0:s0 + 512],
                                start=(c == 0), stop=(c == NC - 1))
            fns.append(mm)
        tk = P.ops("pe", fns, deps=[t_w2] + B("betacolb").rdeps() + pb.wdeps())
        pb.wrote(tk)
        B("betacolb").read(tk)
        w2s_b.read(tk)

        def addb(e, s0=s0):
            return e.scalar_tensor_tensor(b2eff[0:1, s0:s0 + 512], PS[0:1, 6, 0:512], 1.0,
                                          ROWF[0:1, RF["bias"]:RF["bias"] + 512],
                                          op0=ALU.mult, op1=ALU.add)
        tc = P.op("dve", addb, deps=[tk, t_b2] + b2eff_b.wdeps())
        pb.read(tc)
        bias_b.read(tc)
        b2eff_b.w.append(tc)
        beff_toks.append(tk)
    gtoks = []
    for c in range(NC):
        def gf(e, c=c):
            return e.tensor_scalar_mul(W2S[:, c, :], W2S[:, c, :], COLF[:, 32 + c:33 + c])
        tk = P.op("dve", gf, deps=[t_w2] + beff_toks + B("gcol").rdeps() + w2s_b.wdeps())
        B("gcol").read(tk)
        gtoks.append(tk)
    w2s_b.wrote(*gtoks)

    P.barrier()
    a1_st.close()

    a2_st = ExitStack()
    XGb = P.sb("XGb", [128, NC, 512], F32, stack=a2_st)
    GBUF = P.sb("GBUF", [128, NC, 512], BF16, stack=a2_st)
    WQU = P.sb("WQU", [128, NC, 3, 128], BF16, stack=a2_st)
    QFp = P.sb("QFp", [128, 2, 512], BF16, stack=a2_st)
    UFp = P.sb("UFp", [128, 2, 512], F32, stack=a2_st)
    YB = P.sb("YB", [128, NC, 512], F32, stack=a2_st)
    ZB = YB
    PT = P.sb("PT", [128, 2, 1024], BF16, stack=a2_st)
    TYF = P.sb("TYF", [128, 512], F32, stack=a2_st)
    rx("rsz", 512)
    rx("nmursz", 512)
    rx("rsy", 512)
    rx("nmursy", 512)
    rb("nmursy_b", 512)
    rb("csq", 1024)
    rb("csu", 1024)

    tap_xa = []
    for g in range(TG):
        P.marker(f"a2-g{g}")
        w = gw(g)
        xgb_b = B("XGb")
        t_load = load_grp(XGb, xgb_b, xdram, g, XD(g), "xl2")
        xng_toks = make_xng(g, rs0, rs0_b, 512 * g, GBUF, "GBUF", XGb, xgb_b, [t_load])
        xng2_b = B("GBUF")
        def qu_dma(i, pu):
            slot = (i % 2) if pu == 0 else 2
            wq_b = B(f"WQU{slot}")
            base = 2048 + 128 * i if pu == 0 else 128 * i
            tk = P.dma("gp", WQU[:, :, slot, :],
                       W1d[:, base:base + 128]
                       .rearrange("(kc p) n -> p kc n", p=128),
                       deps=wq_b.wdeps(), stream=f"w1q{slot}")
            wq_b.wrote(tk)
            return tk

        def qu_proj(i):
            rot = i % 2
            for (pu, slot, dst, dbufn, bcol) in (
                    (0, rot, QFp, "QFp", 16 + i), (1, 2, UFp, "UFp", i)):
                wq_b = B(f"WQU{slot}")
                t_w = wq_b.w[-1]
                csrow_, cs_b_ = rb("csq" if pu == 0 else "csu")
                if g == 0:
                    pbq = psbank(6)
                    fns = []
                    for c in range(NC):
                        def mm(e, c=c, slot=slot):
                            return e.matmul(PS[0:1, 6, 0:128], onescol[:, :],
                                            WQU[:, c, slot, :],
                                            start=(c == 0), stop=(c == NC - 1))
                        fns.append(mm)
                    tkc = P.ops("pe", fns, deps=[t_w, t_onescol] + pbq.wdeps())
                    pbq.wrote(tkc)
                    wq_b.read(tkc)

                    def cpr(e, i=i, csrow_=csrow_):
                        return e.activation(csrow_[0:1, 128 * i:128 * i + 128],
                                            PS[0:1, 6, 0:128], AF.Copy)
                    tcc = P.op("act", cpr, deps=[tkc] + cs_b_.wdeps())
                    pbq.read(tcc)
                    cs_b_.w.append(tcc)
                bank = 6
                pb = psbank(bank)
                fns = []
                for c in range(NC):
                    def mm(e, c=c, slot=slot, w=w, bank=bank):
                        return e.matmul(PS[:, bank, 0:w], WQU[:, c, slot, :],
                                        GBUF[:, c, 0:w], start=(c == 0), stop=False)
                    fns.append(mm)

                def mmr(e, i=i, w=w, g=g, bank=bank, csrow_=csrow_):
                    return e.matmul(PS[:, bank, 0:w], csrow_[0:1, 128 * i:128 * i + 128],
                                    nmurs0bf[0:1, 512 * g:512 * g + w],
                                    start=False, stop=True)
                fns.append(mmr)
                tk = P.ops("pe", fns, deps=[t_w] + xng_toks + cs_b_.rdeps()
                           + nmurs0bf_b.rdeps() + pb.wdeps() + xng2_b.wdeps())
                pb.wrote(tk)
                wq_b.read(tk)
                xng2_b.read(tk)
                cs_b_.read(tk)
                nmurs0bf_b.read(tk)
                db = B(dbufn + str(rot))

                def ev(e, w=w, bank=bank, dst=dst, bcol=bcol, rot=rot):
                    return e.activation(dst[:, rot, 0:w], PS[:, bank, 0:w],
                                        AF.Silu, bias=COLF[:, bcol:bcol + 1])
                te = P.op("act", ev, deps=[tk] + B("b1col").rdeps() + db.wdeps())
                pb.read(te)
                B("b1col").read(te)
                db.wrote(te)

        qu_dma(0, 0)
        qu_dma(0, 1)
        qu_dma(1, 0)
        qu_proj(0)
        for i in range(PAIRS_A):
            rot = i % 2
            if i + 1 < PAIRS_A:
                qu_dma(i + 1, 1)
                if i + 2 < PAIRS_A:
                    qu_dma(i + 2, 0)
                qu_proj(i + 1)
            avb = psbank(4)
            avb2 = psbank(5)
            av_deps = avb.wdeps()
            av_deps2 = avb2.wdeps()
            qf_b = B("QFp" + str(rot))
            pv_last = None
            pend = None
            for jc in range(TT + 1):
                if jc < TT:
                    bA = 2 if jc % 2 == 0 else 0
                    bB = 3 if jc % 2 == 0 else 1
                    sA = psbank(bA)
                    sB = psbank(bB)
                    kf_b = B(f"KF{i}")

                    def mmA(e, i=i, jc=jc, w=w, bA=bA, rot=rot):
                        return e.matmul(PS[:, bA, 0:w], KF[0:64, i, 128 * jc:128 * jc + 128],
                                        QFp[0:64, rot, 0:w], start=True, stop=True,
                                        tile_position=(0, 0))

                    def mmB(e, i=i, jc=jc, w=w, bB=bB, rot=rot):
                        return e.matmul(PS[:, bB, 0:w], KF[64:128, i, 128 * jc:128 * jc + 128],
                                        QFp[64:128, rot, 0:w], start=True, stop=True,
                                        tile_position=(64, 0))
                    tkAB = P.ops("pe", [mmA, mmB], deps=kf_b.rdeps() + qf_b.rdeps()
                                 + sA.wdeps() + sB.wdeps(), cost=600)
                    sA.wrote(tkAB)
                    sB.wrote(tkAB)
                    kf_b.read(tkAB)
                    qf_b.read(tkAB)
                    sl = jc % 2
                    ptb_m = B(f"PTm{sl}")
                    assert w == 512 and bB == bA + 1

                    def evAB(e, sl=sl, bA=bA):
                        return e.activation(PT[:, sl, :], PSWG[:, 512 * bA:512 * bA + 1024],
                                            AF.Silu)
                    teAB = P.op("act", evAB, deps=[tkAB] + ptb_m.wdeps(), cost=1150)
                    sA.read(teAB)
                    sB.read(teAB)
                    ptb_m.wrote(teAB)
                if pend is not None:
                    pjc, psl, pteAB = pend
                    vt_b = B(f"VT{pjc // 4}")
                    ptb_m = B(f"PTm{psl}")

                    def pvA(e, i=i, pjc=pjc, psl=psl, w=w):
                        return e.matmul(PS[0:64, 4, 0:w], VT[:, pjc, 128 * i:128 * i + 64],
                                        PT[:, psl, 0:512], start=(pjc == 0), stop=(pjc == TT - 1),
                                        tile_position=(0, 0))

                    def pvB(e, i=i, pjc=pjc, psl=psl, w=w):
                        return e.matmul(PS[64:128, 5, 0:w], VT[:, pjc, 128 * i + 64:128 * i + 128],
                                        PT[:, psl, 512:1024], start=(pjc == 0), stop=(pjc == TT - 1),
                                        tile_position=(0, 64))
                    tpAB = P.ops("pe", [pvA, pvB], deps=[pteAB] + vt_b.rdeps()
                                 + av_deps + av_deps2, cost=600)
                    vt_b.read(tpAB)
                    ptb_m.read(tpAB)
                    pv_last = (tpAB, tpAB)
                if jc < TT:
                    pend = (jc, sl, teAB)
            avb.wrote(pv_last[0])
            avb2.wrote(pv_last[1])
            zb_ = B(f"YB{i}")
            uf_b = B("UFp" + str(rot))

            def zfA(e, i=i, w=w, rot=rot):
                return e.scalar_tensor_tensor(ZB[0:64, i, 0:w], PS[0:64, 4, 0:w], 1.0 / T,
                                              UFp[0:64, rot, 0:w], op0=ALU.mult, op1=ALU.mult)
            tzA = P.op("dve", zfA, deps=list(pv_last) + uf_b.rdeps() + zb_.wdeps())

            def zfB(e, i=i, w=w, rot=rot):
                return e.scalar_tensor_tensor(ZB[64:128, i, 0:w], PS[64:128, 5, 0:w], 1.0 / T,
                                              UFp[64:128, rot, 0:w], op0=ALU.mult, op1=ALU.mult)
            tzB = P.op("dve", zfB, deps=list(pv_last) + uf_b.rdeps() + zb_.wdeps())
            avb.read(tzA)
            avb2.read(tzB)
            uf_b.read(tzA, tzB)
            zb_.wrote(tzA, tzB)
        rsz, rsz_b, nmursz, nmursz_b, _, _ = ln_stats_bf16(
            lambda c: (ZB[:, c, 0:w], B(f"YB{c}").rdeps(), B(f"YB{c}")), w,
            "rsz", "nmursz", 0, GBUF, "GBUF", src_f32=True, make_bf=False)
        tbz = bcast_row(rsz, rsz_b, 0, w, 6)
        tbzm = bcast_row(nmursz, nmursz_b, 0, w, 5)
        zn_toks = []
        gb = B("GBUF")
        for c in range(NC):
            tyb = B("TYF")

            def zn1(e, c=c, w=w):
                return e.tensor_tensor(TYF[:, 0:w], ZB[:, c, 0:w], PS[:, 6, 0:w],
                                       op=ALU.mult)
            tk1 = P.op("dve", zn1, deps=[tbz] + B(f"YB{c}").rdeps() + tyb.wdeps())
            B(f"YB{c}").read(tk1)
            psbank(6).read(tk1)
            tyb.wrote(tk1)

            def zn2(e, c=c, w=w):
                return e.tensor_tensor(GBUF[:, c, 0:w], TYF[:, 0:w], PS[:, 5, 0:w],
                                       op=ALU.add)
            tk = P.op("dve", zn2, deps=[tbzm, tk1] + gb.wdeps())
            zn_toks.append(tk)
            tyb.read(tk)
            psbank(5).read(tk)
        gb.wrote(*zn_toks)
        for nc_i in range(NC):
            bank = nc_i % 2
            pb = psbank(bank)
            fns = []
            for c in range(NC):
                def mm(e, c=c, nc_i=nc_i, w=w, bank=bank):
                    return e.matmul(PS[:, bank, 0:w], W2S[:, c, 128 * nc_i:128 * nc_i + 128],
                                    GBUF[:, c, 0:w], start=(c == 0), stop=False)
                fns.append(mm)

            def mmb(e, nc_i=nc_i, w=w, bank=bank):
                return e.matmul(PS[:, bank, 0:w], b2eff[0:1, 128 * nc_i:128 * nc_i + 128],
                                onesb[0:1, 0:w], start=False, stop=True)
            fns.append(mmb)
            tk = P.ops("pe", fns, deps=zn_toks + w2s_b.rdeps()
                       + b2eff_b.rdeps() + [t_onesb] + pb.wdeps())
            pb.wrote(tk)
            gb.read(tk)
            w2s_b.read(tk)
            b2eff_b.read(tk)
            yb_ = B(f"YB{nc_i}")

            def ev(e, nc_i=nc_i, w=w, bank=bank):
                return e.activation(YB[:, nc_i, 0:w], PS[:, bank, 0:w], AF.Copy)
            te = P.op("act", ev, deps=[tk] + yb_.wdeps())
            pb.read(te)
            yb_.wrote(te)
        rsy, rsy_b, nmursy, nmursy_b, _, _ = ln_stats_bf16(
            lambda c: (YB[:, c, 0:w], B(f"YB{c}").rdeps(), B(f"YB{c}")), w,
            "rsy", "nmursy", 0, GBUF, "GBUF", src_f32=True, make_bf=False)
        tby = bcast_row(rsy, rsy_b, 0, w, 6)
        tbm = bcast_row(nmursy, nmursy_b, 0, w, 5)
        xup = []
        for c in range(NC):
            tyb = B("TYF")

            def f1(e, c=c, w=w):
                return e.tensor_tensor(TYF[:, 0:w], YB[:, c, 0:w], PS[:, 6, 0:w],
                                       op=ALU.mult)
            t1 = P.op("dve", f1, deps=[tby] + B(f"YB{c}").rdeps() + tyb.wdeps())
            B(f"YB{c}").read(t1)
            psbank(6).read(t1)
            tyb.wrote(t1)

            def f2(e, c=c, w=w):
                return e.tensor_tensor(XGb[:, c, 0:w], XGb[:, c, 0:w], PS[:, 5, 0:w],
                                       op=ALU.add)
            t2 = P.op("dve", f2, deps=[tbm] + xgb_b.wdeps())
            psbank(5).read(t2)

            def f3(e, c=c, w=w):
                return e.tensor_tensor(XGb[:, c, 0:w], XGb[:, c, 0:w], TYF[:, 0:w],
                                       op=ALU.add)
            t3 = P.op("dve", f3, deps=[t1, t2])
            tyb.read(t3)
            xup.append(t3)
        xgb_b.wrote(*xup)
        store_grp(XGb, xgb_b, xdram, g, XD(g), "xs2")

    txa = tap_dram("xA", xdram)
    P.barrier()
    a2_st.close()
    sa_st.close()

    if upto == "a2":
        fin_dram_taps([txa])
        return nc

    rb_reset()
    rx("rs20", 512)
    rx("nmurs20", 512)
    rx("rs21", 512)
    rx("nmurs21", 512)
    b0_st = ExitStack()
    XGc = P.sb("XGc", [128, 2, NC, 512], F32, stack=b0_st)
    SQB0 = P.sb("SQB0", [128, 2, NC, 512], BF16, stack=b0_st)
    ld_pend = {}
    for g in range(TG):
        P.marker(f"b0-g{g}")
        w = gw(g)
        gp = g % 2
        xgc_b = B(f"XGc{gp}")
        if g == 0:
            ld_pend[0] = load_grp(XGc[:, 0], xgc_b, xdram, 0, XD(0), "xl3")
        t_load = ld_pend.pop(g)
        if g + 1 < TG:
            ld_pend[g + 1] = load_grp(XGc[:, (g + 1) % 2], B(f"XGc{(g + 1) % 2}"),
                                      xdram, g + 1, XD(g + 1), "xl3")
        bk_r, bk_m = (6, 5) if gp == 0 else (4, 3)
        rs2, rs2_b, nmurs2, nmurs2_b, _, _ = ln_stats_bf16(
            lambda c: (XGc[:, gp, c, 0:w], [t_load], xgc_b), w,
            f"rs2{gp}", f"nmurs2{gp}", 0, SQB0[:, gp], f"SQB0{gp}",
            src_f32=True, make_bf=False, par=gp)
        tbr = bcast_row(rs2, rs2_b, 0, w, bk_r)
        tbm = bcast_row(nmurs2, nmurs2_b, 0, w, bk_m)
        x2toks = []
        for c in range(NC):
            def f1(e, c=c, w=w, gp=gp, bk_r=bk_r):
                return e.tensor_tensor(XGc[:, gp, c, 0:w], XGc[:, gp, c, 0:w],
                                       PS[:, bk_r, 0:w], op=ALU.mult)
            t1 = P.op("dve", f1, deps=[tbr, t_load] + xgc_b.wdeps())
            psbank(bk_r).read(t1)

            def f2(e, c=c, w=w, gp=gp, bk_m=bk_m):
                return e.tensor_tensor(XGc[:, gp, c, 0:w], XGc[:, gp, c, 0:w],
                                       PS[:, bk_m, 0:w], op=ALU.add)
            t2 = P.op("dve", f2, deps=[tbm, t1])
            psbank(bk_m).read(t2)
            x2toks.append(t2)
        xgc_b.wrote(*x2toks)
        store_grp(XGc[:, gp], xgc_b, x2dram, g, X2DB(g), "xs3")
    P.barrier()
    b0_st.close()

    tx2 = tap_dram("x2", x2dram)
    if upto == "b0":
        fin_dram_taps([tx2])
        return nc

    def stage_b_layer(l, upto_l=None):
        rb_reset()
        b1_stk = ExitStack()
        X2G = P.sb(f"X2G_{l}", [128, NC, 513], F32, stack=b1_stk)
        VTB = P.sb(f"VTB_{l}", [128, TT, 512], BF16, stack=b1_stk)
        QR = P.sb(f"QR_{l}", [128, PAIRS_F, T], BF16, stack=b1_stk)
        KR = P.sb(f"KR_{l}", [128, PAIRS_F, T], BF16, stack=b1_stk)
        b1i_stk = ExitStack()
        WQKV = P.sb(f"WQKV_{l}", [128, NC, 1536], BF16, stack=b1i_stk)
        CQL = P.sb(f"CQL_{l}", [128, TT, 64], F32, stack=b1i_stk)
        SQL = P.sb(f"SQL_{l}", [128, TT, 64], F32, stack=b1i_stk)
        CKL = P.sb(f"CKL_{l}", [128, TT, 64], F32, stack=b1i_stk)
        SKL = P.sb(f"SKL_{l}", [128, TT, 64], F32, stack=b1i_stk)
        QS64 = P.sb(f"QS64_{l}", [128, 4, 64], F32, stack=b1i_stk)
        X2B = P.sb(f"X2B_{l}", [128, NC, 513], BF16, stack=b1i_stk)
        SQR = P.sb(f"SQR_{l}", [128, 2, 512], BF16, stack=b1i_stk)
        NRM = P.sb(f"NRM_{l}", [128, 2, 4, 8], F32, stack=b1i_stk)
        TAF = P.sb(f"TAF_{l}", [128, 2, 512], F32, stack=b1i_stk)
        TBF = P.sb(f"TBF_{l}", [128, 2, 512], F32, stack=b1i_stk)
        TCF = P.sb(f"TCF_{l}", [128, 2, 512], F32, stack=b1i_stk)
        QRtm = P.sb(f"QRtm_{l}", [128, 2, 4, 512], BF16, stack=b1i_stk)

        wqkv_toks = []
        for nq in range(3):
            tkq = P.dma("gp", WQKV[:, :, 512 * nq:512 * nq + 512],
                        wqkvd[l][:, 512 * nq:512 * nq + 512]
                        .rearrange("(kc p) n -> p kc n", p=128),
                        deps=B("WQKV").wdeps(), stream=f"wqkv{nq}")
            wqkv_toks.append(tkq)
        B("WQKV").wrote(*wqkv_toks)
        tbl_toks = []
        for (dst, dname, raw) in ((CQL, "CQL", CQd), (SQL, "SQL", SQd),
                                  (CKL, "CKL", CKd), (SKL, "SKL", SKd)):
            tk = P.dma("sp", dst[:, :, :], raw, deps=B(dname).wdeps(), stream="tbl")
            B(dname).wrote(tk)
            tbl_toks.append(tk)
        bias_ap, bias_b = rf("bias")
        for (iq, vec) in ((0, qscaled), (1, kscaled)):
            tk = P.dma("sp", ROWF[0:1, RF["bias"]:RF["bias"] + 64],
                       vec[l].rearrange("(one n) -> one n", one=1),
                       deps=bias_b.wdeps(), stream="bias")
            bias_b.wrote(tk)
            pb = psbank(6)

            def mm(e, iq=iq):
                return e.matmul(PS[:, 6, 0:64], ones[0:1, 0:128],
                                ROWF[0:1, RF["bias"]:RF["bias"] + 64],
                                start=True, stop=True)
            tk2 = P.op("pe", mm, deps=[tk, t_ones] + pb.wdeps())
            pb.wrote(tk2)
            bias_b.read(tk2)
            qs_b = B("QS64")

            def cp(e, iq=iq):
                return e.tensor_copy(QS64[:, iq, :], PS[:, 6, 0:64])
            tk3 = P.op("dve", cp, deps=[tk2] + qs_b.wdeps())
            pb.read(tk3)
            qs_b.w.append(tk3)
        qs_b = B("QS64")
        swp_toks = []
        for iq in range(2):
            def c1(e, iq=iq):
                return e.tensor_copy(QS64[:, 2 + iq, 0:32], QS64[:, iq, 32:64])
            ta = P.op("dve", c1, deps=qs_b.wdeps())

            def c2(e, iq=iq):
                return e.tensor_copy(QS64[:, 2 + iq, 32:64], QS64[:, iq, 0:32])
            tb2 = P.op("dve", c2, deps=qs_b.wdeps() + [ta])
            swp_toks.extend([ta, tb2])
        qs_b.w.extend(swp_toks)
        fold_toks = []
        for (dname, dst, iq) in (("CQL", CQL, 0), ("SQL", SQL, 2),
                                 ("CKL", CKL, 1), ("SKL", SKL, 3)):
            def ff(e, dst=dst, iq=iq):
                return e.tensor_tensor(
                    dst[:, :, :], dst[:, :, :],
                    QS64[:, iq:iq + 1, :].broadcast_to([128, TT, 64]), op=ALU.mult)
            tk = P.op("dve", ff, deps=B(dname).rdeps() + B("QS64").rdeps())
            B(dname).wrote(tk)
            B("QS64").read(tk)
            fold_toks.append(tk)

        tr_pend = []

        def drain_transposes(keep_last):
            while len(tr_pend) > keep_last:
                ptau, piq, ppar, pt10, PDST, pdstname = tr_pend.pop(0)
                qrtm_b = B(f"QRtm{piq}{ppar}")
                dstb = B(pdstname)
                for pr in range(4):
                    pbb = B("PSBbank")

                    def tr(e, piq=piq, ppar=ppar, pr=pr):
                        return e.transpose(PSB[:, 0:128],
                                           QRtm[:, piq, ppar, 128 * pr:128 * pr + 128],
                                           idb[:, :])
                    tt = P.op("pe", tr, deps=[pt10, t_idb] + pbb.wdeps())
                    pbb.wrote(tt)
                    qrtm_b.read(tt)

                    def cpv(e, pr=pr, ptau=ptau, PDST=PDST):
                        return e.activation(PDST[:, pr, 128 * ptau:128 * ptau + 128],
                                            PSB[:, 0:128], AF.Copy)
                    tv = P.op("act", cpv, deps=[tt] + dstb.wdeps())
                    pbb.read(tv)
                    dstb.w.append(tv)

        for g in range(TG):
            P.marker(f"b1-g{g}-L{l}")
            w = gw(g)
            x2g_b = B("X2G")
            x2b_b = B("X2B")
            if g == 0:
                tl = P.dma("sp", X2G[:, :, 1:513], x2dram[:, :, 0:512],
                           deps=x2g_b.wdeps() + X2DB(0).rdeps(), stream="x2l")

                def mz(e):
                    return e.memset(X2G[:, :, 0:1], 0.0)
                tz = P.op("dve", mz, deps=x2g_b.wdeps())
                ld_toks = [tl, tz]
            else:
                tl = P.dma("sp", X2G[:, :, 0:513],
                           x2dram[:, :, 512 * g - 1:512 * g + 512],
                           deps=x2g_b.wdeps() + X2DB(g).rdeps() + X2DB(g - 1).rdeps(),
                           stream="x2l")
                ld_toks = [tl]
            x2g_b.wrote(*ld_toks)
            X2DB(g).read(*ld_toks)
            casts = []
            for c in range(NC):
                def cp(e, c=c):
                    return e.activation(X2B[:, c, :], X2G[:, c, :], AF.Copy)
                tk = P.op("act", cp, deps=ld_toks + x2b_b.wdeps(), cost=600)
                casts.append(tk)
                x2g_b.read(tk)
            x2b_b.wrote(*casts)
            for j in range(w // 128):
                tau = 4 * g + j
                drain_transposes(3)
                bmap = (0, 1, 2) if tau % 2 == 0 else (3, 4, 5)
                mm_toks = []
                for nq in range(3):
                    bank = bmap[nq]
                    pb = psbank(bank)
                    fns = []
                    for c in range(NC):
                        off = (1 + 128 * j) if c < 4 else (128 * j)

                        def mm(e, c=c, nq=nq, off=off, bank=bank):
                            return e.matmul(PS[:, bank, :],
                                            X2B[:, c, off:off + 128],
                                            WQKV[:, c, 512 * nq:512 * nq + 512],
                                            start=(c == 0), stop=(c == NC - 1))
                        fns.append(mm)
                    tk = P.ops("pe", fns, deps=casts + [wqkv_toks[nq]] + pb.wdeps())
                    pb.wrote(tk)
                    x2b_b.read(tk)
                    mm_toks.append(tk)
                vtb_b = B(f"VTB{tau // 4}")

                def ev(e, tau=tau, bv=bmap[2]):
                    return e.activation(VTB[:, tau, :], PS[:, bv, :], AF.Copy)
                te = P.op("act", ev, deps=[mm_toks[2]] + vtb_b.wdeps())
                psbank(bmap[2]).read(te)
                vtb_b.w.append(te)
                for (iq, bank, TBL_C, TBL_S, DST, dstname) in (
                        (0, bmap[0], CQL, SQL, QR, "QR"), (1, bmap[1], CKL, SKL, KR, "KR")):
                    pb = psbank(bank)
                    cpar = (2 * tau + iq) % 2
                    tpar = tau % 2
                    sq_b = B(f"SQR{iq}")

                    def f_sq(e, bank=bank, iq=iq):
                        return e.activation(SQR[:, iq, :], PS[:, bank, :], AF.Square)
                    t1 = P.op("act", f_sq, deps=[mm_toks[iq]] + sq_b.wdeps(), cost=700)
                    sq_b.wrote(t1)
                    nrm_b = B(f"NRM{tpar}{iq}")

                    def f_red(e, iq=iq, tpar=tpar):
                        return e.reduce_sum(NRM[:, tpar, 2 * iq, :],
                                            SQR[:, iq, :].rearrange("p (h d) -> p h d", h=8),
                                            axis=AX.X)
                    t2 = P.op("dve", f_red, deps=[t1] + nrm_b.wdeps(), cost=700)
                    sq_b.read(t2)
                    nrm_b.w.append(t2)

                    def f_max(e, iq=iq, tpar=tpar):
                        return e.tensor_scalar_max(NRM[:, tpar, 2 * iq + 1, :],
                                                   NRM[:, tpar, 2 * iq, :], 1e-24)
                    t3 = P.op("dve", f_max, deps=[t2], cost=150)
                    nrm_b.w.append(t3)

                    def f_ln(e, iq=iq, tpar=tpar):
                        return e.activation(NRM[:, tpar, 2 * iq + 1, :],
                                            NRM[:, tpar, 2 * iq + 1, :], AF.Ln)
                    t4 = P.op("act", f_ln, deps=[t3], cost=250)
                    nrm_b.w.append(t4)

                    def f_exp(e, iq=iq, tpar=tpar):
                        return e.activation(NRM[:, tpar, 2 * iq, :],
                                            NRM[:, tpar, 2 * iq + 1, :],
                                            AF.Exp, scale=-0.5)
                    t5 = P.op("act", f_exp, deps=[t4], cost=250)
                    nrm_b.w.append(t5)
                    ta_b = B(f"TAF{cpar}")

                    def f_t0(e, iq=iq, bank=bank, cpar=cpar, tpar=tpar):
                        return e.tensor_tensor(
                            TAF[:, cpar, :].rearrange("p (h d) -> p h d", h=8),
                            PS[:, bank, :].rearrange("p (h d) -> p h d", h=8),
                            NRM[:, tpar, 2 * iq, :].unsqueeze(2)
                            .broadcast_to([128, 8, 64]),
                            op=ALU.mult)
                    t6 = P.op("dve", f_t0, deps=[t5, mm_toks[iq]] + ta_b.wdeps(), cost=700)
                    pb.read(t6)
                    nrm_b.r.append(t6)
                    ta_b.wrote(t6)
                    tb_b = B(f"TBF{cpar}")

                    def f_cos(e, tau=tau, TBL_C=TBL_C, cpar=cpar):
                        return e.tensor_tensor(
                            TBF[:, cpar, :].rearrange("p (h d) -> p h d", h=8),
                            TAF[:, cpar, :].rearrange("p (h d) -> p h d", h=8),
                            TBL_C[:, tau:tau + 1, :].broadcast_to([128, 8, 64]),
                            op=ALU.mult)
                    t7 = P.op("dve", f_cos, deps=[t6] + fold_toks + tb_b.wdeps(), cost=700)
                    ta_b.read(t7)
                    tb_b.wrote(t7)
                    def f_sinA(e, tau=tau, TBL_S=TBL_S, cpar=cpar):
                        return e.tensor_tensor(
                            TCF[:, cpar, :].rearrange("p (h d) -> p h d", h=8)[:, :, 0:32],
                            TAF[:, cpar, :].rearrange("p (h d) -> p h d", h=8)[:, :, 32:64],
                            TBL_S[:, tau:tau + 1, 0:32].broadcast_to([128, 8, 32]),
                            op=ALU.mult)
                    tc_b = B(f"TCF{cpar}")
                    t8 = P.op("dve", f_sinA, deps=[t6, t7] + fold_toks + tc_b.wdeps(),
                              cost=450)

                    def f_sinB(e, tau=tau, TBL_S=TBL_S, cpar=cpar):
                        return e.tensor_tensor(
                            TCF[:, cpar, :].rearrange("p (h d) -> p h d", h=8)[:, :, 32:64],
                            TAF[:, cpar, :].rearrange("p (h d) -> p h d", h=8)[:, :, 0:32],
                            TBL_S[:, tau:tau + 1, 32:64].broadcast_to([128, 8, 32]),
                            op=ALU.mult)
                    t9 = P.op("dve", f_sinB, deps=[t6, t7, t8], cost=450)
                    ta_b.read(t8, t9)
                    tc_b.wrote(t8, t9)
                    par = tau % 4
                    qrtm_b = B(f"QRtm{iq}{par}")

                    def f_add(e, iq=iq, par=par, cpar=cpar):
                        return e.tensor_tensor(QRtm[:, iq, par, :], TBF[:, cpar, :],
                                               TCF[:, cpar, :], op=ALU.add)
                    t10 = P.op("dve", f_add, deps=[t7, t8, t9] + qrtm_b.wdeps(), cost=700)
                    tb_b.read(t10)
                    tc_b.read(t10)
                    qrtm_b.wrote(t10)
                    tr_pend.append((tau, iq, par, t10, DST, dstname))

        drain_transposes(0)
        tap("VTB", VTB[:, :, :], [t for g in range(TG) for t in B(f"VTB{g}").w], dtype=BF16)
        tap("QR", QR[:, :, :], B("QR").w, dtype=BF16)
        tap("KR", KR[:, :, :], B("KR").w, dtype=BF16)
        if upto_l == "b1":
            return b1i_stk, b1_stk

        P.barrier()
        b1i_stk.close()
        b2_stk = ExitStack()
        WOS = P.sb(f"WOS_{l}", [128, 4, 1024], BF16, stack=b2_stk)
        AVF = P.sb(f"AVF_{l}", [128, PAIRS_F, T], BF16, stack=b2_stk)
        PTB = P.sb(f"PTB_{l}", [128, 2, 1280], BF16, stack=b2_stk)
        RCPT = P.sb(f"RCPT_{l}", [1, 2 * PAIRS_F, T], BF16, stack=b2_stk)
        t_wo = P.dma("gp", WOS[:, :, :],
                     wod[l].rearrange("(ic p) n -> p ic n", p=128),
                     deps=B("WOS").wdeps(), stream="wo")
        B("WOS").wrote(t_wo)
        PSf = PS.rearrange("p b n -> p (b n)")

        qr_b = B("QR")
        kr_b = B("KR")
        PSW = PS.rearrange("p b n -> p (b n)")

        def blk(par):
            return (1536 * par, [psbank(3 * par), psbank(3 * par + 1), psbank(3 * par + 2)])

        def segs_of(fb, ncw):
            out = []
            pos = fb
            endp = fb + 128 * ncw
            while pos < endp:
                seg = min(endp - pos, 512 - (pos % 512) if pos % 512 else 512)
                out.append((pos, seg))
                pos += seg
            return out

        for i in range(PAIRS_F):
            P.marker(f"b2-pair{i}-L{l}")

            def front(tau, i=i):
                c0 = max(0, tau - 4)
                ncw = tau - c0 + 1
                par = tau % 2
                fb, bbufs = blk(par)
                bdeps = [d for b_ in bbufs for d in b_.wdeps()]
                sl_hi = tau - c0
                lo_mask = (c0 == tau - 4)
                sim_fns = []
                for sidx in range(ncw):
                    for h, (p0, p1) in enumerate(((0, 64), (64, 128))):
                        c = c0 + sidx
                        off = fb + 640 * h + 128 * sidx
                        bank, boff = off // 512, off % 512

                        def mm(e, i=i, tau=tau, c=c, bank=bank, boff=boff,
                               p0=p0, p1=p1):
                            return e.matmul(PS[:, bank, boff:boff + 128],
                                            KR[p0:p1, i, 128 * c:128 * c + 128],
                                            QR[p0:p1, i, 128 * tau:128 * tau + 128],
                                            start=True, stop=True,
                                            tile_position=(p0, 0))
                        sim_fns.append(mm)
                tks = P.ops("pe", sim_fns, deps=qr_b.rdeps() + kr_b.rdeps() + bdeps,
                            cost=200 + 150 * ncw)
                qr_b.read(tks)
                kr_b.read(tks)
                for b_ in bbufs:
                    b_.wrote(tks)
                m2D = PSW[:, fb:fb + 1280].rearrange("p (two n) -> p two n", two=2)
                mtoks = [tks]

                def addm(e, m2D=m2D, sl=sl_hi):
                    return e.tensor_tensor(m2D[:, :, 128 * sl:128 * sl + 128],
                                           m2D[:, :, 128 * sl:128 * sl + 128],
                                           MK[:, 0:1, :].broadcast_to([128, 2, 128]),
                                           op=ALU.add)
                tm1 = P.op("dve", addm, deps=[tks] + t_mk, cost=500)
                mtoks.append(tm1)
                if lo_mask and sl_hi != 0:
                    def addl(e, m2D=m2D):
                        return e.tensor_tensor(m2D[:, :, 0:128],
                                               m2D[:, :, 0:128],
                                               MK[:, 1:2, :].broadcast_to([128, 2, 128]),
                                               op=ALU.add)
                    tm2 = P.op("dve", addl, deps=[tks, tm1] + t_mk, cost=500)
                    mtoks.append(tm2)
                for b_ in bbufs:
                    b_.w.extend(mtoks[1:])
                ptb_b = B(f"PTB{par}")
                exp_deps = mtoks + ptb_b.wdeps()
                tex_l = []
                for h in range(2):
                    def fex(e, fb=fb, h=h, par=par, ww=128 * ncw):
                        return e.activation(PTB[:, par, 640 * h:640 * h + ww],
                                            PSW[:, fb + 640 * h:fb + 640 * h + ww],
                                            AF.Exp, scale=8.0)
                    tex_l.append(P.op("act", fex, deps=exp_deps + tex_l,
                                      cost=200 + 130 * ncw))
                for b_ in bbufs:
                    b_.read(*tex_l)
                ptb_b.wrote(*tex_l)

            def mid(tau, i=i):
                c0 = max(0, tau - 4)
                ncw = tau - c0 + 1
                par = tau % 2
                fb, bbufs = blk(par)
                ptb_b = B(f"PTB{par}")
                avr_b = B(f"avr{par}")
                s6_b = B("s6")
                fns = []
                for h in range(2):
                    for sidx in range(ncw):
                        c = c0 + sidx

                        def pv(e, i=i, h=h, c=c, sidx=sidx, ncw=ncw, fb=fb, par=par):
                            if h == 0:
                                pslice = PSW[0:64, fb + 1280:fb + 1408]
                            else:
                                pslice = PSW[64:128, fb + 1408:fb + 1536]
                            return e.matmul(pslice,
                                            VTB[:, c, 64 * (2 * i + h):64 * (2 * i + h) + 64],
                                            PTB[:, par,
                                                640 * h + 128 * sidx:640 * h + 128 * sidx + 128],
                                            start=(sidx == 0), stop=(sidx == ncw - 1),
                                            tile_position=(0, 64 * h))
                        fns.append(pv)
                for h in range(2):
                    for sidx in range(ncw):
                        def sm(e, h=h, sidx=sidx, ncw=ncw, par=par):
                            return e.matmul(PS[0:1, 6, 256 + 128 * h:384 + 128 * h],
                                            onescol[:, :],
                                            PTB[:, par,
                                                640 * h + 128 * sidx:640 * h + 128 * sidx + 128],
                                            start=(sidx == 0), stop=(sidx == ncw - 1))
                        fns.append(sm)
                vdeps = []
                for cc in range(c0, tau + 1):
                    vdeps += B(f"VTB{cc // 4}").rdeps()
                tk = P.ops("pe", fns, deps=ptb_b.rdeps() + vdeps + avr_b.wdeps()
                           + s6_b.wdeps() + [t_onescol], cost=150 + 200 * ncw)
                for cc in range(c0, tau + 1):
                    B(f"VTB{cc // 4}").read(tk)
                ptb_b.read(tk)
                avr_b.wrote(tk)
                s6_b.wrote(tk)
                rcp_b = B(f"RCPT{i}")
                trcs = []
                for h in range(2):
                    def frc(e, tau=tau, h=h, i=i):
                        with nc.allow_low_precision(reason="bf16 recip rows for attn norm"):
                            return e.reciprocal(RCPT[0:1, 2 * i + h,
                                                     128 * tau:128 * tau + 128],
                                                PS[0:1, 6, 256 + 128 * h:384 + 128 * h])
                    trc = P.op("dve", frc, deps=[tk] + rcp_b.wdeps() + trcs, cost=300)
                    trcs.append(trc)
                s6_b.read(*trcs)
                rcp_b.w.extend(trcs)

            def back(tau, i=i):
                par = tau % 2
                fb, _ = blk(par)
                avr_b = B(f"avr{par}")
                avf_b = B("AVF")

                def evA(e, i=i, tau=tau, fb=fb):
                    return e.activation(AVF[0:64, i, 128 * tau:128 * tau + 128],
                                        PSW[0:64, fb + 1280:fb + 1408], AF.Copy)
                teA = P.op("act", evA, deps=avr_b.rdeps() + avf_b.wdeps(), cost=250)

                def evB(e, i=i, tau=tau, fb=fb):
                    return e.activation(AVF[64:128, i, 128 * tau:128 * tau + 128],
                                        PSW[64:128, fb + 1408:fb + 1536], AF.Copy)
                teB = P.op("act", evB, deps=avr_b.rdeps() + [teA] + avf_b.wdeps(),
                           cost=250)
                avr_b.read(teA, teB)
                avf_b.w.extend([teA, teB])

            for tau in range(TT):
                front(tau)
                if tau >= 1:
                    mid(tau - 1)
                if tau >= 2:
                    back(tau - 2)
            mid(TT - 1)
            back(TT - 2)
            back(TT - 1)

        tap("AVF", AVF[:, :, :], B("AVF").w, dtype=BF16)
        if upto_l == "b2":
            return b2_stk, b1_stk

        X2R = P.sb(f"X2R_{l}", [128, NC, 512], F32, stack=b2_stk)
        avf_b = B("AVF")
        for g in range(TG):
            P.marker(f"b3-g{g}-L{l}")
            w = gw(g)
            x2r_b = B("X2R")
            tl = load_grp(X2R, x2r_b, x2dram, g, X2DB(g), "x2r")
            for ic in range(PAIRS_F):
                rcp_b = B(f"RCPT{ic}")
                pb2 = psbank(2)
                pb3 = psbank(3)

                def nbA(e, ic=ic, g=g, w=w):
                    return e.matmul(PS[:, 2, 0:w], onesb[0:1, 0:128],
                                    RCPT[0:1, 2 * ic, 512 * g:512 * g + w],
                                    start=True, stop=True)

                def nbB(e, ic=ic, g=g, w=w):
                    return e.matmul(PS[:, 3, 0:w], onesb[0:1, 0:128],
                                    RCPT[0:1, 2 * ic + 1, 512 * g:512 * g + w],
                                    start=True, stop=True)
                tb_ = P.ops("pe", [nbA, nbB], deps=rcp_b.rdeps() + [t_onesb]
                            + pb2.wdeps() + pb3.wdeps()
                            + B("avr0").wdeps() + B("avr1").wdeps(), cost=2000)
                rcp_b.read(tb_)
                pb2.wrote(tb_)
                pb3.wrote(tb_)

                def nmA(e, ic=ic, g=g, w=w):
                    return e.tensor_tensor(AVF[0:64, ic, 512 * g:512 * g + w],
                                           AVF[0:64, ic, 512 * g:512 * g + w],
                                           PS[0:64, 2, 0:w], op=ALU.mult)
                tnA = P.op("dve", nmA, deps=[tb_] + avf_b.wdeps(), cost=700)

                def nmB(e, ic=ic, g=g, w=w):
                    return e.tensor_tensor(AVF[64:128, ic, 512 * g:512 * g + w],
                                           AVF[64:128, ic, 512 * g:512 * g + w],
                                           PS[64:128, 3, 0:w], op=ALU.mult)
                tnB = P.op("dve", nmB, deps=[tb_, tnA] + avf_b.wdeps(), cost=700)
                pb2.read(tnA)
                pb3.read(tnB)
                avf_b.w.extend([tnA, tnB])
            up_toks = []
            for nci in range(NC):
                bank = nci % 2
                pb = psbank(bank)
                fns = []
                for ic in range(4):
                    def mm(e, ic=ic, nci=nci, g=g, w=w, bank=bank):
                        return e.matmul(PS[:, bank, 0:w],
                                        WOS[:, ic, 128 * nci:128 * nci + 128],
                                        AVF[:, ic, 512 * g:512 * g + w],
                                        start=(ic == 0), stop=(ic == 3))
                    fns.append(mm)
                tk = P.ops("pe", fns, deps=avf_b.rdeps() + [t_wo] + pb.wdeps())
                pb.wrote(tk)
                avf_b.read(tk)

                def fr(e, nci=nci, w=w, bank=bank):
                    return e.tensor_tensor(X2R[:, nci, 0:w], X2R[:, nci, 0:w],
                                           PS[:, bank, 0:w], op=ALU.add)
                t2 = P.op("dve", fr, deps=[tk, tl])
                pb.read(t2)
                up_toks.append(t2)
            x2r_b.wrote(*up_toks)
            store_grp(X2R, x2r_b, x2dram, g, X2DB(g), "x2w")
        if upto_l == "b3":
            return (b2_stk, b1_stk)
        P.barrier()
        b2_stk.close()
        b1_stk.close()

        f_stk = ExitStack()
        X2G4 = P.sb(f"X2G4_{l}", [128, 2, NC, 513], F32, stack=f_stk)
        XNF = P.sb(f"XNF_{l}", [128, NC, T], BF16, stack=f_stk)
        HG = P.sb(f"HG_{l}", [128, 11, T], BF16, stack=f_stk)
        W1C = P.sb(f"W1C_{l}", [128, NC, 2, 2, 128], BF16, stack=f_stk)
        GELS = P.sb(f"GELS_{l}", [128, 2, 512], F32, stack=f_stk)
        f_sq_stk = ExitStack()
        SQF = P.sb(f"SQF_{l}", [128, NC, 512], BF16, stack=f_sq_stk)
        rx("rsf0", T // 2)
        rx("rsf0m", T // 2)
        rx("rsf1", T // 2)
        rx("rsf1m", T // 2)
        gcolf_b = vec_to_cols(lngd[l].rearrange("n -> (n)") if False else lngd[l],
                              1024, 48, f"gcolf{l}")
        lnbf_b = vec_to_cols(lnbd[l], 1024, 56, f"lnbcol{l}")
        lnbb_b = B(f"lnbcolb{l}")
        bt2 = []
        for c in range(NC):
            def cpb(e, c=c):
                return e.tensor_copy(COLB[:, 8 + c:9 + c], COLF[:, 56 + c:57 + c])
            tkb = P.op("dve", cpb, deps=lnbf_b.rdeps() + lnbb_b.wdeps())
            lnbf_b.read(tkb)
            bt2.append(tkb)
        lnbb_b.wrote(*bt2)
        rx("b2f", 1024)
        b2frow, b2f_b = vec_to_row(b2fd[l], 1024, "b2f", f32=True, stream="bias2")
        for _rot in (0, 1):
            for _pu in (0, 1):
                B(f"fbias{_rot}{_pu}").r.extend(B("rf_bias").w + B("rf_bias").r)

        P.marker(f"b4stats-L{l}")
        def b4s_load(g):
            gp = g % 2
            x4_b = B(f"X2G4{gp}")
            if g == 0:
                tl = P.dma("sp", X2G4[:, 0, :, 1:513], x2dram[:, :, 0:512],
                           deps=x4_b.wdeps() + X2DB(0).rdeps(), stream="x2f")

                def mz(e):
                    return e.memset(X2G4[:, 0, :, 0:1], 0.0)
                tz = P.op("dve", mz, deps=x4_b.wdeps())
                ld_toks = [tl, tz]
            else:
                tl = P.dma("sp", X2G4[:, gp, :, 0:513],
                           x2dram[:, :, 512 * g - 1:512 * g + 512],
                           deps=x4_b.wdeps() + X2DB(g).rdeps() + X2DB(g - 1).rdeps(),
                           stream="x2f")
                ld_toks = [tl]
            x4_b.wrote(*ld_toks)
            X2DB(g).read(*ld_toks)
            return ld_toks

        b4s_pend = {0: b4s_load(0)}
        for g in range(TG):
            w = gw(g)
            gp = g % 2
            x4_b = B(f"X2G4{gp}")
            ld_toks = b4s_pend.pop(g)
            if g + 1 < TG:
                b4s_pend[g + 1] = b4s_load(g + 1)

            bk_r, bk_m = (6, 5) if gp == 0 else (4, 3)
            roff = 512 * (g // 2)

            def shsrc(c, gp=gp, ld_toks=ld_toks, x4_b=x4_b):
                off = 1 if c < 4 else 0
                return (X2G4[:, gp, c, off:off + w], list(ld_toks), x4_b)
            rsf, rsf_b, nmursf, nmursf_b, _, _ = ln_stats_bf16(
                shsrc, w, f"rsf{gp}", f"rsf{gp}m", roff, SQF, "SQF",
                src_f32=True, make_bf=False, par=gp)
            tbr = bcast_row(rsf, rsf_b, roff, w, bk_r)
            tbm = bcast_row(nmursf, nmursf_b, roff, w, bk_m)
            xnf_b = B("XNF")
            for c in range(NC):
                off = 1 if c < 4 else 0
                tyb = B(f"GELS{gp}")

                def x1(e, c=c, w=w, off=off, gp=gp, bk_r=bk_r):
                    return e.scalar_tensor_tensor(GELS[:, gp, 0:w],
                                                  X2G4[:, gp, c, off:off + w],
                                                  COLF[:, 48 + c:49 + c], PS[:, bk_r, 0:w],
                                                  op0=ALU.mult, op1=ALU.mult)
                t1 = P.op("dve", x1, deps=[tbr] + list(ld_toks) + gcolf_b.rdeps()
                          + tyb.wdeps())
                x4_b.read(t1)
                gcolf_b.read(t1)
                psbank(bk_r).read(t1)
                tyb.wrote(t1)

                def x2f_(e, c=c, w=w, g=g, gp=gp, bk_m=bk_m):
                    return e.scalar_tensor_tensor(XNF[:, c, 512 * g:512 * g + w],
                                                  PS[:, bk_m, 0:w], COLF[:, 48 + c:49 + c],
                                                  GELS[:, gp, 0:w], op0=ALU.mult, op1=ALU.add)
                t2 = P.op("dve", x2f_, deps=[tbm, t1] + xnf_b.wdeps())
                psbank(bk_m).read(t2)
                tyb.read(t2)
                gcolf_b.read(t2)
                xnf_b.w.append(t2)

        f_sq_stk.close()
        W2H = P.sb(f"W2H_{l}", [128, 11, 1024], BF16, stack=f_stk)
        for half in range(2):
            P.marker(f"b4half{half}-L{l}")
            w2h_b = B("W2H")
            r0 = 1408 * half
            nfull = 11 if half == 0 else 10
            tw2toks = []
            tw2a = P.dma("gp", W2H[:, 0:nfull, :],
                         w2fd[l][r0:r0 + 128 * nfull, :]
                         .rearrange("(kc p) n -> p kc n", p=128),
                         deps=w2h_b.wdeps() + B("SQF").wdeps(), stream="w2h")
            tw2toks.append(tw2a)
            if half == 1:
                tw2b = P.dma("gp", W2H[0:42, 10, :], w2fd[l][2688:2730, :],
                             deps=w2h_b.wdeps(), stream="w2h")
                tw2toks.append(tw2b)
            w2h_b.wrote(*tw2toks)
            def issue_w1(ci, half=half):
                gci = 11 * half + ci
                aw = 128 if gci < 21 else 42
                rot = ci % 2
                wc_b = B(f"W1C{rot}")
                t_wa = P.dma("gp", W1C[:, :, 0, rot, 0:aw],
                             w1fd[l][:, 128 * gci:128 * gci + aw]
                             .rearrange("(kc p) n -> p kc n", p=128),
                             deps=wc_b.wdeps(), stream=f"w1c{rot}")
                t_wg = P.dma("gp", W1C[:, :, 1, rot, 0:aw],
                             w1fd[l][:, 2730 + 128 * gci:2730 + 128 * gci + aw]
                             .rearrange("(kc p) n -> p kc n", p=128),
                             deps=wc_b.wdeps(), stream=f"w1c{rot}")
                wc_b.wrote(t_wa, t_wg)
                for (pu, boff) in ((0, 128 * gci), (1, 2730 + 128 * gci)):
                    boffr = 256 * rot + 128 * pu
                    bias_b2 = B(f"fbias{rot}{pu}")
                    tb1 = P.dma("sp", ROWF[0:1, RF["bias"] + boffr:RF["bias"] + boffr + aw],
                                b1fd[l].rearrange("(one n) -> one n", one=1)
                                [0:1, boff:boff + aw],
                                deps=bias_b2.wdeps(), stream=f"bias{rot}{pu}")
                    bias_b2.wrote(tb1)
                return t_wa, t_wg

            w1dma = {0: issue_w1(0)}
            for ci in range(11):
                gci = 11 * half + ci
                aw = 128 if gci < 21 else 42
                rot = ci % 2
                wc_b = B(f"W1C{rot}")
                t_wa, t_wg = w1dma.pop(ci)
                if ci + 1 < 11:
                    w1dma[ci + 1] = issue_w1(ci + 1)
                bcol_toks = []
                for (pu, boff) in ((0, 128 * gci), (1, 2730 + 128 * gci)):
                    pb = psbank(4)
                    fns = []
                    for c in range(NC):
                        def mm(e, c=c, pu=pu, rot=rot, aw=aw):
                            return e.matmul(PS[0:1, 4, 0:aw], COLB[:, 8 + c:9 + c],
                                            W1C[:, c, pu, rot, 0:aw],
                                            start=(c == 0), stop=(c == NC - 1))
                        fns.append(mm)
                    tkc = P.ops("pe", fns, deps=[t_wa, t_wg] + lnbb_b.rdeps() + pb.wdeps())
                    pb.wrote(tkc)
                    wc_b.read(tkc)
                    lnbb_b.read(tkc)
                    boffr = 256 * rot + 128 * pu
                    bias_b2 = B(f"fbias{rot}{pu}")
                    tb1 = bias_b2.w[-1]

                    def addb(e, aw=aw, boffr=boffr):
                        return e.scalar_tensor_tensor(
                            ROWF[0:1, RF["bias"] + boffr:RF["bias"] + boffr + aw],
                            PS[0:1, 4, 0:aw], 1.0,
                            ROWF[0:1, RF["bias"] + boffr:RF["bias"] + boffr + aw],
                            op0=ALU.mult, op1=ALU.add)
                    ta = P.op("dve", addb, deps=[tkc, tb1])
                    pb.read(ta)
                    bias_b2.wrote(ta)
                    slot = 64 + 4 * rot + pu
                    pb2 = psbank(4)

                    def trp(e, aw=aw, boffr=boffr):
                        return e.transpose(PS[0:aw, 4, 4:5],
                                           ROWF[0:1, RF["bias"] + boffr:RF["bias"] + boffr + aw],
                                           idf[0:1, 0:1])
                    tt = P.op("pe", trp, deps=[ta, t_idf] + pb2.wdeps())
                    pb2.wrote(tt)
                    bias_b2.read(tt)
                    cbuf = B(f"bcol{pu}{rot}")

                    def cpc(e, aw=aw, slot=slot):
                        return e.tensor_copy(COLF[0:aw, slot:slot + 1], PS[0:aw, 4, 4:5])
                    tc2 = P.op("dve", cpc, deps=[tt] + cbuf.wdeps())
                    pb2.read(tc2)
                    cbuf.wrote(tc2)
                    bcol_toks.append(tc2)
                for g in range(TG):
                    w = gw(g)
                    ba = g % 2
                    bg = 2 + g % 2
                    pa = psbank(ba)
                    pg = psbank(bg)
                    fns_a = []
                    fns_g = []
                    for c in range(NC):
                        def mma(e, c=c, rot=rot, aw=aw, g=g, w=w, ba=ba):
                            return e.matmul(PS[0:aw, ba, 0:w], W1C[:, c, 0, rot, 0:aw],
                                            XNF[:, c, 512 * g:512 * g + w],
                                            start=(c == 0), stop=(c == NC - 1))
                        fns_a.append(mma)

                        def mmg(e, c=c, rot=rot, aw=aw, g=g, w=w, bg=bg):
                            return e.matmul(PS[0:aw, bg, 0:w], W1C[:, c, 1, rot, 0:aw],
                                            XNF[:, c, 512 * g:512 * g + w],
                                            start=(c == 0), stop=(c == NC - 1))
                        fns_g.append(mmg)
                    xnf_b = B("XNF")
                    tka = P.ops("pe", fns_a + fns_g, deps=[t_wa, t_wg] + xnf_b.rdeps()
                                + pa.wdeps() + pg.wdeps(), cost=3700)
                    tkg = tka
                    pa.wrote(tka)
                    pg.wrote(tka)
                    wc_b.read(tka)
                    xnf_b.read(tka)
                    gpar = g % 2
                    gel_b = B(f"GELS{gpar}")
                    sga = 64 + 4 * rot
                    bca = B(f"bcol0{rot}")
                    bcg = B(f"bcol1{rot}")

                    def fg(e, aw=aw, w=w, bg=bg, sga=sga, gpar=gpar):
                        return e.activation(GELS[0:aw, gpar, 0:w], PS[0:aw, bg, 0:w],
                                            AF.Gelu, bias=COLF[0:aw, sga + 1:sga + 2])
                    tge = P.op("act", fg, deps=[tkg] + bcg.rdeps() + gel_b.wdeps(),
                               cost=700)
                    pg.read(tge)
                    bcg.read(tge)
                    gel_b.wrote(tge)
                    hg_b = B(f"HG{ci}")

                    def fh(e, aw=aw, w=w, g=g, ci=ci, ba=ba, sga=sga, gpar=gpar):
                        return e.scalar_tensor_tensor(HG[0:aw, ci, 512 * g:512 * g + w],
                                                      PS[0:aw, ba, 0:w],
                                                      COLF[0:aw, sga:sga + 1],
                                                      GELS[0:aw, gpar, 0:w],
                                                      op0=ALU.add, op1=ALU.mult)
                    th = P.op("dve", fh, deps=[tka, tge] + bca.rdeps() + hg_b.wdeps(),
                              cost=700)
                    pa.read(th)
                    gel_b.read(th)
                    bca.read(th)
                    hg_b.wrote(th)
            w2r_pend = {}
            for g in range(TG):
                w = gw(g)
                gp = g % 2
                x4_b = B(f"X2G4{gp}")
                if g == 0:
                    w2r_pend[0] = P.dma("sp", X2G4[:, 0, :, 0:w],
                                        x2dram[:, :, 0:w],
                                        deps=x4_b.wdeps() + X2DB(0).rdeps(),
                                        stream="x2r2")
                    B("X2G40").wrote(w2r_pend[0])
                    X2DB(0).read(w2r_pend[0])
                tl = w2r_pend.pop(g)
                if g + 1 < TG:
                    gpn = (g + 1) % 2
                    x4n = B(f"X2G4{gpn}")
                    tln = P.dma("sp", X2G4[:, gpn, :, 0:gw(g + 1)],
                                x2dram[:, :, 512 * (g + 1):512 * (g + 1) + gw(g + 1)],
                                deps=x4n.wdeps() + X2DB(g + 1).rdeps(), stream="x2r2")
                    x4n.wrote(tln)
                    X2DB(g + 1).read(tln)
                    w2r_pend[g + 1] = tln
                up_toks = []
                for nci in range(NC):
                    bank = 4 + nci % 2
                    pb = psbank(bank)
                    fns = []
                    nk = 11 if half == 0 else 11
                    for ci in range(11):
                        gci = 11 * half + ci
                        aw = 128 if gci < 21 else 42

                        def mm(e, ci=ci, aw=aw, nci=nci, g=g, w=w, bank=bank, half=half):
                            return e.matmul(PS[:, bank, 0:w],
                                            W2H[0:aw, ci, 128 * nci:128 * nci + 128],
                                            HG[0:aw, ci, 512 * g:512 * g + w],
                                            start=(ci == 0), stop=(ci == 10 and half == 1))
                        fns.append(mm)
                    if half == 0:
                        def mmb(e, nci=nci, w=w, bank=bank, g=g):
                            return e.matmul(PS[:, bank, 0:w],
                                            b2frow[0:1, 128 * nci:128 * nci + 128],
                                            ones[0:1, 0:w], start=False, stop=True)
                        fns.append(mmb)
                    hdeps = [t for ci in range(11) for t in B(f"HG{ci}").rdeps()]
                    tk = P.ops("pe", fns, deps=tw2toks + hdeps + b2f_b.rdeps()
                               + [t_ones] + pb.wdeps())
                    pb.wrote(tk)
                    w2h_b.read(tk)
                    for ci in range(11):
                        B(f"HG{ci}").read(tk)
                    b2f_b.read(tk)

                    def fr(e, nci=nci, w=w, bank=bank, gp=gp):
                        return e.tensor_tensor(X2G4[:, gp, nci, 0:w], X2G4[:, gp, nci, 0:w],
                                               PS[:, bank, 0:w], op=ALU.add)
                    t2 = P.op("dve", fr, deps=[tk, tl])
                    pb.read(t2)
                    up_toks.append(t2)
                x4_b.wrote(*up_toks)
                tst = P.dma("sp", x2dram[:, :, 512 * g:512 * g + w],
                            X2G4[:, gp, :, 0:w],
                            deps=x4_b.rdeps() + X2DB(g).wdeps(), stream="x2w2")
                x4_b.read(tst)
                X2DB(g).wrote(tst)
        P.barrier()
        f_stk.close()
        return ()
    if upto in ("b1", "b2", "b3", "b4"):
        stks = stage_b_layer(0, upto_l=None if upto == "b4" else upto)
        if upto == "b4":
            tpf = tap_dram("post_ffn", x2dram)
            fin_dram_taps([tpf])
            return nc
        if upto == "b3":
            tpw = tap_dram("post_wo", x2dram)
            for st_ in stks:
                if st_ is not None:
                    P.barrier()
                    st_.close()
            fin_dram_taps([tpw])
            return nc
        P.barrier()
        P.finalize(tap_toks)
        P.materialize()
        for st_ in stks:
            if st_ is not None:
                st_.close()
        P.es.close()
        return nc

    for l in range(2):
        stage_b_layer(l)

    rb_reset()
    rx("rs30", 512)
    rx("nmurs30", 512)
    rx("rs31", 512)
    rx("nmurs31", 512)
    fin_st = ExitStack()
    FG = P.sb("FG", [128, 2, NC, 512], F32, stack=fin_st)
    XAG = P.sb("XAG", [128, 2, NC, 512], F32, stack=fin_st)
    SQN = P.sb("SQN", [128, NC, 512], BF16, stack=fin_st)
    TFN = P.sb("TFN", [128, 512], F32, stack=fin_st)
    OUTM = P.sb("OUTM", [128, 2, D], F32, stack=fin_st)
    out_toks = []
    fin_pend = {}

    def fin_load(g):
        gp = g % 2
        return (load_grp(FG[:, gp], B(f"FG{gp}"), x2dram, g, X2DB(g), "fl"),
                load_grp(XAG[:, gp], B(f"XAG{gp}"), xdram, g, XD(g), "xal"))

    fin_pend[0] = fin_load(0)
    for g in range(TG):
        P.marker(f"fin-g{g}")
        w = gw(g)
        gp = g % 2
        fg_b = B(f"FG{gp}")
        xag_b = B(f"XAG{gp}")
        tlf, tlx = fin_pend.pop(g)
        if g + 1 < TG:
            fin_pend[g + 1] = fin_load(g + 1)
        bk_r, bk_m = (6, 5) if gp == 0 else (4, 3)
        rs3, rs3_b, nmurs3, nmurs3_b, _, _ = ln_stats_bf16(
            lambda c: (FG[:, gp, c, 0:w], [tlf], fg_b), w,
            f"rs3{gp}", f"nmurs3{gp}", 0, SQN, "SQN", src_f32=True,
            make_bf=False, par=gp)
        tbr = bcast_row(rs3, rs3_b, 0, w, bk_r)
        tbm = bcast_row(nmurs3, nmurs3_b, 0, w, bk_m)
        up_toks = []
        for c in range(NC):
            tfb = B("TFN")

            def f1(e, c=c, w=w, gp=gp, bk_r=bk_r):
                return e.tensor_tensor(TFN[:, 0:w], FG[:, gp, c, 0:w],
                                       PS[:, bk_r, 0:w], op=ALU.mult)
            t1 = P.op("dve", f1, deps=[tbr, tlf] + tfb.wdeps())
            fg_b.read(t1)
            psbank(bk_r).read(t1)
            tfb.wrote(t1)

            def f2(e, c=c, w=w, gp=gp, bk_m=bk_m):
                return e.tensor_tensor(XAG[:, gp, c, 0:w], XAG[:, gp, c, 0:w],
                                       PS[:, bk_m, 0:w], op=ALU.add)
            t2 = P.op("dve", f2, deps=[tbm, tlx] + xag_b.wdeps())
            psbank(bk_m).read(t2)

            def f3(e, c=c, w=w, gp=gp):
                return e.tensor_tensor(XAG[:, gp, c, 0:w], XAG[:, gp, c, 0:w],
                                       TFN[:, 0:w], op=ALU.add)
            t3 = P.op("dve", f3, deps=[t1, t2])
            tfb.read(t3)
            up_toks.append(t3)
        xag_b.wrote(*up_toks)
        for j in range(w // 128):
            tau = 4 * g + j
            sl = tau % 2
            om_b = B(f"OUTM{sl}")
            cp_toks = []
            for c in range(NC):
                bank = c % 4
                pb = psbank(bank)

                def tr(e, c=c, j=j, bank=bank, gp=gp):
                    return e.transpose(PS[:, bank, 0:128],
                                       XAG[:, gp, c, 128 * j:128 * j + 128],
                                       idf[:, :])
                tt = P.op("pe", tr, deps=up_toks + [t_idf] + pb.wdeps())
                pb.wrote(tt)
                xag_b.read(tt)

                def cp(e, c=c, sl=sl, bank=bank):
                    return e.tensor_copy(OUTM[:, sl, 128 * c:128 * c + 128],
                                         PS[:, bank, 0:128])
                tc = P.op("dve", cp, deps=[tt] + om_b.wdeps())
                pb.read(tc)
                cp_toks.append(tc)
            om_b.wrote(*cp_toks)
            tso = P.dma("sp", out_d.rearrange("(a p) d -> p a d", p=128)[:, tau, :],
                        OUTM[:, sl, :], deps=cp_toks, stream=f"out{sl}")
            om_b.read(tso)
            out_toks.append(tso)

    P.finalize(out_toks + tap_toks)
    P.materialize()
    fin_st.close()
    P.es.close()
    return nc


def kernel(**inputs):
    T = 2048
    x = np.ascontiguousarray(np.asarray(inputs["L0_attn_x"], dtype=np.float32))
    consts = make_consts(T)
    nc = build_nc(T)
    common = {k: np.ascontiguousarray(np.asarray(v)) for k, v in inputs.items()
              if k != "L0_attn_x"}
    common.update(consts)
    in_maps = []
    for b in range(8):
        m = dict(common)
        m["x"] = np.ascontiguousarray(x[b])
        in_maps.append(m)
    res = run_bass_kernel_spmd(nc, in_maps, core_ids=list(range(8)))
    return np.stack([res.results[b]["out"] for b in range(8)], axis=0)
